# version 01
# baseline (speedup 1.0000x reference)
import math
import numpy as np
import concourse.bass as bass
import concourse.mybir as mybir
from concourse.bass_utils import run_bass_kernel_spmd

F32 = mybir.dt.float32
BF16 = mybir.dt.bfloat16
AF = mybir.ActivationFunctionType
OP = mybir.AluOpType
AX = mybir.AxisListType

D, H, HD, FF, NF = 1024, 16, 64, 4096, 16384
TSEQ, NCORES, BPC = 2048, 8, 2
MT = 256
NBLK = MT // 128
NMT = TSEQ // MT
NCH = D // 128
HMT = 3 + MT
EPS = 1e-5
R1024 = 1.0 / 1024.0


def _split8(v, n):
    out = []
    for _ in range(n):
        c = (np.array(v, np.float32).view(np.uint32) & np.uint32(0xFFFF0000)).view(np.float32)
        out.append(float(c))
        v = v - float(c)
    return out

CW2PI = _split8(2 * math.pi, 6)
_l10 = math.log(10.0)
L10H = float(np.float32(_l10)); L10L = float(np.float32(_l10 - L10H))
_l2 = math.log(2.0)
LN2H = float(np.float32(_l2)); LN2L = float(np.float32(_l2 - LN2H))
PIO2 = float(np.float32(math.pi / 2))


def host_constants():
    c = {}
    t = np.arange(TSEQ, dtype=np.float64)
    idx64 = np.arange(HD, dtype=np.float64) / 63.0
    freqs64 = (np.arange(H, dtype=np.float64)[:, None] * (10.0 ** idx64)[None, :]).reshape(-1)
    ang64 = t[:, None] * freqs64[None, :]
    c["ntab"] = np.asarray(np.rint(ang64 / (2 * np.pi)), dtype=np.float32)
    kd = np.rint(idx64 * np.log2(10.0))
    c["ktab"] = np.ascontiguousarray(np.broadcast_to(kd.astype(np.float32), (128, HD)))
    c["twok"] = np.ascontiguousarray(np.broadcast_to((2.0 ** kd).astype(np.float32), (128, HD)))
    c["tri"] = np.triu(np.ones((128, 128), dtype=np.float32))
    c["shift"] = np.eye(128, k=1, dtype=np.float32)
    c["eye"] = np.eye(128, dtype=np.float32)
    selc = np.zeros((128, 2), dtype=np.float32)
    selc[127, 0] = 1.0
    selc[:, 1] = 1.0
    c["selc"] = selc
    sl = np.zeros((128, 128), dtype=np.float32)
    sl[127, 0] = 1.0
    c["sel_last"] = sl
    return c


def _col8(v):
    return np.ascontiguousarray(np.asarray(v, np.float32).reshape(NCH, 128).T)


def host_prep(inp):
    g = lambda k: np.asarray(inp[k], dtype=np.float32)
    d = {}
    d["qk_wT"] = np.ascontiguousarray(g("L0_attn_qk_w").T)
    d["v_wT"] = np.ascontiguousarray(g("L0_attn_v_w").T)
    d["g_wT"] = np.ascontiguousarray(g("L0_attn_g_w").T)
    d["o_wT"] = np.ascontiguousarray(g("L0_attn_o_w").T)
    for nm, key in (("gate_t4", "L0_attn_gate_w"), ("up_t4", "L0_attn_up_w")):
        wT = g(key).T
        d[nm] = np.ascontiguousarray(wT.reshape(NCH, 128, 32, 128).transpose(2, 0, 1, 3))
    d["down_wT"] = np.ascontiguousarray(g("L0_attn_down_w").T)
    wcT = np.ascontiguousarray(g("L0_ffn_w_cproj").T)
    hh, cc, gg = np.meshgrid(np.arange(4), np.arange(8), np.arange(4), indexing="ij")
    base = ((hh * 4096)[..., None, None]
            + ((cc * 8)[..., None, None] + np.arange(8)[None, None, None, :, None]) * 64
            + (gg * 16)[..., None, None]
            + np.arange(16)[None, None, None, None, :])
    d["wc_Tp"] = np.ascontiguousarray(wcT[base.reshape(-1)])
    mk6 = g("L0_ffn_rail_maker").reshape(4, 8, 8, 16, 16)
    up6 = g("L0_ffn_rail_up_proj").reshape(4, 8, 8, 4, 16, 16)
    mbd = np.zeros((4, 8, 128, 128), dtype=np.float32)
    ubd = np.zeros((4, 8, 4, 128, 128), dtype=np.float32)
    for rho in range(8):
        s = slice(rho * 16, rho * 16 + 16)
        mbd[:, :, s, s] = mk6[:, :, rho].transpose(0, 1, 3, 2)
        ubd[:, :, :, s, s] = up6[:, :, rho].transpose(0, 1, 2, 4, 3)
    d["mk_bd"] = mbd
    d["up_bd"] = np.ascontiguousarray(ubd)
    d["convw_col"] = np.ascontiguousarray(
        g("L0_attn_conv_w")[:, 0, :].reshape(NCH, 128, 4).transpose(1, 0, 2).reshape(128, 32))
    for nm, key in (("ln1w", "L0_attn_ln1_w"), ("ln1b", "L0_attn_ln1_b"),
                    ("convb", "L0_attn_conv_b"), ("ln2w", "L0_attn_ln2_w"),
                    ("ln2b", "L0_attn_ln2_b")):
        d[nm + "_col"] = _col8(g(key))
    for nm, key in (("gb", "L0_attn_g_b"), ("onw", "L0_attn_onorm_w"),
                    ("onb", "L0_attn_onorm_b"), ("gam1", "L0_attn_gamma1"),
                    ("gam2", "L0_attn_gamma2")):
        d[nm + "_row"] = np.ascontiguousarray(g(key).reshape(1, D))
    d["gamma"] = np.ascontiguousarray(g("L0_attn_gamma_attn").reshape(H, 1))
    d.update(host_constants())
    return d



class Prog:
    ENG = ("sync", "gpsimd", "vector", "scalar", "tensor")

    def __init__(self):
        self.stream = {e: [] for e in self.ENG}
        self.count = {}
        self.hw = {e: {} for e in self.ENG}

    def emit(self, eng, fn, waits=(), inc=None, inc_n=1):
        ws = []
        for w in waits:
            if w is None:
                continue
            s, v = w
            if v > 0 and self.hw[eng].get(s, 0) < v:
                self.hw[eng][s] = v
                ws.append((s, v))
        if inc is not None:
            self.count[inc] = self.count.get(inc, 0) + inc_n
        self.stream[eng].append((fn, tuple(ws), inc, inc_n))
        return (inc, self.count[inc]) if inc is not None else None

    def tok(self, sem):
        return (sem, self.count.get(sem, 0))


def build_nc(nmt_per_seq=NMT, nseq=BPC):
    nc = bass.Bass("TRN2", target_bir_lowering=False)
    P = Prog()

    dr = {}
    def din(name, shape, dt=F32):
        dr[name] = nc.dram_tensor(name, list(shape), dt, kind="ExternalInput").ap()

    din("x", (BPC, TSEQ, D))
    for nm in ("qk_wT", "v_wT", "g_wT", "o_wT"):
        din(nm, (D, D))
    din("gate_t4", (32, NCH, 128, 128)); din("up_t4", (32, NCH, 128, 128))
    din("down_wT", (FF, D)); din("wc_Tp", (NF, D))
    din("mk_bd", (4, 8, 128, 128)); din("up_bd", (4, 8, 4, 128, 128))
    din("convw_col", (128, 32))
    for nm in ("ln1w_col", "ln1b_col", "convb_col", "ln2w_col", "ln2b_col"):
        din(nm, (128, NCH))
    for nm in ("gb_row", "onw_row", "onb_row", "gam1_row", "gam2_row"):
        din(nm, (1, D))
    din("gamma", (H, 1)); din("ntab", (TSEQ, D))
    din("ktab", (128, HD)); din("twok", (128, HD))
    din("tri", (128, 128)); din("shift", (128, 128)); din("eye", (128, 128))
    din("selc", (128, 2)); din("sel_last", (128, 128))
    out_d = nc.dram_tensor("out", [BPC, TSEQ, D], F32, kind="ExternalOutput").ap()

    sc = {
        "qk": nc.dram_tensor("s_qk", [NCH, 128, D], BF16).ap(),
        "v": nc.dram_tensor("s_v", [NCH, 128, D], BF16).ap(),
        "g": nc.dram_tensor("s_g", [NCH, 128, D], BF16).ap(),
        "o": nc.dram_tensor("s_o", [NCH, 128, D], BF16).ap(),
        "gate": nc.dram_tensor("s_gate", [32, 128, D], BF16).ap(),
        "up": nc.dram_tensor("s_up", [32, 128, D], BF16).ap(),
        "down": nc.dram_tensor("s_down", [32, 128, D], BF16).ap(),
        "wc": nc.dram_tensor("s_wc", [128, 128, D], BF16).ap(),
        "cos": nc.dram_tensor("s_cos", [TSEQ, D], F32).ap(),
    }

    Ct = nc.alloc_sbuf_tensor("Ct", [128, 128 * 128], BF16)
    dec = nc.alloc_sbuf_tensor("dec", [128, 16 * H], F32)
    rdec = nc.alloc_sbuf_tensor("rdec", [128, 16 * H], F32)
    sob = nc.alloc_sbuf_tensor("sob", [128, D], F32)
    bob = nc.alloc_sbuf_tensor("bob", [128, D], F32)
    g2b = nc.alloc_sbuf_tensor("g2b", [128, D], F32)
    cols = nc.alloc_sbuf_tensor("cols", [128, 48], F32)
    cwcol = nc.alloc_sbuf_tensor("cwcol", [128, 32], F32)
    bgcol = nc.alloc_sbuf_tensor("bgcol", [128, 32], F32)
    bucol = nc.alloc_sbuf_tensor("bucol", [128, 32], F32)
    mmrow = nc.alloc_sbuf_tensor("mmrow", [1, 2304], F32)
    gbhl = nc.alloc_sbuf_tensor("gbhl", [2, D], BF16)
    hl = nc.alloc_sbuf_tensor("hl", [2, D], BF16)
    selc_s = nc.alloc_sbuf_tensor("selc_s", [128, 2], F32)
    sel_last_s = nc.alloc_sbuf_tensor("sel_last_s", [128, 128], F32)
    tri_s = nc.alloc_sbuf_tensor("tri_s", [128, 128], F32)
    shf_s = nc.alloc_sbuf_tensor("shf_s", [128, 128], F32)
    eye_s = nc.alloc_sbuf_tensor("eye_s", [128, 128], F32)
    eyeb = nc.alloc_sbuf_tensor("eyeb", [128, 128], BF16)
    ones2b = nc.alloc_sbuf_tensor("ones2b", [2, 128], BF16)
    gamsb = nc.alloc_sbuf_tensor("gamsb", [H, 1], F32)
    pcol = nc.alloc_sbuf_tensor("pcol", [H, 1], F32)
    ccol = nc.alloc_sbuf_tensor("ccol", [128, 16], F32)
    stat = nc.alloc_sbuf_tensor("stat", [128, 16], F32)
    kn = nc.alloc_sbuf_tensor("kn", [128, 48], F32)
    Bx = nc.alloc_sbuf_tensor("Bx", [128, NBLK * D], F32)
    Bt1 = nc.alloc_sbuf_tensor("Bt1", [128, NBLK * D], F32)
    Bt2 = nc.alloc_sbuf_tensor("Bt2", [128, NBLK * D], F32)
    Bt3 = nc.alloc_sbuf_tensor("Bt3", [128, NBLK * D], F32)
    Bsq = nc.alloc_sbuf_tensor("Bsq", [128, D], F32)
    Bk = nc.alloc_sbuf_tensor("Bk", [128, NBLK * D], F32)
    Bv = nc.alloc_sbuf_tensor("Bv", [128, NBLK * D], F32)
    Bg = nc.alloc_sbuf_tensor("Bg", [128, NBLK * D], F32)
    Bcos = nc.alloc_sbuf_tensor("Bcos", [128, NBLK * D], F32)
    Bhfm = nc.alloc_sbuf_tensor("Bhfm", [128, NCH * HMT], F32)
    Bxc = nc.alloc_sbuf_tensor("Bxc", [128, NCH * MT], BF16)
    Bmb = nc.alloc_sbuf_tensor("Bmb", [128, NBLK * D], BF16)
    Bfm = nc.alloc_sbuf_tensor("Bfm", [128, NCH * MT], BF16)
    Bgu = nc.alloc_sbuf_tensor("Bgu", [128, 32 * MT], BF16)
    Bha = nc.alloc_sbuf_tensor("Bha", [128, 2 * MT], BF16)
    Bsl = nc.alloc_sbuf_tensor("Bsl", [128, 2 * MT], F32)
    Wq = nc.alloc_sbuf_tensor("Wq", [128, 2 * D], BF16)
    Wgu = nc.alloc_sbuf_tensor("Wgu", [128, 2 * 2048], BF16)
    Wdn = nc.alloc_sbuf_tensor("Wdn", [128, 2 * 1024], BF16)
    Wcp = nc.alloc_sbuf_tensor("Wcp", [128, 2 * 1024], BF16)
    ps1 = nc.alloc_psum_tensor("ps1", [128, 512], F32)
    ps2 = nc.alloc_psum_tensor("ps2", [128, 512], F32)
    ps3 = nc.alloc_psum_tensor("ps3", [128, 512], F32)
    ps4 = nc.alloc_psum_tensor("ps4", [128, 512], F32)
    ps5 = nc.alloc_psum_tensor("ps5", [128, 512], F32)
    ps6 = nc.alloc_psum_tensor("ps6", [128, 512], F32)
    psT = nc.alloc_psum_tensor("psT", [128, 2048], BF16)
    sLD = nc.alloc_semaphore("LD")
    sWT = nc.alloc_semaphore("WT")
    sTN = nc.alloc_semaphore("TN")
    sVC = nc.alloc_semaphore("VC")
    sSC = nc.alloc_semaphore("SC")
    sGP = nc.alloc_semaphore("GP")
    sST = nc.alloc_semaphore("ST")
    with nc.Block() as block:
        sems = {"LD": sLD, "WT": sWT, "TN": sTN, "VC": sVC, "SC": sSC,
                "GP": sGP, "ST": sST}
        psA = (ps1, ps2)
        psB = (ps3, ps4)
        psZ = (ps5, ps6)

        carry = mmrow[0:1, 0:1024]
        ktail = mmrow[0:1, 1024:2048]
        ones1 = mmrow[0:1, 2048:2176]
        e0r = mmrow[0:1, 2176:2304]
        ROW_GB = Bsq[0:1, 0:1024]
        ROW_ONW = Bcos[0:1, 0:1024]
        ROW_ONB = Bcos[0:1, 1024:2048]
        ROW_G2 = Bhfm[0:1, 0:1024]
        rowtmp = Bhfm[0:1, 1024:2048]
        HI32 = Bk[0:1, 0:1024]
        LO32 = Bk[0:1, 1024:2048]
        HL16 = Bxc[0:1, 0:2048]
        ones1c = selc_s[:, 1:2]

        def betaG_ap(mi, a, bnd):
            t_ = Bv if mi < 16 else Bg
            mo = mi % 16
            return t_[0:1, mo * 128 + a:mo * 128 + bnd]

        def betaU_ap(mi, a, bnd):
            t_ = Bk if mi < 16 else Bt3
            mo = mi % 16
            return t_[0:1, mo * 128 + a:mo * 128 + bnd]
        hl32 = Bt2
        dech = Bx
        Wc32 = Bt1
        Wc16 = Bmb
        Bnt = Bv
        Bfrq = Bg
        Bex = Bk[:, 0:HD]
        Bex2 = Bk[:, HD:2 * HD]
        Bex3 = Bk[:, 2 * HD:3 * HD]
        Bo = Bt3

        def E(eng, fn, waits=(), inc=None, inc_n=1):
            return P.emit(eng, fn, waits, inc, inc_n)

        def DMA(eng, dst, src, waits=(), sem="LD"):
            def fn(e, d_=dst, s_=src):
                return e.dma_start(out=d_, in_=s_)
            return E(eng, fn, waits, inc=sem, inc_n=16)

        def v_tt(out, a, b, op, waits=(), inc="VC"):
            return E("vector", lambda e, o=out, x=a, y=b, p=op:
                     e.tensor_tensor(out=o, in0=x, in1=y, op=p), waits, inc)

        def v_ts(out, a, s1, s2, op0, op1=None, waits=(), inc="VC"):
            if op1 is None:
                return E("vector", lambda e, o=out, x=a, u=s1, p=op0:
                         e.tensor_scalar(out=o, in0=x, scalar1=u, scalar2=None, op0=p),
                         waits, inc)
            return E("vector", lambda e, o=out, x=a, u=s1, w=s2, p=op0, q=op1:
                     e.tensor_scalar(out=o, in0=x, scalar1=u, scalar2=w, op0=p, op1=q),
                     waits, inc)

        def v_stt(out, a, s1, b, op0, op1, waits=(), inc="VC"):
            return E("vector", lambda e, o=out, x=a, u=s1, y=b, p=op0, q=op1:
                     e.scalar_tensor_tensor(out=o, in0=x, scalar=u, in1=y, op0=p, op1=q),
                     waits, inc)

        def v_cp(out, a, waits=(), inc="VC"):
            return E("vector", lambda e, o=out, x=a: e.tensor_copy(out=o, in_=x), waits, inc)

        def v_ms(out, c, waits=(), inc="VC"):
            return E("vector", lambda e, o=out, k=c: e.memset(o, k), waits, inc)

        def v_red(out, a, waits=(), inc="VC"):
            return E("vector", lambda e, o=out, x=a:
                     e.tensor_reduce(out=o, in_=x, axis=AX.X, op=OP.add), waits, inc)

        def s_act(out, a, func, bias=0.0, scale=1.0, waits=(), inc="SC"):
            return E("scalar", lambda e, o=out, x=a, f=func, b=bias, s=scale:
                     e.activation(out=o, in_=x, func=f, bias=b, scale=s), waits, inc)

        def s_cp(out, a, waits=(), inc="SC"):
            return E("scalar", lambda e, o=out, x=a: e.copy(out=o, in_=x), waits, inc)

        def t_mm(psum, lhsT, rhs, start, stop, waits=(), inc="TN"):
            return E("tensor", lambda e, o=psum, l=lhsT, r=rhs, a=start, b=stop:
                     e.matmul(o, l, r, start=a, stop=b), waits, inc)

        def t_tr(psum, src, ident, waits=(), inc="TN"):
            return E("tensor", lambda e, o=psum, x=src, i=ident:
                     e.transpose(o, x, i), waits, inc)

        def recip_pos(dst, src, tmp, waits):
            t1 = s_act(tmp, src, AF.Ln, waits=waits)
            t2 = s_act(dst, tmp, AF.Exp, scale=-1.0, waits=[t1])
            t3 = v_tt(tmp, src, dst, OP.mult, waits=[t2])
            t4 = v_ts(tmp, tmp, -1.0, 2.0, OP.mult, OP.add, waits=[t3])
            return v_tt(dst, dst, tmp, OP.mult, waits=[t4])

        def rsqrt_pos(dst, src, tmp, waits):
            t1 = s_act(tmp, src, AF.Ln, waits=waits)
            t2 = s_act(dst, tmp, AF.Exp, scale=-0.5, waits=[t1])
            t3 = v_tt(tmp, dst, dst, OP.mult, waits=[t2])
            t4 = v_tt(tmp, tmp, src, OP.mult, waits=[t3])
            t5 = v_ts(tmp, tmp, -0.5, 1.5, OP.mult, OP.add, waits=[t4])
            return v_tt(dst, dst, tmp, OP.mult, waits=[t5])

        for dst, src in (
            (tri_s[:, :], dr["tri"][:, :]), (shf_s[:, :], dr["shift"][:, :]),
            (eye_s[:, :], dr["eye"][:, :]),
            (cols[:, 0:8], dr["ln1w_col"][:, :]), (cols[:, 8:16], dr["ln1b_col"][:, :]),
            (cols[:, 16:24], dr["convb_col"][:, :]), (cols[:, 24:32], dr["ln2w_col"][:, :]),
            (cols[:, 32:40], dr["ln2b_col"][:, :]),
            (ROW_GB, dr["gb_row"][:, :]), (ROW_ONW, dr["onw_row"][:, :]),
            (ROW_ONB, dr["onb_row"][:, :]), (ROW_G2, dr["gam2_row"][:, :]),
            (rowtmp, dr["gam1_row"][:, :]),
            (selc_s[:, :], dr["selc"][:, :]), (sel_last_s[:, :], dr["sel_last"][:, :]),
            (gamsb[:, :], dr["gamma"][:, :]),
            (cwcol[:, :], dr["convw_col"][:, :]),
            (Bex2, dr["ktab"][:, :]), (Bex3, dr["twok"][:, :]),
        ):
            DMA("sync", dst, src)
        W0 = [P.tok("LD")]

        ch = v_ms(ones1, 1.0, W0)
        ch = v_ms(e0r, 0.0, [ch])
        ch = v_ms(mmrow[0:1, 2176:2177], 1.0, [ch])
        ch = v_ms(ones2b[:, :], 1.0, [ch])
        ch = v_ms(carry, 0.0, [ch])
        ch = v_ms(ktail, 0.0, [ch])
        ch = v_cp(eyeb[:, :], eye_s[:, :], [ch])

        ch = v_ms(dech[0:H, 0:1], 1.0, [ch])
        ch = v_cp(pcol[:, :], gamsb[:, :], [ch])
        n = 1
        while n < TSEQ:
            nn = min(n, TSEQ - n)
            ch = v_ts(dech[0:H, n:n + nn], dech[0:H, 0:nn], pcol[0:H, 0:1], None,
                      OP.mult, waits=[ch])
            n += nn
            if n < TSEQ:
                ch = v_tt(pcol[:, :], pcol[:, :], pcol[:, :], OP.mult, waits=[ch])
        for tb in range(16):
            tt = t_tr(ps1[:, 0:H], dech[0:H, tb * 128:(tb + 1) * 128],
                      eye_s[0:H, 0:H], [ch])
            ch = s_cp(dec[:, tb * H:(tb + 1) * H], ps1[:, 0:H], [tt])
        ch = v_ts(rdec[:, :], dec[:, :], 1e-6, None, OP.add, waits=[ch])
        ch = recip_pos(Bt1[:, 0:16 * H], rdec[:, :], Bt2[:, 0:16 * H], [ch])
        ch = v_cp(rdec[:, :], Bt1[:, 0:16 * H], [ch])

        ch = E("gpsimd", lambda e: e.iota(Bex, [[1, HD]], channel_multiplier=0,
               allow_small_or_imprecise_dtypes=True), [ch], inc="GP")
        ch = v_ms(stat[:, 0:1], 63.0, [ch])
        ch = recip_pos(stat[:, 1:2], stat[:, 0:1], stat[:, 2:3], [ch])
        ch = v_ts(Bex, Bex, stat[:, 1:2], None, OP.mult, waits=[ch])
        ch = v_ts(Bfrq[:, 0:HD], Bex, L10H, None, OP.mult, waits=[ch])
        ch = v_stt(Bfrq[:, 0:HD], Bex2, -LN2H, Bfrq[:, 0:HD], OP.mult, OP.add, [ch])
        ch = v_ts(Bfrq[:, HD:2 * HD], Bex, L10L, None, OP.mult, waits=[ch])
        ch = v_tt(Bfrq[:, 0:HD], Bfrq[:, 0:HD], Bfrq[:, HD:2 * HD], OP.add, waits=[ch])
        ch = v_stt(Bfrq[:, 0:HD], Bex2, -LN2L, Bfrq[:, 0:HD], OP.mult, OP.add, [ch])
        ch = v_ms(ccol[:, 0:2], 1.0, [ch])
        for j in range(2, 11):
            ch = v_ms(stat[:, 3:4], float(j), [ch])
            ch = recip_pos(stat[:, 4:5], stat[:, 3:4], stat[:, 5:6], [ch])
            ch = v_ts(ccol[:, j:j + 1], ccol[:, j - 1:j], stat[:, 4:5], None, OP.mult, waits=[ch])
        ch = v_ts(Bex2, Bfrq[:, 0:HD], 0.0, ccol[:, 10:11], OP.mult, OP.add, [ch])
        for j in range(9, -1, -1):
            ch = v_tt(Bex2, Bex2, Bfrq[:, 0:HD], OP.mult, waits=[ch])
            ch = v_ts(Bex2, Bex2, ccol[:, j:j + 1], None, OP.add, waits=[ch])
        ch = v_tt(Bex2, Bex2, Bex3, OP.mult, waits=[ch])
        for h in range(H - 1, -1, -1):
            ch = v_ts(Bfrq[:, h * HD:(h + 1) * HD], Bex2, float(h), None, OP.mult, waits=[ch])

        for tb in range(16):
            l1 = DMA("sync", Bnt[:, 0:D], dr["ntab"][tb * 128:(tb + 1) * 128, :], [ch])
            ch = E("gpsimd", lambda e, b=tb: e.iota(stat[:, 6:7], [[1, 1]], base=b * 128,
                   channel_multiplier=1, allow_small_or_imprecise_dtypes=True),
                   [ch], inc="GP")
            ch = v_ts(Bt1[:, 0:D], Bfrq[:, 0:D], stat[:, 6:7], None, OP.mult, waits=[ch])
            for ci in CW2PI:
                ch = v_stt(Bt1[:, 0:D], Bnt[:, 0:D], -ci, Bt1[:, 0:D], OP.mult, OP.add,
                           [ch, l1])
            ch = s_act(Bt1[:, 0:D], Bt1[:, 0:D], AF.Abs, waits=[ch])
            ch = v_ts(Bt2[:, 0:D], Bt1[:, 0:D], -1.0, PIO2, OP.mult, OP.add, waits=[ch])
            ch = s_act(Bt2[:, 0:D], Bt2[:, 0:D], AF.Sin, waits=[ch])
            ch = DMA("sync", sc["cos"][tb * 128:(tb + 1) * 128, :], Bt2[:, 0:D], [ch],
                     sem="ST")

        def bcast_row(dst128, src_row, w):
            w = v_cp(Bxc[0:1, 0:1024], src_row, [w])
            w = v_cp(HI32, Bxc[0:1, 0:1024], [w])
            w = v_tt(LO32, src_row, HI32, OP.subtract, waits=[w])
            w = v_cp(Bxc[0:1, 1024:2048], LO32, [w])
            w = DMA("gpsimd", hl[0:2, :], HL16, [w], sem="WT")
            for sl in range(2):
                pp = (ps1, ps2)[sl]
                m1 = t_mm(pp[:, 0:512], ones2b[:, :], hl[:, sl * 512:(sl + 1) * 512],
                          True, True, waits=[w])
                w = s_cp(dst128[:, sl * 512:(sl + 1) * 512], pp[:, 0:512], [m1])
            return w

        ch = v_tt(rowtmp, rowtmp, ROW_ONW, OP.mult, waits=[ch])
        ch = bcast_row(sob, rowtmp, ch)
        ch = DMA("sync", rowtmp, dr["gam1_row"][:, :], [ch])
        ch = v_tt(rowtmp, rowtmp, ROW_ONB, OP.mult, waits=[ch])
        ch = bcast_row(bob, rowtmp, ch)
        ch = bcast_row(g2b, ROW_G2, ch)
        ch = v_cp(Bxc[0:1, 0:1024], ROW_GB, [ch])
        ch = v_cp(HI32, Bxc[0:1, 0:1024], [ch])
        ch = v_tt(LO32, ROW_GB, HI32, OP.subtract, waits=[ch])
        ch = v_cp(Bxc[0:1, 1024:2048], LO32, [ch])
        ch = DMA("gpsimd", gbhl[0:2, :], HL16, [ch], sem="WT")

        conv_specs = []
        for nm in ("qk", "v", "g", "o"):
            for c in range(NCH):
                conv_specs.append(("plain", dr[nm + "_wT"][c * 128:(c + 1) * 128, :],
                                   sc[nm][c, :, :], None))
        for f in range(32):
            conv_specs.append(("g2", dr["down_wT"][f * 128:(f + 1) * 128, :],
                               sc["down"][f, :, :], None))
        for f in range(128):
            conv_specs.append(("plain", dr["wc_Tp"][f * 128:(f + 1) * 128, :],
                               sc["wc"][f, :, :], None))
        for wi, nm in (("gate_t4", "gate"), ("up_t4", "up")):
            for m in range(32):
                conv_specs.append(("gu", dr[wi][m, :, :, :].rearrange("c p w -> p c w"),
                                   sc[nm][m, :, :], (nm, m)))
        slot_tok = {0: ch, 1: ch}
        prev_v = ch
        prev_beta = ch
        beta_last = {"gate": None, "up": None}
        for i, (mode, src, dst, extra) in enumerate(conv_specs):
            par = i % 2
            wdst = Wc32[:, par * D:(par + 1) * D]
            if mode == "gu":
                wdst = wdst.rearrange("p (c w) -> p c w", w=128)
            lw = DMA("gpsimd", wdst, src, [slot_tok[par]], sem="WT")
            if mode == "g2":
                tv = v_tt(Wc16[:, par * D:(par + 1) * D], Wc32[:, par * D:(par + 1) * D],
                          g2b[:, :], OP.mult, waits=[lw, prev_v])
            elif mode == "gu":
                tv = prev_v
                for c in range(NCH):
                    tv = v_ts(Wc16[:, par * D + c * 128:par * D + (c + 1) * 128],
                              Wc32[:, par * D + c * 128:par * D + (c + 1) * 128],
                              cols[:, 24 + c:25 + c], None, OP.mult, waits=[lw, tv])
            else:
                tv = v_cp(Wc16[:, par * D:(par + 1) * D], Wc32[:, par * D:(par + 1) * D],
                          [lw, prev_v])
            st = DMA("gpsimd", dst, Wc16[:, par * D:(par + 1) * D], [tv], sem="WT")
            new_tok = st
            if mode == "gu":
                nm_, m_ = extra
                pm = None
                for c in range(NCH):
                    pm = t_mm(ps3[0:1, 0:128], cols[:, 32 + c:33 + c],
                              Wc32[:, par * D + c * 128:par * D + (c + 1) * 128],
                              c == 0, c == 7,
                              waits=[lw, prev_beta] if c == 0 else [])
                bfn = betaG_ap if nm_ == "gate" else betaU_ap
                be = s_cp(bfn(m_, 0, 128), ps3[0:1, 0:128], [pm])
                prev_beta = be
                beta_last[nm_] = be
                new_tok = be
            slot_tok[par] = new_tok
            prev_v = tv
        CONV_END = [slot_tok[0], slot_tok[1], prev_beta]

        ch2 = prev_beta
        for m in range(32):
            tt = t_tr(ps4[:, 0:1], betaG_ap(m, 0, 128), eye_s[0:1, 0:1], [ch2])
            ch2 = s_cp(bgcol[:, m:m + 1], ps4[:, 0:1], [tt])
        for m in range(32):
            tt = t_tr(ps4[:, 0:1], betaU_ap(m, 0, 128), eye_s[0:1, 0:1], [ch2])
            ch2 = s_cp(bucol[:, m:m + 1], ps4[:, 0:1], [tt])

        ctok = [ch2] + CONV_END
        last_c = ch2
        for hq in range(4):
            for cq in range(8):
                l1 = DMA("gpsimd", Wc32[:, 0:128], dr["mk_bd"][hq, cq, :, :],
                         [last_c] + ctok, sem="WT")
                t1 = v_cp(Wc16[:, 0:128], Wc32[:, 0:128], [l1, last_c])
                for gq in range(4):
                    l2 = DMA("gpsimd", Wc32[:, 128:256], dr["up_bd"][hq, cq, gq, :, :],
                             [last_c], sem="WT")
                    t2 = v_cp(Wc16[:, 128:256], Wc32[:, 128:256], [l2, t1])
                    mm = t_mm(ps3[:, 0:128], Wc16[:, 0:128], Wc16[:, 128:256],
                              True, True, waits=[t2, last_c])
                    ti = ((hq * 8) + cq) * 4 + gq
                    last_c = s_cp(Ct[:, ti * 128:(ti + 1) * 128], ps3[:, 0:128], [mm])
        PRO_DONE = [last_c, P.tok("WT"), P.tok("ST"), P.tok("VC"), P.tok("SC"),
                    P.tok("TN"), P.tok("GP")]

        def layer_norm(src, dst, w):
            w1 = v_tt(Bsq[:, 0:D], src, src, OP.mult, waits=w)
            w2 = v_red(stat[:, 0:1], src, waits=w)
            w3 = v_red(stat[:, 1:2], Bsq[:, 0:D], waits=[w1])
            w4 = v_ts(stat[:, 2:3], stat[:, 0:1], R1024, None, OP.mult, waits=[w2])
            w5 = v_ts(stat[:, 3:4], stat[:, 1:2], R1024, None, OP.mult, waits=[w3])
            w6 = v_tt(stat[:, 4:5], stat[:, 2:3], stat[:, 2:3], OP.mult, waits=[w4])
            w7 = v_tt(stat[:, 5:6], stat[:, 3:4], stat[:, 4:5], OP.subtract,
                      waits=[w5, w6])
            w8 = v_ts(stat[:, 5:6], stat[:, 5:6], EPS, None, OP.add, waits=[w7])
            w9 = rsqrt_pos(stat[:, 6:7], stat[:, 5:6], stat[:, 7:8], [w8])
            return v_ts(dst, src, stat[:, 2:3], stat[:, 6:7], OP.subtract, OP.mult,
                        waits=[w9])

        CH = PRO_DONE
        hv = Bhfm[:, :].rearrange("p (c w) -> p c w", w=HMT)
        t2v = Bt2[:, :].rearrange("p (c w) -> p c w", w=D)
        t2c = Bt2[:, 0:NCH * MT].rearrange("p (c w) -> p c w", w=MT)
        xcv = Bxc[:, :].rearrange("p (c w) -> p c w", w=MT)

        for b in range(nseq):
            z = v_ms(carry, 0.0, CH)
            z = v_ms(ktail, 0.0, [z])
            z = v_ms(hv[:, :, 0:3], 0.0, [z])
            CH = [z]
            for m in range(nmt_per_seq):
                t0 = m * MT
                gb0 = t0 // 128

                lx = []
                for blk in range(NBLK):
                    lx.append(DMA("sync", Bx[:, blk * D:(blk + 1) * D],
                                  dr["x"][b, t0 + blk * 128:t0 + (blk + 1) * 128, :], CH))
                    lx.append(DMA("sync", Bcos[:, blk * D:(blk + 1) * D],
                                  sc["cos"][t0 + blk * 128:t0 + (blk + 1) * 128, :], CH))

                ln1t = CH
                ln1 = []
                for blk in range(NBLK):
                    tln = layer_norm(Bx[:, blk * D:(blk + 1) * D],
                                     Bt1[:, blk * D:(blk + 1) * D],
                                     [lx[2 * blk]] + ln1t)
                    ln1.append(tln)
                    ln1t = [tln]

                if m > 0:
                    hcp = v_cp(hv[:, :, 0:3], hv[:, :, MT:MT + 3], CH)
                else:
                    hcp = CH[0]
                evs = hcp
                par = 0
                for blk in range(NBLK):
                    for c in range(NCH):
                        pt = psZ[par]
                        tt = t_tr(pt[:, 0:128],
                                  Bt1[:, blk * D + c * 128:blk * D + (c + 1) * 128],
                                  eye_s[:, :], waits=[ln1[blk], evs])
                        evs = s_cp(
                            Bhfm[:, c * HMT + 3 + blk * 128:c * HMT + 3 + (blk + 1) * 128],
                            pt[:, 0:128], [tt])
                        par ^= 1
                TRX = evs

                cv = TRX
                for c in range(NCH):
                    cv = v_ts(hv[:, c:c + 1, 3:3 + MT], hv[:, c:c + 1, 3:3 + MT],
                              cols[:, c:c + 1], cols[:, 8 + c:9 + c], OP.mult, OP.add,
                              waits=[cv])
                for c in range(NCH):
                    cv = v_ts(t2c[:, c:c + 1, :], hv[:, c:c + 1, 0:MT],
                              cwcol[:, c * 4:c * 4 + 1], cols[:, 16 + c:17 + c],
                              OP.mult, OP.add, waits=[cv])
                for k in range(1, 4):
                    for c in range(NCH):
                        dst = xcv[:, c:c + 1, :] if k == 3 else t2c[:, c:c + 1, :]
                        cv = v_stt(dst, hv[:, c:c + 1, k:k + MT],
                                   cwcol[:, c * 4 + k:c * 4 + k + 1],
                                   t2c[:, c:c + 1, :], OP.mult, OP.add, waits=[cv])
                XC = cv

                prev_ev = XC
                wslot = {0: XC, 1: XC}
                for nm, dstb in (("qk", Bk), ("v", Bv), ("g", Bg)):
                    mend = None
                    for c in range(NCH):
                        par = c % 2
                        lw = DMA("gpsimd", Wq[:, par * D:(par + 1) * D], sc[nm][c, :, :],
                                 [wslot[par]], sem="WT")
                        for blk in range(NBLK):
                            pa, pb = (psA, psB)[blk]
                            w_ = [lw, XC, prev_ev] if c == 0 else [lw]
                            last = (c == 7) and nm != "g"
                            t_mm(pa[:, 0:512],
                                 Bxc[:, c * MT + blk * 128:c * MT + (blk + 1) * 128],
                                 Wq[:, par * D:par * D + 512], c == 0, last, waits=w_)
                            mend = t_mm(pb[:, 0:512],
                                        Bxc[:, c * MT + blk * 128:c * MT + (blk + 1) * 128],
                                        Wq[:, par * D + 512:(par + 1) * D], c == 0, last)
                        wslot[par] = mend
                    if nm == "g":
                        for blk in range(NBLK):
                            pa, pb = (psA, psB)[blk]
                            t_mm(pa[:, 0:512], ones2b[:, :], gbhl[:, 0:512], False, True)
                            mend = t_mm(pb[:, 0:512], ones2b[:, :], gbhl[:, 512:1024],
                                        False, True)
                    ev = mend
                    for blk in range(NBLK):
                        pa, pb = (psA, psB)[blk]
                        if nm == "g":
                            ev = s_act(dstb[:, blk * D:blk * D + 512], pa[:, 0:512],
                                       AF.Sigmoid, waits=[ev])
                            ev = s_act(dstb[:, blk * D + 512:(blk + 1) * D], pb[:, 0:512],
                                       AF.Sigmoid, waits=[ev])
                        else:
                            ev = s_cp(dstb[:, blk * D:blk * D + 512], pa[:, 0:512], [ev])
                            ev = s_cp(dstb[:, blk * D + 512:(blk + 1) * D], pb[:, 0:512],
                                      [ev])
                    prev_ev = ev
                QKVG = prev_ev

                kk = QKVG
                for blk in range(NBLK):
                    kk = v_tt(Bsq[:, 0:D], Bk[:, blk * D:(blk + 1) * D],
                              Bk[:, blk * D:(blk + 1) * D], OP.mult, waits=[kk])
                    kk = v_red(kn[:, 32:48],
                               Bsq[:, 0:D].rearrange("p (h w) -> p h w", w=HD),
                               waits=[kk])
                    kk = v_ts(kn[:, 32:48], kn[:, 32:48], -1e-24, None, OP.add, waits=[kk])
                    kk = s_act(kn[:, 32:48], kn[:, 32:48], AF.Relu, waits=[kk])
                    kk = v_ts(kn[:, 32:48], kn[:, 32:48], 1e-24, None, OP.add, waits=[kk])
                    kk = rsqrt_pos(kn[:, 0:16], kn[:, 32:48], kn[:, 16:32], [kk])
                    for h in range(H):
                        kk = v_ts(Bk[:, blk * D + h * HD:blk * D + (h + 1) * HD],
                                  Bk[:, blk * D + h * HD:blk * D + (h + 1) * HD],
                                  kn[:, h:h + 1], None, OP.mult, waits=[kk])
                mix = kk
                cu_prev = kk

                for blk in range(NBLK):
                    gbi = gb0 + blk
                    m1 = t_mm(ps5[:, 0:512], shf_s[:, :], Bk[:, blk * D:blk * D + 512],
                              True, False, waits=[mix])
                    m2 = t_mm(ps6[:, 0:512], shf_s[:, :],
                              Bk[:, blk * D + 512:(blk + 1) * D], True, False)
                    if blk == 0:
                        m1 = t_mm(ps5[:, 0:512], e0r, ktail[0:1, 0:512], False, True)
                        m2 = t_mm(ps6[:, 0:512], e0r, ktail[0:1, 512:1024], False, True)
                    else:
                        m1 = t_mm(ps5[:, 0:512], sel_last_s[:, :],
                                  Bk[:, 0:512], False, True)
                        m2 = t_mm(ps6[:, 0:512], sel_last_s[:, :],
                                  Bk[:, 512:1024], False, True)
                    s1 = v_tt(Bt1[:, blk * D:blk * D + 512],
                              Bcos[:, blk * D:blk * D + 512], ps5[:, 0:512], OP.add,
                              waits=[m1, lx[2 * blk + 1]])
                    s1 = v_tt(Bt1[:, blk * D + 512:(blk + 1) * D],
                              Bcos[:, blk * D + 512:(blk + 1) * D], ps6[:, 0:512],
                              OP.add, waits=[m2, s1])
                    s2 = v_tt(Bg[:, blk * D:(blk + 1) * D], Bv[:, blk * D:(blk + 1) * D],
                              Bg[:, blk * D:(blk + 1) * D], OP.mult, waits=[s1])
                    s3 = v_tt(Bt1[:, blk * D:(blk + 1) * D], Bt1[:, blk * D:(blk + 1) * D],
                              Bg[:, blk * D:(blk + 1) * D], OP.mult, waits=[s2])
                    for h in range(H):
                        s3 = v_ts(Bt1[:, blk * D + h * HD:blk * D + (h + 1) * HD],
                                  Bt1[:, blk * D + h * HD:blk * D + (h + 1) * HD],
                                  dec[:, gbi * H + h:gbi * H + h + 1], None, OP.mult,
                                  waits=[s3])
                    c1 = t_mm(ps1[:, 0:512], tri_s[:, :], Bt1[:, blk * D:blk * D + 512],
                              True, False, waits=[s3])
                    c1 = t_mm(ps1[:, 0:512], ones1, carry[0:1, 0:512], False, True)
                    c2 = t_mm(ps2[:, 0:512], tri_s[:, :],
                              Bt1[:, blk * D + 512:(blk + 1) * D], True, False)
                    c2 = t_mm(ps2[:, 0:512], ones1, carry[0:1, 512:1024], False, True)
                    b1 = t_mm(ps3[0:1, 0:512], ones1c, Bt1[:, blk * D:blk * D + 512],
                              True, True, waits=[s3, cu_prev])
                    b2 = t_mm(ps4[0:1, 0:512], ones1c,
                              Bt1[:, blk * D + 512:(blk + 1) * D], True, True)
                    cu = v_tt(carry[0:1, 0:512], carry[0:1, 0:512], ps3[0:1, 0:512],
                              OP.add, waits=[c1, b1])
                    cu = v_tt(carry[0:1, 512:1024], carry[0:1, 512:1024],
                              ps4[0:1, 0:512], OP.add, waits=[c2, b2, cu])
                    cu_prev = cu
                    mv = cu
                    for h in range(H):
                        pp = ps1 if h < 8 else ps2
                        off = (h % 8) * HD
                        mv = v_ts(Bv[:, blk * D + h * HD:blk * D + (h + 1) * HD],
                                  pp[:, off:off + HD],
                                  rdec[:, gbi * H + h:gbi * H + h + 1], None, OP.mult,
                                  waits=[c1, c2, mv])
                    mix = mv
                kt1 = t_mm(ps3[0:1, 0:512], selc_s[:, 0:1], Bk[:, D:D + 512],
                           True, True, waits=[mix])
                kt2 = t_mm(ps4[0:1, 0:512], selc_s[:, 0:1], Bk[:, D + 512:2 * D],
                           True, True)
                ktv = s_cp(ktail[0:1, 0:512], ps3[0:1, 0:512], [kt1])
                ktv = s_cp(ktail[0:1, 512:1024], ps4[0:1, 0:512], [ktv, kt2])
                mix = v_tt(Bv[:, :], Bv[:, :], Bk[:, :], OP.mult, waits=[mix, ktv])
                mix = v_cp(Bmb[:, :], Bv[:, :], [mix])

                def fm_transpose(w0):
                    par_ = 0
                    ev_ = w0
                    for blk in range(NBLK):
                        for c in range(NCH):
                            po = par_ * 1024
                            tt = t_tr(psT[:, po:po + 128],
                                      Bmb[:, blk * D + c * 128:blk * D + (c + 1) * 128],
                                      eyeb[:, :], waits=[w0, ev_])
                            ev_ = s_cp(
                                Bfm[:, c * MT + blk * 128:c * MT + (blk + 1) * 128],
                                psT[:, po:po + 128], [tt])
                            par_ ^= 1
                    return ev_
                MIXFM = fm_transpose(mix)

                wslot = {0: MIXFM, 1: MIXFM}
                mend = None
                for c in range(NCH):
                    par = c % 2
                    lw = DMA("gpsimd", Wq[:, par * D:(par + 1) * D], sc["o"][c, :, :],
                             [wslot[par]], sem="WT")
                    for blk in range(NBLK):
                        pa, pb = (psA, psB)[blk]
                        w_ = [lw, MIXFM] if c == 0 else [lw]
                        t_mm(pa[:, 0:512],
                             Bfm[:, c * MT + blk * 128:c * MT + (blk + 1) * 128],
                             Wq[:, par * D:par * D + 512], c == 0, c == 7, waits=w_)
                        mend = t_mm(pb[:, 0:512],
                                    Bfm[:, c * MT + blk * 128:c * MT + (blk + 1) * 128],
                                    Wq[:, par * D + 512:(par + 1) * D], c == 0, c == 7)
                    wslot[par] = mend
                ev = mend
                for blk in range(NBLK):
                    pa, pb = (psA, psB)[blk]
                    ev = s_cp(Bt2[:, blk * D:blk * D + 512], pa[:, 0:512], [ev])
                    ev = s_cp(Bt2[:, blk * D + 512:(blk + 1) * D], pb[:, 0:512], [ev])
                on = [ev]
                for blk in range(NBLK):
                    on = [layer_norm(Bt2[:, blk * D:(blk + 1) * D],
                                     Bt2[:, blk * D:(blk + 1) * D], on)]
                x2 = on[0]
                for blk in range(NBLK):
                    x2 = v_tt(Bt3[:, blk * D:(blk + 1) * D], Bt2[:, blk * D:(blk + 1) * D],
                              sob[:, :], OP.mult, waits=[x2])
                    x2 = v_tt(Bt3[:, blk * D:(blk + 1) * D], Bt3[:, blk * D:(blk + 1) * D],
                              bob[:, :], OP.add, waits=[x2])
                    x2 = v_tt(Bk[:, blk * D:(blk + 1) * D], Bt3[:, blk * D:(blk + 1) * D],
                              Bx[:, blk * D:(blk + 1) * D], OP.add, waits=[x2])

                l2t = [x2]
                for blk in range(NBLK):
                    l2t = [layer_norm(Bk[:, blk * D:(blk + 1) * D],
                                      Bt2[:, blk * D:(blk + 1) * D], l2t)]
                l2c = v_cp(Bmb[:, :], Bt2[:, :], l2t)
                H2FM = fm_transpose(l2c)

                gevs = []
                gut = H2FM
                for mi in range(32):
                    par = mi % 2
                    lg = DMA("gpsimd", Wgu[:, par * 2048:par * 2048 + 1024],
                             sc["gate"][mi, :, :], [gut], sem="WT")
                    lu = DMA("gpsimd", Wgu[:, par * 2048 + 1024:(par + 1) * 2048],
                             sc["up"][mi, :, :], [gut], sem="WT")
                    pg = (ps5, ps3)[par]
                    pu = (ps6, ps4)[par]
                    mg = mu = None
                    for c in range(NCH):
                        w_ = [lg, lu, H2FM]
                        if c == 0 and mi >= 2:
                            w_.append(gevs[mi - 2])
                        mg = t_mm(pg[:, 0:MT],
                                  Wgu[:, par * 2048 + c * 128:par * 2048 + (c + 1) * 128],
                                  Bfm[:, c * MT:(c + 1) * MT], c == 0, c == 7,
                                  waits=w_ if c == 0 else [lg])
                        mu = t_mm(pu[:, 0:MT],
                                  Wgu[:, par * 2048 + 1024 + c * 128:par * 2048 + 1024 + (c + 1) * 128],
                                  Bfm[:, c * MT:(c + 1) * MT], c == 0, c == 7)
                    sl = s_act(Bsl[:, par * MT:(par + 1) * MT], pg[:, 0:MT], AF.Silu,
                               bias=bgcol[:, mi:mi + 1], waits=[mu])
                    gv = v_stt(Bgu[:, mi * MT:(mi + 1) * MT], pu[:, 0:MT],
                               bucol[:, mi:mi + 1], Bsl[:, par * MT:(par + 1) * MT],
                               OP.add, OP.mult, waits=[sl])
                    gevs.append(gv)
                    gut = gv
                GU = gevs[-1]

                wslot = {0: GU, 1: GU}
                dm = None
                for f in range(32):
                    par = f % 2
                    lw = DMA("gpsimd", Wdn[:, par * 1024:(par + 1) * 1024],
                             sc["down"][f, :, :], [wslot[par]], sem="WT")
                    for blk in range(NBLK):
                        pa, pb = (psA, psB)[blk]
                        w_ = [lw, GU] if f == 0 else [lw]
                        t_mm(pa[:, 0:512],
                             Bgu[:, f * MT + blk * 128:f * MT + (blk + 1) * 128],
                             Wdn[:, par * 1024:par * 1024 + 512], f == 0, f == 31,
                             waits=w_)
                        dm = t_mm(pb[:, 0:512],
                                  Bgu[:, f * MT + blk * 128:f * MT + (blk + 1) * 128],
                                  Wdn[:, par * 1024 + 512:(par + 1) * 1024],
                                  f == 0, f == 31)
                    wslot[par] = dm
                yv = dm
                for blk in range(NBLK):
                    pa, pb = (psA, psB)[blk]
                    yv = v_tt(Bt3[:, blk * D:blk * D + 512], pa[:, 0:512],
                              Bk[:, blk * D:blk * D + 512], OP.add, waits=[yv])
                    yv = v_tt(Bt3[:, blk * D + 512:(blk + 1) * D], pb[:, 0:512],
                              Bk[:, blk * D + 512:(blk + 1) * D], OP.add, waits=[yv])
                    yv = v_tt(Bt3[:, blk * D:(blk + 1) * D], Bt3[:, blk * D:(blk + 1) * D],
                              Bx[:, blk * D:(blk + 1) * D], OP.add, waits=[yv])

                ol = [yv]
                for blk in range(NBLK):
                    ol = [layer_norm(Bt3[:, blk * D:(blk + 1) * D],
                                     Bt1[:, blk * D:(blk + 1) * D], ol)]
                ol2 = v_cp(Bmb[:, :], Bt1[:, :], ol)
                XFFM = fm_transpose(ol2)

                wslot = {0: XFFM, 1: XFFM}
                cp_last = XFFM
                ge = XFFM
                for f in range(128):
                    par = f % 2
                    lw = DMA("gpsimd", Wcp[:, par * 1024:(par + 1) * 1024],
                             sc["wc"][f, :, :], [wslot[par]], sem="WT")
                    pz = psZ[par]
                    cq = (f // 4) % 8
                    zm = t_mm(pz[:, 0:MT], Ct[:, f * 128:(f + 1) * 128],
                              Bfm[:, cq * MT:(cq + 1) * MT], True, True,
                              waits=[XFFM, cp_last])
                    ge = s_act(Bha[:, par * MT:(par + 1) * MT], pz[:, 0:MT], AF.Gelu,
                               waits=[zm])
                    for blk in range(NBLK):
                        pa, pb = (psA, psB)[blk]
                        w_ = [ge, lw]
                        t_mm(pa[:, 0:512],
                             Bha[:, par * MT + blk * 128:par * MT + (blk + 1) * 128],
                             Wcp[:, par * 1024:par * 1024 + 512], f == 0, f == 127,
                             waits=w_)
                        cp_last = t_mm(pb[:, 0:512],
                                       Bha[:, par * MT + blk * 128:par * MT + (blk + 1) * 128],
                                       Wcp[:, par * 1024 + 512:(par + 1) * 1024],
                                       f == 0, f == 127)
                    wslot[par] = cp_last
                fin = cp_last
                for blk in range(NBLK):
                    pa, pb = (psA, psB)[blk]
                    fin = v_tt(Bt2[:, blk * D:blk * D + 512], pa[:, 0:512],
                               Bt1[:, blk * D:blk * D + 512], OP.add, waits=[fin])
                    fin = v_tt(Bt2[:, blk * D + 512:(blk + 1) * D], pb[:, 0:512],
                               Bt1[:, blk * D + 512:(blk + 1) * D], OP.add, waits=[fin])
                fl = [fin]
                for blk in range(NBLK):
                    fl = [layer_norm(Bt2[:, blk * D:(blk + 1) * D],
                                     Bo[:, blk * D:(blk + 1) * D], fl)]
                st = None
                for blk in range(NBLK):
                    st = DMA("sync", out_d[b, t0 + blk * 128:t0 + (blk + 1) * 128, :],
                             Bo[:, blk * D:(blk + 1) * D], fl, sem="ST")
                CH = [fl[0], st]

        E("sync", lambda e: e.nop(), [P.tok("ST")])

        @block.sync
        def _(e):
            P.replay("sync", e, sems)

        @block.gpsimd
        def _(e):
            P.replay("gpsimd", e, sems)

        @block.vector
        def _(e):
            P.replay("vector", e, sems)

        @block.scalar
        def _(e):
            P.replay("scalar", e, sems)

        @block.tensor
        def _(e):
            P.replay("tensor", e, sems)

    return nc


def _replay(self, eng, e, sems):
    for fn, ws, inc, inc_n in self.stream[eng]:
        for (s, v) in ws:
            e.wait_ge(sems[s], v)
        ins = fn(e)
        if inc is not None:
            if ins is not None:
                ins.then_inc(sems[inc], inc_n)
            else:
                e.sem_inc(sems[inc], inc_n)

Prog.replay = _replay


def kernel(**inputs):
    x = np.asarray(inputs["L0_attn_x"], dtype=np.float32)
    prep = host_prep(inputs)
    nc = build_nc()
    in_maps = []
    for c in range(NCORES):
        mm = dict(prep)
        mm["x"] = np.ascontiguousarray(x[c * BPC:(c + 1) * BPC])
        in_maps.append(mm)
    res = run_bass_kernel_spmd(nc, in_maps, core_ids=list(range(NCORES)))
    out = np.concatenate([np.asarray(r["out"]) for r in res.results], axis=0)
    return out.astype(np.float32)


# version 02
# speedup vs baseline: 1.2686x; 1.2686x over previous
import math
import numpy as np
import concourse.bass as bass
import concourse.mybir as mybir
from concourse.bass_utils import run_bass_kernel_spmd

F32 = mybir.dt.float32
BF16 = mybir.dt.bfloat16
AF = mybir.ActivationFunctionType
OP = mybir.AluOpType
AX = mybir.AxisListType

D, H, HD, FF, NF = 1024, 16, 64, 4096, 16384
TSEQ, NCORES, BPC = 2048, 8, 2
MT = 256
NBLK = MT // 128
NMT = TSEQ // MT
NCH = D // 128
HMT = 3 + MT
EPS = 1e-5
R1024 = 1.0 / 1024.0


def _split8(v, n):
    out = []
    for _ in range(n):
        c = (np.array(v, np.float32).view(np.uint32) & np.uint32(0xFFFF0000)).view(np.float32)
        out.append(float(c))
        v = v - float(c)
    return out

CW2PI = _split8(2 * math.pi, 6)
_l10 = math.log(10.0)
L10H = float(np.float32(_l10)); L10L = float(np.float32(_l10 - L10H))
_l2 = math.log(2.0)
LN2H = float(np.float32(_l2)); LN2L = float(np.float32(_l2 - LN2H))
PIO2 = float(np.float32(math.pi / 2))


def host_constants():
    c = {}
    t = np.arange(TSEQ, dtype=np.float64)
    idx64 = np.arange(HD, dtype=np.float64) / 63.0
    freqs64 = (np.arange(H, dtype=np.float64)[:, None] * (10.0 ** idx64)[None, :]).reshape(-1)
    ang64 = t[:, None] * freqs64[None, :]
    c["ntab"] = np.asarray(np.rint(ang64 / (2 * np.pi)), dtype=np.float32)
    kd = np.rint(idx64 * np.log2(10.0))
    c["ktab"] = np.ascontiguousarray(np.broadcast_to(kd.astype(np.float32), (128, HD)))
    c["twok"] = np.ascontiguousarray(np.broadcast_to((2.0 ** kd).astype(np.float32), (128, HD)))
    c["tri"] = np.triu(np.ones((128, 128), dtype=np.float32))
    c["shift"] = np.eye(128, k=1, dtype=np.float32)
    c["eye"] = np.eye(128, dtype=np.float32)
    selc = np.zeros((128, 2), dtype=np.float32)
    selc[127, 0] = 1.0
    selc[:, 1] = 1.0
    c["selc"] = selc
    sl = np.zeros((128, 128), dtype=np.float32)
    sl[127, 0] = 1.0
    c["sel_last"] = sl
    return c


def _col8(v):
    return np.ascontiguousarray(np.asarray(v, np.float32).reshape(NCH, 128).T)


def host_prep(inp):
    g = lambda k: np.asarray(inp[k], dtype=np.float32)
    d = {}
    d["qk_wT"] = np.ascontiguousarray(g("L0_attn_qk_w").T)
    d["v_wT"] = np.ascontiguousarray(g("L0_attn_v_w").T)
    d["g_wT"] = np.ascontiguousarray(g("L0_attn_g_w").T)
    d["o_wT"] = np.ascontiguousarray(g("L0_attn_o_w").T)
    for nm, key in (("gate_t4", "L0_attn_gate_w"), ("up_t4", "L0_attn_up_w")):
        wT = g(key).T
        d[nm] = np.ascontiguousarray(wT.reshape(NCH, 128, 32, 128).transpose(2, 0, 1, 3))
    d["down_wT"] = np.ascontiguousarray(g("L0_attn_down_w").T)
    wcT = np.ascontiguousarray(g("L0_ffn_w_cproj").T)
    hh, cc, gg = np.meshgrid(np.arange(4), np.arange(8), np.arange(4), indexing="ij")
    base = ((hh * 4096)[..., None, None]
            + ((cc * 8)[..., None, None] + np.arange(8)[None, None, None, :, None]) * 64
            + (gg * 16)[..., None, None]
            + np.arange(16)[None, None, None, None, :])
    d["wc_Tp"] = np.ascontiguousarray(wcT[base.reshape(-1)])
    mk6 = g("L0_ffn_rail_maker").reshape(4, 8, 8, 16, 16)
    up6 = g("L0_ffn_rail_up_proj").reshape(4, 8, 8, 4, 16, 16)
    mbd = np.zeros((4, 8, 128, 128), dtype=np.float32)
    ubd = np.zeros((4, 8, 4, 128, 128), dtype=np.float32)
    for rho in range(8):
        s = slice(rho * 16, rho * 16 + 16)
        mbd[:, :, s, s] = mk6[:, :, rho].transpose(0, 1, 3, 2)
        ubd[:, :, :, s, s] = up6[:, :, rho].transpose(0, 1, 2, 4, 3)
    d["mk_bd"] = mbd
    d["up_bd"] = np.ascontiguousarray(ubd)
    d["convw_col"] = np.ascontiguousarray(
        g("L0_attn_conv_w")[:, 0, :].reshape(NCH, 128, 4).transpose(1, 0, 2).reshape(128, 32))
    for nm, key in (("ln1w", "L0_attn_ln1_w"), ("ln1b", "L0_attn_ln1_b"),
                    ("convb", "L0_attn_conv_b"), ("ln2w", "L0_attn_ln2_w"),
                    ("ln2b", "L0_attn_ln2_b")):
        d[nm + "_col"] = _col8(g(key))
    for nm, key in (("gb", "L0_attn_g_b"), ("onw", "L0_attn_onorm_w"),
                    ("onb", "L0_attn_onorm_b"), ("gam1", "L0_attn_gamma1"),
                    ("gam2", "L0_attn_gamma2")):
        d[nm + "_row"] = np.ascontiguousarray(g(key).reshape(1, D))
    d["gamma"] = np.ascontiguousarray(g("L0_attn_gamma_attn").reshape(H, 1))
    d.update(host_constants())
    return d



class Prog:
    ENG = ("sync", "gpsimd", "vector", "scalar", "tensor")

    def __init__(self):
        self.stream = {e: [] for e in self.ENG}
        self.count = {}
        self.hw = {e: {} for e in self.ENG}

    def emit(self, eng, fn, waits=(), inc=None, inc_n=1):
        ws = []
        for w in waits:
            if w is None:
                continue
            s, v = w
            if v > 0 and self.hw[eng].get(s, 0) < v:
                self.hw[eng][s] = v
                ws.append((s, v))
        if inc is not None:
            self.count[inc] = self.count.get(inc, 0) + inc_n
        self.stream[eng].append((fn, tuple(ws), inc, inc_n))
        return (inc, self.count[inc]) if inc is not None else None

    def tok(self, sem):
        return (sem, self.count.get(sem, 0))


def build_nc(nmt_per_seq=NMT, nseq=BPC):
    nc = bass.Bass("TRN2", target_bir_lowering=False)
    P = Prog()

    dr = {}
    def din(name, shape, dt=F32):
        dr[name] = nc.dram_tensor(name, list(shape), dt, kind="ExternalInput").ap()

    din("x", (BPC, TSEQ, D))
    for nm in ("qk_wT", "v_wT", "g_wT", "o_wT"):
        din(nm, (D, D))
    din("gate_t4", (32, NCH, 128, 128)); din("up_t4", (32, NCH, 128, 128))
    din("down_wT", (FF, D)); din("wc_Tp", (NF, D))
    din("mk_bd", (4, 8, 128, 128)); din("up_bd", (4, 8, 4, 128, 128))
    din("convw_col", (128, 32))
    for nm in ("ln1w_col", "ln1b_col", "convb_col", "ln2w_col", "ln2b_col"):
        din(nm, (128, NCH))
    for nm in ("gb_row", "onw_row", "onb_row", "gam1_row", "gam2_row"):
        din(nm, (1, D))
    din("gamma", (H, 1)); din("ntab", (TSEQ, D))
    din("ktab", (128, HD)); din("twok", (128, HD))
    din("tri", (128, 128)); din("shift", (128, 128)); din("eye", (128, 128))
    din("selc", (128, 2)); din("sel_last", (128, 128))
    out_d = nc.dram_tensor("out", [BPC, TSEQ, D], F32, kind="ExternalOutput").ap()

    sc = {
        "qk": nc.dram_tensor("s_qk", [NCH, 128, D], BF16).ap(),
        "v": nc.dram_tensor("s_v", [NCH, 128, D], BF16).ap(),
        "g": nc.dram_tensor("s_g", [NCH, 128, D], BF16).ap(),
        "o": nc.dram_tensor("s_o", [NCH, 128, D], BF16).ap(),
        "gate": nc.dram_tensor("s_gate", [32, 128, D], BF16).ap(),
        "up": nc.dram_tensor("s_up", [32, 128, D], BF16).ap(),
        "down": nc.dram_tensor("s_down", [32, 128, D], BF16).ap(),
        "wc": nc.dram_tensor("s_wc", [128, 128, D], BF16).ap(),
        "cos": nc.dram_tensor("s_cos", [TSEQ, D], F32).ap(),
    }

    Ct = nc.alloc_sbuf_tensor("Ct", [128, 128 * 128], BF16)
    dec = nc.alloc_sbuf_tensor("dec", [128, 16 * H], F32)
    rdec = nc.alloc_sbuf_tensor("rdec", [128, 16 * H], F32)
    sob = nc.alloc_sbuf_tensor("sob", [128, D], F32)
    bob = nc.alloc_sbuf_tensor("bob", [128, D], F32)
    g2b = nc.alloc_sbuf_tensor("g2b", [128, D], F32)
    cols = nc.alloc_sbuf_tensor("cols", [128, 48], F32)
    cwcol = nc.alloc_sbuf_tensor("cwcol", [128, 32], F32)
    bgcol = nc.alloc_sbuf_tensor("bgcol", [128, 32], F32)
    bucol = nc.alloc_sbuf_tensor("bucol", [128, 32], F32)
    mmrow = nc.alloc_sbuf_tensor("mmrow", [1, 2304], F32)
    gbhl = nc.alloc_sbuf_tensor("gbhl", [2, D], BF16)
    hl = nc.alloc_sbuf_tensor("hl", [2, D], BF16)
    selc_s = nc.alloc_sbuf_tensor("selc_s", [128, 2], F32)
    sel_last_s = nc.alloc_sbuf_tensor("sel_last_s", [128, 128], F32)
    tri_s = nc.alloc_sbuf_tensor("tri_s", [128, 128], F32)
    shf_s = nc.alloc_sbuf_tensor("shf_s", [128, 128], F32)
    eye_s = nc.alloc_sbuf_tensor("eye_s", [128, 128], F32)
    eyeb = nc.alloc_sbuf_tensor("eyeb", [128, 128], BF16)
    ones2b = nc.alloc_sbuf_tensor("ones2b", [2, 128], BF16)
    gamsb = nc.alloc_sbuf_tensor("gamsb", [H, 1], F32)
    pcol = nc.alloc_sbuf_tensor("pcol", [H, 1], F32)
    ccol = nc.alloc_sbuf_tensor("ccol", [128, 16], F32)
    stat = nc.alloc_sbuf_tensor("stat", [128, 16], F32)
    kn = nc.alloc_sbuf_tensor("kn", [128, 48], F32)
    Bx = nc.alloc_sbuf_tensor("Bx", [128, NBLK * D], F32)
    Bt1 = nc.alloc_sbuf_tensor("Bt1", [128, NBLK * D], F32)
    Bt2 = nc.alloc_sbuf_tensor("Bt2", [128, NBLK * D], F32)
    Bt3 = nc.alloc_sbuf_tensor("Bt3", [128, NBLK * D], F32)
    Bsq = nc.alloc_sbuf_tensor("Bsq", [128, D], F32)
    Bk = nc.alloc_sbuf_tensor("Bk", [128, NBLK * D], F32)
    Bv = nc.alloc_sbuf_tensor("Bv", [128, NBLK * D], F32)
    Bg = nc.alloc_sbuf_tensor("Bg", [128, NBLK * D], F32)
    Bcos = nc.alloc_sbuf_tensor("Bcos", [128, NBLK * D], F32)
    Bhfm = nc.alloc_sbuf_tensor("Bhfm", [128, NCH * HMT], F32)
    Bxc = nc.alloc_sbuf_tensor("Bxc", [128, NCH * MT], BF16)
    Bmb = nc.alloc_sbuf_tensor("Bmb", [128, NBLK * D], BF16)
    Bfm = nc.alloc_sbuf_tensor("Bfm", [128, NCH * MT], BF16)
    Bgu = nc.alloc_sbuf_tensor("Bgu", [128, 32 * MT], BF16)
    Bha = nc.alloc_sbuf_tensor("Bha", [128, 2 * MT], BF16)
    Bsl = nc.alloc_sbuf_tensor("Bsl", [128, 2 * MT], F32)
    Wq = nc.alloc_sbuf_tensor("Wq", [128, 2 * D], BF16)
    Wgu = nc.alloc_sbuf_tensor("Wgu", [128, 4 * 2048], BF16)
    Wdn = nc.alloc_sbuf_tensor("Wdn", [128, 4 * 1024], BF16)
    Wcp = nc.alloc_sbuf_tensor("Wcp", [128, 4 * 1024], BF16)
    ps1 = nc.alloc_psum_tensor("ps1", [128, 512], F32)
    ps2 = nc.alloc_psum_tensor("ps2", [128, 512], F32)
    ps3 = nc.alloc_psum_tensor("ps3", [128, 512], F32)
    ps4 = nc.alloc_psum_tensor("ps4", [128, 512], F32)
    ps5 = nc.alloc_psum_tensor("ps5", [128, 512], F32)
    ps6 = nc.alloc_psum_tensor("ps6", [128, 512], F32)
    psT = nc.alloc_psum_tensor("psT", [128, 2048], BF16)
    SEM_NAMES = (["LD", "WT", "TN", "VC", "SC", "GP", "ST"]
                 + ["Q%d" % i for i in range(2)]
                 + ["G%d" % i for i in range(4)]
                 + ["D%d" % i for i in range(4)]
                 + ["C%d" % i for i in range(4)]
                 + ["X%d" % i for i in range(2)]
                 + ["O%d" % i for i in range(2)]
                 + ["V%d" % i for i in range(2)]
                 + ["U%d" % i for i in range(2)])
    sem_h = {n: nc.alloc_semaphore(n) for n in SEM_NAMES}
    with nc.Block() as block:
        sems = sem_h
        psA = (ps1, ps2)
        psB = (ps3, ps4)
        psZ = (ps5, ps6)

        carry = mmrow[0:1, 0:1024]
        ktail = mmrow[0:1, 1024:2048]
        ones1 = mmrow[0:1, 2048:2176]
        e0r = mmrow[0:1, 2176:2304]
        ROW_GB = Bsq[0:1, 0:1024]
        ROW_ONW = Bcos[0:1, 0:1024]
        ROW_ONB = Bcos[0:1, 1024:2048]
        ROW_G2 = Bhfm[0:1, 0:1024]
        rowtmp = Bhfm[0:1, 1024:2048]
        HI32 = Bk[0:1, 0:1024]
        LO32 = Bk[0:1, 1024:2048]
        HL16 = Bxc[0:1, 0:2048]
        ones1c = selc_s[:, 1:2]

        def betaG_ap(mi, a, bnd):
            t_ = Bv if mi < 16 else Bg
            mo = mi % 16
            return t_[0:1, mo * 128 + a:mo * 128 + bnd]

        def betaU_ap(mi, a, bnd):
            t_ = Bk if mi < 16 else Bt3
            mo = mi % 16
            return t_[0:1, mo * 128 + a:mo * 128 + bnd]
        hl32 = Bt2
        dech = Bx
        Wc32 = Bt1
        Wc16 = Bmb
        Bnt = Bv
        Bfrq = Bg
        Bex = Bk[:, 0:HD]
        Bex2 = Bk[:, HD:2 * HD]
        Bex3 = Bk[:, 2 * HD:3 * HD]
        Bo = Bt3

        def E(eng, fn, waits=(), inc=None, inc_n=1):
            return P.emit(eng, fn, waits, inc, inc_n)

        def DMA(eng, dst, src, waits=(), sem="LD"):
            def fn(e, d_=dst, s_=src):
                return e.dma_start(out=d_, in_=s_)
            return E(eng, fn, waits, inc=sem, inc_n=16)

        def v_tt(out, a, b, op, waits=(), inc="VC"):
            return E("vector", lambda e, o=out, x=a, y=b, p=op:
                     e.tensor_tensor(out=o, in0=x, in1=y, op=p), waits, inc)

        def v_ts(out, a, s1, s2, op0, op1=None, waits=(), inc="VC"):
            if op1 is None:
                return E("vector", lambda e, o=out, x=a, u=s1, p=op0:
                         e.tensor_scalar(out=o, in0=x, scalar1=u, scalar2=None, op0=p),
                         waits, inc)
            return E("vector", lambda e, o=out, x=a, u=s1, w=s2, p=op0, q=op1:
                     e.tensor_scalar(out=o, in0=x, scalar1=u, scalar2=w, op0=p, op1=q),
                     waits, inc)

        def v_stt(out, a, s1, b, op0, op1, waits=(), inc="VC"):
            return E("vector", lambda e, o=out, x=a, u=s1, y=b, p=op0, q=op1:
                     e.scalar_tensor_tensor(out=o, in0=x, scalar=u, in1=y, op0=p, op1=q),
                     waits, inc)

        def v_cp(out, a, waits=(), inc="VC"):
            return E("vector", lambda e, o=out, x=a: e.tensor_copy(out=o, in_=x), waits, inc)

        def v_ms(out, c, waits=(), inc="VC"):
            return E("vector", lambda e, o=out, k=c: e.memset(o, k), waits, inc)

        def v_red(out, a, waits=(), inc="VC"):
            return E("vector", lambda e, o=out, x=a:
                     e.tensor_reduce(out=o, in_=x, axis=AX.X, op=OP.add), waits, inc)

        def s_act(out, a, func, bias=0.0, scale=1.0, waits=(), inc="SC"):
            return E("scalar", lambda e, o=out, x=a, f=func, b=bias, s=scale:
                     e.activation(out=o, in_=x, func=f, bias=b, scale=s), waits, inc)

        def s_cp(out, a, waits=(), inc="SC"):
            return E("scalar", lambda e, o=out, x=a: e.copy(out=o, in_=x), waits, inc)

        def t_mm(psum, lhsT, rhs, start, stop, waits=(), inc="TN"):
            return E("tensor", lambda e, o=psum, l=lhsT, r=rhs, a=start, b=stop:
                     e.matmul(o, l, r, start=a, stop=b), waits, inc)

        def t_tr(psum, src, ident, waits=(), inc="TN"):
            return E("tensor", lambda e, o=psum, x=src, i=ident:
                     e.transpose(o, x, i), waits, inc)

        def recip_pos(dst, src, tmp, waits):
            t1 = s_act(tmp, src, AF.Ln, waits=waits)
            t2 = s_act(dst, tmp, AF.Exp, scale=-1.0, waits=[t1])
            t3 = v_tt(tmp, src, dst, OP.mult, waits=[t2])
            t4 = v_ts(tmp, tmp, -1.0, 2.0, OP.mult, OP.add, waits=[t3])
            return v_tt(dst, dst, tmp, OP.mult, waits=[t4])

        def rsqrt_pos(dst, src, tmp, waits):
            t1 = s_act(tmp, src, AF.Ln, waits=waits)
            t2 = s_act(dst, tmp, AF.Exp, scale=-0.5, waits=[t1])
            t3 = v_tt(tmp, dst, dst, OP.mult, waits=[t2])
            t4 = v_tt(tmp, tmp, src, OP.mult, waits=[t3])
            t5 = v_ts(tmp, tmp, -0.5, 1.5, OP.mult, OP.add, waits=[t4])
            return v_tt(dst, dst, tmp, OP.mult, waits=[t5])

        for dst, src in (
            (tri_s[:, :], dr["tri"][:, :]), (shf_s[:, :], dr["shift"][:, :]),
            (eye_s[:, :], dr["eye"][:, :]),
            (cols[:, 0:8], dr["ln1w_col"][:, :]), (cols[:, 8:16], dr["ln1b_col"][:, :]),
            (cols[:, 16:24], dr["convb_col"][:, :]), (cols[:, 24:32], dr["ln2w_col"][:, :]),
            (cols[:, 32:40], dr["ln2b_col"][:, :]),
            (ROW_GB, dr["gb_row"][:, :]), (ROW_ONW, dr["onw_row"][:, :]),
            (ROW_ONB, dr["onb_row"][:, :]), (ROW_G2, dr["gam2_row"][:, :]),
            (rowtmp, dr["gam1_row"][:, :]),
            (selc_s[:, :], dr["selc"][:, :]), (sel_last_s[:, :], dr["sel_last"][:, :]),
            (gamsb[:, :], dr["gamma"][:, :]),
            (cwcol[:, :], dr["convw_col"][:, :]),
            (Bex2, dr["ktab"][:, :]), (Bex3, dr["twok"][:, :]),
        ):
            DMA("sync", dst, src)
        W0 = [P.tok("LD")]

        ch = v_ms(ones1, 1.0, W0)
        ch = v_ms(e0r, 0.0, [ch])
        ch = v_ms(mmrow[0:1, 2176:2177], 1.0, [ch])
        ch = v_ms(ones2b[:, :], 1.0, [ch])
        ch = v_ms(carry, 0.0, [ch])
        ch = v_ms(ktail, 0.0, [ch])
        ch = v_cp(eyeb[:, :], eye_s[:, :], [ch])

        ch = v_ms(dech[0:H, 0:1], 1.0, [ch])
        ch = v_cp(pcol[:, :], gamsb[:, :], [ch])
        n = 1
        while n < TSEQ:
            nn = min(n, TSEQ - n)
            ch = v_ts(dech[0:H, n:n + nn], dech[0:H, 0:nn], pcol[0:H, 0:1], None,
                      OP.mult, waits=[ch])
            n += nn
            if n < TSEQ:
                ch = v_tt(pcol[:, :], pcol[:, :], pcol[:, :], OP.mult, waits=[ch])
        for tb in range(16):
            tt = t_tr(ps1[:, 0:H], dech[0:H, tb * 128:(tb + 1) * 128],
                      eye_s[0:H, 0:H], [ch])
            ch = s_cp(dec[:, tb * H:(tb + 1) * H], ps1[:, 0:H], [tt])
        ch = v_ts(rdec[:, :], dec[:, :], 1e-6, None, OP.add, waits=[ch])
        ch = recip_pos(Bt1[:, 0:16 * H], rdec[:, :], Bt2[:, 0:16 * H], [ch])
        ch = v_cp(rdec[:, :], Bt1[:, 0:16 * H], [ch])

        ch = E("gpsimd", lambda e: e.iota(Bex, [[1, HD]], channel_multiplier=0,
               allow_small_or_imprecise_dtypes=True), [ch], inc="GP")
        ch = v_ms(stat[:, 0:1], 63.0, [ch])
        ch = recip_pos(stat[:, 1:2], stat[:, 0:1], stat[:, 2:3], [ch])
        ch = v_ts(Bex, Bex, stat[:, 1:2], None, OP.mult, waits=[ch])
        ch = v_ts(Bfrq[:, 0:HD], Bex, L10H, None, OP.mult, waits=[ch])
        ch = v_stt(Bfrq[:, 0:HD], Bex2, -LN2H, Bfrq[:, 0:HD], OP.mult, OP.add, [ch])
        ch = v_ts(Bfrq[:, HD:2 * HD], Bex, L10L, None, OP.mult, waits=[ch])
        ch = v_tt(Bfrq[:, 0:HD], Bfrq[:, 0:HD], Bfrq[:, HD:2 * HD], OP.add, waits=[ch])
        ch = v_stt(Bfrq[:, 0:HD], Bex2, -LN2L, Bfrq[:, 0:HD], OP.mult, OP.add, [ch])
        ch = v_ms(ccol[:, 0:2], 1.0, [ch])
        for j in range(2, 11):
            ch = v_ms(stat[:, 3:4], float(j), [ch])
            ch = recip_pos(stat[:, 4:5], stat[:, 3:4], stat[:, 5:6], [ch])
            ch = v_ts(ccol[:, j:j + 1], ccol[:, j - 1:j], stat[:, 4:5], None, OP.mult, waits=[ch])
        ch = v_ts(Bex2, Bfrq[:, 0:HD], 0.0, ccol[:, 10:11], OP.mult, OP.add, [ch])
        for j in range(9, -1, -1):
            ch = v_tt(Bex2, Bex2, Bfrq[:, 0:HD], OP.mult, waits=[ch])
            ch = v_ts(Bex2, Bex2, ccol[:, j:j + 1], None, OP.add, waits=[ch])
        ch = v_tt(Bex2, Bex2, Bex3, OP.mult, waits=[ch])
        for h in range(H - 1, -1, -1):
            ch = v_ts(Bfrq[:, h * HD:(h + 1) * HD], Bex2, float(h), None, OP.mult, waits=[ch])

        for tb in range(16):
            l1 = DMA("sync", Bnt[:, 0:D], dr["ntab"][tb * 128:(tb + 1) * 128, :], [ch])
            ch = E("gpsimd", lambda e, b=tb: e.iota(stat[:, 6:7], [[1, 1]], base=b * 128,
                   channel_multiplier=1, allow_small_or_imprecise_dtypes=True),
                   [ch], inc="GP")
            ch = v_ts(Bt1[:, 0:D], Bfrq[:, 0:D], stat[:, 6:7], None, OP.mult, waits=[ch])
            for ci in CW2PI:
                ch = v_stt(Bt1[:, 0:D], Bnt[:, 0:D], -ci, Bt1[:, 0:D], OP.mult, OP.add,
                           [ch, l1])
            ch = s_act(Bt1[:, 0:D], Bt1[:, 0:D], AF.Abs, waits=[ch])
            ch = v_ts(Bt2[:, 0:D], Bt1[:, 0:D], -1.0, PIO2, OP.mult, OP.add, waits=[ch])
            ch = s_act(Bt2[:, 0:D], Bt2[:, 0:D], AF.Sin, waits=[ch])
            ch = DMA("sync", sc["cos"][tb * 128:(tb + 1) * 128, :], Bt2[:, 0:D], [ch],
                     sem="ST")

        def bcast_row(dst128, src_row, w):
            w = v_cp(Bxc[0:1, 0:1024], src_row, [w])
            w = v_cp(HI32, Bxc[0:1, 0:1024], [w])
            w = v_tt(LO32, src_row, HI32, OP.subtract, waits=[w])
            w = v_cp(Bxc[0:1, 1024:2048], LO32, [w])
            w = DMA("gpsimd", hl[0:2, :], HL16, [w], sem="WT")
            for sl in range(2):
                pp = (ps1, ps2)[sl]
                m1 = t_mm(pp[:, 0:512], ones2b[:, :], hl[:, sl * 512:(sl + 1) * 512],
                          True, True, waits=[w])
                w = s_cp(dst128[:, sl * 512:(sl + 1) * 512], pp[:, 0:512], [m1])
            return w

        ch = v_tt(rowtmp, rowtmp, ROW_ONW, OP.mult, waits=[ch])
        ch = bcast_row(sob, rowtmp, ch)
        ch = DMA("sync", rowtmp, dr["gam1_row"][:, :], [ch])
        ch = v_tt(rowtmp, rowtmp, ROW_ONB, OP.mult, waits=[ch])
        ch = bcast_row(bob, rowtmp, ch)
        ch = bcast_row(g2b, ROW_G2, ch)
        ch = v_cp(Bxc[0:1, 0:1024], ROW_GB, [ch])
        ch = v_cp(HI32, Bxc[0:1, 0:1024], [ch])
        ch = v_tt(LO32, ROW_GB, HI32, OP.subtract, waits=[ch])
        ch = v_cp(Bxc[0:1, 1024:2048], LO32, [ch])
        ch = DMA("gpsimd", gbhl[0:2, :], HL16, [ch], sem="WT")

        conv_specs = []
        for nm in ("qk", "v", "g", "o"):
            for c in range(NCH):
                conv_specs.append(("plain", dr[nm + "_wT"][c * 128:(c + 1) * 128, :],
                                   sc[nm][c, :, :], None))
        for f in range(32):
            conv_specs.append(("g2", dr["down_wT"][f * 128:(f + 1) * 128, :],
                               sc["down"][f, :, :], None))
        for f in range(128):
            conv_specs.append(("plain", dr["wc_Tp"][f * 128:(f + 1) * 128, :],
                               sc["wc"][f, :, :], None))
        for wi, nm in (("gate_t4", "gate"), ("up_t4", "up")):
            for m in range(32):
                conv_specs.append(("gu", dr[wi][m, :, :, :].rearrange("c p w -> p c w"),
                                   sc[nm][m, :, :], (nm, m)))
        slot_tok = {0: ch, 1: ch}
        prev_v = ch
        prev_beta = ch
        beta_last = {"gate": None, "up": None}
        for i, (mode, src, dst, extra) in enumerate(conv_specs):
            par = i % 2
            wdst = Wc32[:, par * D:(par + 1) * D]
            if mode == "gu":
                wdst = wdst.rearrange("p (c w) -> p c w", w=128)
            lw = DMA("gpsimd", wdst, src, [slot_tok[par]], sem="V%d" % par)
            if mode == "g2":
                tv = v_tt(Wc16[:, par * D:(par + 1) * D], Wc32[:, par * D:(par + 1) * D],
                          g2b[:, :], OP.mult, waits=[lw, prev_v])
            elif mode == "gu":
                tv = prev_v
                for c in range(NCH):
                    tv = v_ts(Wc16[:, par * D + c * 128:par * D + (c + 1) * 128],
                              Wc32[:, par * D + c * 128:par * D + (c + 1) * 128],
                              cols[:, 24 + c:25 + c], None, OP.mult, waits=[lw, tv])
            else:
                tv = v_cp(Wc16[:, par * D:(par + 1) * D], Wc32[:, par * D:(par + 1) * D],
                          [lw, prev_v])
            st = DMA("gpsimd", dst, Wc16[:, par * D:(par + 1) * D], [tv],
                     sem="U%d" % par)
            new_tok = st
            if mode == "gu":
                nm_, m_ = extra
                pm = None
                for c in range(NCH):
                    pm = t_mm(ps3[0:1, 0:128], cols[:, 32 + c:33 + c],
                              Wc32[:, par * D + c * 128:par * D + (c + 1) * 128],
                              c == 0, c == 7,
                              waits=[lw, prev_beta] if c == 0 else [])
                bfn = betaG_ap if nm_ == "gate" else betaU_ap
                be = s_cp(bfn(m_, 0, 128), ps3[0:1, 0:128], [pm])
                prev_beta = be
                beta_last[nm_] = be
                new_tok = be
            slot_tok[par] = new_tok
            prev_v = tv
        CONV_END = [slot_tok[0], slot_tok[1], prev_beta]

        ch2 = prev_beta
        for m in range(32):
            tt = t_tr(ps4[:, 0:1], betaG_ap(m, 0, 128), eye_s[0:1, 0:1], [ch2])
            ch2 = s_cp(bgcol[:, m:m + 1], ps4[:, 0:1], [tt])
        for m in range(32):
            tt = t_tr(ps4[:, 0:1], betaU_ap(m, 0, 128), eye_s[0:1, 0:1], [ch2])
            ch2 = s_cp(bucol[:, m:m + 1], ps4[:, 0:1], [tt])

        ctok = [ch2] + CONV_END
        last_c = ch2
        for hq in range(4):
            for cq in range(8):
                l1 = DMA("gpsimd", Wc32[:, 0:128], dr["mk_bd"][hq, cq, :, :],
                         [last_c] + ctok, sem="V0")
                t1 = v_cp(Wc16[:, 0:128], Wc32[:, 0:128], [l1, last_c])
                for gq in range(4):
                    l2 = DMA("gpsimd", Wc32[:, 128:256], dr["up_bd"][hq, cq, gq, :, :],
                             [last_c], sem="V1")
                    t2 = v_cp(Wc16[:, 128:256], Wc32[:, 128:256], [l2, t1])
                    mm = t_mm(ps3[:, 0:128], Wc16[:, 0:128], Wc16[:, 128:256],
                              True, True, waits=[t2, last_c])
                    ti = ((hq * 8) + cq) * 4 + gq
                    last_c = s_cp(Ct[:, ti * 128:(ti + 1) * 128], ps3[:, 0:128], [mm])
        PRO_DONE = [last_c, P.tok("WT"), P.tok("ST"), P.tok("VC"), P.tok("SC"),
                    P.tok("TN"), P.tok("GP"), P.tok("V0"), P.tok("V1"),
                    P.tok("U0"), P.tok("U1"), P.tok("LD")]

        def layer_norm(src, dst, w):
            w1 = v_tt(Bsq[:, 0:D], src, src, OP.mult, waits=w)
            w2 = v_red(stat[:, 0:1], src, waits=w)
            w3 = v_red(stat[:, 1:2], Bsq[:, 0:D], waits=[w1])
            w4 = v_ts(stat[:, 2:3], stat[:, 0:1], R1024, None, OP.mult, waits=[w2])
            w5 = v_ts(stat[:, 3:4], stat[:, 1:2], R1024, None, OP.mult, waits=[w3])
            w6 = v_tt(stat[:, 4:5], stat[:, 2:3], stat[:, 2:3], OP.mult, waits=[w4])
            w7 = v_tt(stat[:, 5:6], stat[:, 3:4], stat[:, 4:5], OP.subtract,
                      waits=[w5, w6])
            w8 = v_ts(stat[:, 5:6], stat[:, 5:6], EPS, None, OP.add, waits=[w7])
            w9 = rsqrt_pos(stat[:, 6:7], stat[:, 5:6], stat[:, 7:8], [w8])
            return v_ts(dst, src, stat[:, 2:3], stat[:, 6:7], OP.subtract, OP.mult,
                        waits=[w9])

        CH = PRO_DONE
        slot_hist = {"q": [list(PRO_DONE)] * 2, "gu": [list(PRO_DONE)] * 4,
                     "dn": [list(PRO_DONE)] * 4, "cp": [list(PRO_DONE)] * 4}
        prev_x_read = [PRO_DONE[0]]
        prev_cos_read = [PRO_DONE[0]]
        prev_store = [PRO_DONE[0]]
        hv = Bhfm[:, :].rearrange("p (c w) -> p c w", w=HMT)
        t2v = Bt2[:, :].rearrange("p (c w) -> p c w", w=D)
        t2c = Bt2[:, 0:NCH * MT].rearrange("p (c w) -> p c w", w=MT)
        xcv = Bxc[:, :].rearrange("p (c w) -> p c w", w=MT)

        for b in range(nseq):
            z = v_ms(carry, 0.0, CH)
            z = v_ms(ktail, 0.0, [z])
            z = v_ms(hv[:, :, 0:3], 0.0, [z])
            CH = [z]
            for m in range(nmt_per_seq):
                t0 = m * MT
                gb0 = t0 // 128

                lx = []
                for blk in range(NBLK):
                    lx.append(DMA("sync", Bx[:, blk * D:(blk + 1) * D],
                                  dr["x"][b, t0 + blk * 128:t0 + (blk + 1) * 128, :],
                                  [prev_x_read[0]], sem="X%d" % blk))
                    lx.append(DMA("sync", Bcos[:, blk * D:(blk + 1) * D],
                                  sc["cos"][t0 + blk * 128:t0 + (blk + 1) * 128, :],
                                  [prev_cos_read[0]], sem="O%d" % blk))

                ln1t = CH
                ln1 = []
                for blk in range(NBLK):
                    tln = layer_norm(Bx[:, blk * D:(blk + 1) * D],
                                     Bt1[:, blk * D:(blk + 1) * D],
                                     [lx[2 * blk]] + ln1t)
                    ln1.append(tln)
                    ln1t = [tln]

                if m > 0:
                    hcp = v_cp(hv[:, :, 0:3], hv[:, :, MT:MT + 3], CH)
                else:
                    hcp = CH[0]
                evs = hcp
                par = 0
                for blk in range(NBLK):
                    for c in range(NCH):
                        pt = psZ[par]
                        tt = t_tr(pt[:, 0:128],
                                  Bt1[:, blk * D + c * 128:blk * D + (c + 1) * 128],
                                  eye_s[:, :], waits=[ln1[blk], evs])
                        evs = s_cp(
                            Bhfm[:, c * HMT + 3 + blk * 128:c * HMT + 3 + (blk + 1) * 128],
                            pt[:, 0:128], [tt])
                        par ^= 1
                TRX = evs

                cv = TRX
                for c in range(NCH):
                    cv = v_ts(hv[:, c:c + 1, 3:3 + MT], hv[:, c:c + 1, 3:3 + MT],
                              cols[:, c:c + 1], cols[:, 8 + c:9 + c], OP.mult, OP.add,
                              waits=[cv])
                for c in range(NCH):
                    cv = v_ts(t2c[:, c:c + 1, :], hv[:, c:c + 1, 0:MT],
                              cwcol[:, c * 4:c * 4 + 1], cols[:, 16 + c:17 + c],
                              OP.mult, OP.add, waits=[cv])
                for k in range(1, 4):
                    for c in range(NCH):
                        dst = xcv[:, c:c + 1, :] if k == 3 else t2c[:, c:c + 1, :]
                        cv = v_stt(dst, hv[:, c:c + 1, k:k + MT],
                                   cwcol[:, c * 4 + k:c * 4 + k + 1],
                                   t2c[:, c:c + 1, :], OP.mult, OP.add, waits=[cv])
                XC = cv

                pair_cycle = [(ps1, ps2), (ps3, ps4), (ps5, ps6)]
                pair_evac = {0: XC, 1: XC, 2: XC}
                last_ev = XC
                for ti, (nm, dstb) in enumerate((("qk", Bk), ("v", Bv), ("g", Bg))):
                    mend = {}
                    for c in range(NCH):
                        par = c % 2
                        lw = DMA("gpsimd", Wq[:, par * D:(par + 1) * D], sc[nm][c, :, :],
                                 slot_hist["q"][par], sem="Q%d" % par)
                        for blk in range(NBLK):
                            pr = (ti * NBLK + blk) % 3
                            pa, pb = pair_cycle[pr]
                            w_ = [lw, XC, pair_evac[pr]] if c == 0 else [lw]
                            last = (c == 7) and nm != "g"
                            t_mm(pa[:, 0:512],
                                 Bxc[:, c * MT + blk * 128:c * MT + (blk + 1) * 128],
                                 Wq[:, par * D:par * D + 512], c == 0, last, waits=w_)
                            mend[blk] = t_mm(pb[:, 0:512],
                                             Bxc[:, c * MT + blk * 128:c * MT + (blk + 1) * 128],
                                             Wq[:, par * D + 512:(par + 1) * D],
                                             c == 0, last)
                        slot_hist["q"][par] = [mend[NBLK - 1]]
                    if nm == "g":
                        for blk in range(NBLK):
                            pr = (ti * NBLK + blk) % 3
                            pa, pb = pair_cycle[pr]
                            t_mm(pa[:, 0:512], ones2b[:, :], gbhl[:, 0:512], False, True)
                            mend[blk] = t_mm(pb[:, 0:512], ones2b[:, :],
                                             gbhl[:, 512:1024], False, True)
                    for blk in range(NBLK):
                        pr = (ti * NBLK + blk) % 3
                        pa, pb = pair_cycle[pr]
                        if nm == "g":
                            e1 = s_act(dstb[:, blk * D:blk * D + 512], pa[:, 0:512],
                                       AF.Sigmoid, waits=[mend[blk], last_ev])
                            e2 = s_act(dstb[:, blk * D + 512:(blk + 1) * D], pb[:, 0:512],
                                       AF.Sigmoid, waits=[e1])
                        else:
                            e1 = s_cp(dstb[:, blk * D:blk * D + 512], pa[:, 0:512],
                                      [mend[blk], last_ev])
                            e2 = s_cp(dstb[:, blk * D + 512:(blk + 1) * D], pb[:, 0:512],
                                      [e1])
                        pair_evac[pr] = e2
                        last_ev = e2
                QKVG = last_ev

                kk = QKVG
                for blk in range(NBLK):
                    kk = v_tt(Bsq[:, 0:D], Bk[:, blk * D:(blk + 1) * D],
                              Bk[:, blk * D:(blk + 1) * D], OP.mult, waits=[kk])
                    kk = v_red(kn[:, 32:48],
                               Bsq[:, 0:D].rearrange("p (h w) -> p h w", w=HD),
                               waits=[kk])
                    kk = v_ts(kn[:, 32:48], kn[:, 32:48], -1e-24, None, OP.add, waits=[kk])
                    kk = s_act(kn[:, 32:48], kn[:, 32:48], AF.Relu, waits=[kk])
                    kk = v_ts(kn[:, 32:48], kn[:, 32:48], 1e-24, None, OP.add, waits=[kk])
                    kk = rsqrt_pos(kn[:, 0:16], kn[:, 32:48], kn[:, 16:32], [kk])
                    for h in range(H):
                        kk = v_ts(Bk[:, blk * D + h * HD:blk * D + (h + 1) * HD],
                                  Bk[:, blk * D + h * HD:blk * D + (h + 1) * HD],
                                  kn[:, h:h + 1], None, OP.mult, waits=[kk])
                mix = kk
                cu_prev = kk

                for blk in range(NBLK):
                    gbi = gb0 + blk
                    m1 = t_mm(ps5[:, 0:512], shf_s[:, :], Bk[:, blk * D:blk * D + 512],
                              True, False, waits=[mix])
                    m2 = t_mm(ps6[:, 0:512], shf_s[:, :],
                              Bk[:, blk * D + 512:(blk + 1) * D], True, False)
                    if blk == 0:
                        m1 = t_mm(ps5[:, 0:512], e0r, ktail[0:1, 0:512], False, True)
                        m2 = t_mm(ps6[:, 0:512], e0r, ktail[0:1, 512:1024], False, True)
                    else:
                        m1 = t_mm(ps5[:, 0:512], sel_last_s[:, :],
                                  Bk[:, 0:512], False, True)
                        m2 = t_mm(ps6[:, 0:512], sel_last_s[:, :],
                                  Bk[:, 512:1024], False, True)
                    s1 = v_tt(Bt1[:, blk * D:blk * D + 512],
                              Bcos[:, blk * D:blk * D + 512], ps5[:, 0:512], OP.add,
                              waits=[m1, lx[2 * blk + 1]])
                    s1 = v_tt(Bt1[:, blk * D + 512:(blk + 1) * D],
                              Bcos[:, blk * D + 512:(blk + 1) * D], ps6[:, 0:512],
                              OP.add, waits=[m2, s1])
                    s2 = v_tt(Bg[:, blk * D:(blk + 1) * D], Bv[:, blk * D:(blk + 1) * D],
                              Bg[:, blk * D:(blk + 1) * D], OP.mult, waits=[s1])
                    s3 = v_tt(Bt1[:, blk * D:(blk + 1) * D], Bt1[:, blk * D:(blk + 1) * D],
                              Bg[:, blk * D:(blk + 1) * D], OP.mult, waits=[s2])
                    for h in range(H):
                        s3 = v_ts(Bt1[:, blk * D + h * HD:blk * D + (h + 1) * HD],
                                  Bt1[:, blk * D + h * HD:blk * D + (h + 1) * HD],
                                  dec[:, gbi * H + h:gbi * H + h + 1], None, OP.mult,
                                  waits=[s3])
                    c1 = t_mm(ps1[:, 0:512], tri_s[:, :], Bt1[:, blk * D:blk * D + 512],
                              True, False, waits=[s3])
                    c1 = t_mm(ps1[:, 0:512], ones1, carry[0:1, 0:512], False, True)
                    c2 = t_mm(ps2[:, 0:512], tri_s[:, :],
                              Bt1[:, blk * D + 512:(blk + 1) * D], True, False)
                    c2 = t_mm(ps2[:, 0:512], ones1, carry[0:1, 512:1024], False, True)
                    b1 = t_mm(ps3[0:1, 0:512], ones1c, Bt1[:, blk * D:blk * D + 512],
                              True, True, waits=[s3, cu_prev])
                    b2 = t_mm(ps4[0:1, 0:512], ones1c,
                              Bt1[:, blk * D + 512:(blk + 1) * D], True, True)
                    cu = v_tt(carry[0:1, 0:512], carry[0:1, 0:512], ps3[0:1, 0:512],
                              OP.add, waits=[c1, b1])
                    cu = v_tt(carry[0:1, 512:1024], carry[0:1, 512:1024],
                              ps4[0:1, 0:512], OP.add, waits=[c2, b2, cu])
                    cu_prev = cu
                    mv = cu
                    for h in range(H):
                        pp = ps1 if h < 8 else ps2
                        off = (h % 8) * HD
                        mv = v_ts(Bv[:, blk * D + h * HD:blk * D + (h + 1) * HD],
                                  pp[:, off:off + HD],
                                  rdec[:, gbi * H + h:gbi * H + h + 1], None, OP.mult,
                                  waits=[c1, c2, mv])
                    mix = mv
                kt1 = t_mm(ps3[0:1, 0:512], selc_s[:, 0:1], Bk[:, D:D + 512],
                           True, True, waits=[mix])
                kt2 = t_mm(ps4[0:1, 0:512], selc_s[:, 0:1], Bk[:, D + 512:2 * D],
                           True, True)
                ktv = s_cp(ktail[0:1, 0:512], ps3[0:1, 0:512], [kt1])
                ktv = s_cp(ktail[0:1, 512:1024], ps4[0:1, 0:512], [ktv, kt2])
                mix = v_tt(Bv[:, :], Bv[:, :], Bk[:, :], OP.mult, waits=[mix, ktv])
                mix = v_cp(Bmb[:, :], Bv[:, :], [mix])

                def fm_transpose(w0):
                    par_ = 0
                    ev_ = w0
                    for blk in range(NBLK):
                        for c in range(NCH):
                            po = par_ * 1024
                            tt = t_tr(psT[:, po:po + 128],
                                      Bmb[:, blk * D + c * 128:blk * D + (c + 1) * 128],
                                      eyeb[:, :], waits=[w0, ev_])
                            ev_ = s_cp(
                                Bfm[:, c * MT + blk * 128:c * MT + (blk + 1) * 128],
                                psT[:, po:po + 128], [tt])
                            par_ ^= 1
                    return ev_
                MIXFM = fm_transpose(mix)

                mend = None
                for c in range(NCH):
                    par = c % 2
                    lw = DMA("gpsimd", Wq[:, par * D:(par + 1) * D], sc["o"][c, :, :],
                             slot_hist["q"][par] + [MIXFM], sem="Q%d" % par)
                    for blk in range(NBLK):
                        pa, pb = (psA, psB)[blk]
                        w_ = [lw, MIXFM] if c == 0 else [lw]
                        t_mm(pa[:, 0:512],
                             Bfm[:, c * MT + blk * 128:c * MT + (blk + 1) * 128],
                             Wq[:, par * D:par * D + 512], c == 0, c == 7, waits=w_)
                        mend = t_mm(pb[:, 0:512],
                                    Bfm[:, c * MT + blk * 128:c * MT + (blk + 1) * 128],
                                    Wq[:, par * D + 512:(par + 1) * D], c == 0, c == 7)
                    slot_hist["q"][par] = [mend]
                ev = mend
                for blk in range(NBLK):
                    pa, pb = (psA, psB)[blk]
                    ev = s_cp(Bt2[:, blk * D:blk * D + 512], pa[:, 0:512], [ev])
                    ev = s_cp(Bt2[:, blk * D + 512:(blk + 1) * D], pb[:, 0:512], [ev])
                on = [ev]
                for blk in range(NBLK):
                    on = [layer_norm(Bt2[:, blk * D:(blk + 1) * D],
                                     Bt2[:, blk * D:(blk + 1) * D], on)]
                x2 = on[0]
                for blk in range(NBLK):
                    x2 = v_tt(Bt3[:, blk * D:(blk + 1) * D], Bt2[:, blk * D:(blk + 1) * D],
                              sob[:, :], OP.mult, waits=[x2])
                    x2 = v_tt(Bt3[:, blk * D:(blk + 1) * D], Bt3[:, blk * D:(blk + 1) * D],
                              bob[:, :], OP.add, waits=[x2])
                    x2 = v_tt(Bk[:, blk * D:(blk + 1) * D], Bt3[:, blk * D:(blk + 1) * D],
                              Bx[:, blk * D:(blk + 1) * D], OP.add, waits=[x2])

                l2t = [x2]
                for blk in range(NBLK):
                    l2t = [layer_norm(Bk[:, blk * D:(blk + 1) * D],
                                      Bt2[:, blk * D:(blk + 1) * D], l2t)]
                l2c = v_cp(Bmb[:, :], Bt2[:, :], l2t)
                H2FM = fm_transpose(l2c)

                gevs = []
                for mi in range(32):
                    par = mi % 4
                    lg = DMA("gpsimd", Wgu[:, par * 2048:par * 2048 + 1024],
                             sc["gate"][mi, :, :], slot_hist["gu"][par], sem="G%d" % par)
                    lu = DMA("gpsimd", Wgu[:, par * 2048 + 1024:(par + 1) * 2048],
                             sc["up"][mi, :, :], slot_hist["gu"][par], sem="G%d" % par)
                    pg = (ps5, ps3)[mi % 2]
                    pu = (ps6, ps4)[mi % 2]
                    mg = mu = None
                    for c in range(NCH):
                        w_ = [lg, lu, H2FM]
                        if c == 0 and mi >= 2:
                            w_.append(gevs[mi - 2])
                        mg = t_mm(pg[:, 0:MT],
                                  Wgu[:, par * 2048 + c * 128:par * 2048 + (c + 1) * 128],
                                  Bfm[:, c * MT:(c + 1) * MT], c == 0, c == 7,
                                  waits=w_ if c == 0 else [lg])
                        mu = t_mm(pu[:, 0:MT],
                                  Wgu[:, par * 2048 + 1024 + c * 128:par * 2048 + 1024 + (c + 1) * 128],
                                  Bfm[:, c * MT:(c + 1) * MT], c == 0, c == 7)
                    slot_hist["gu"][par] = [mu]
                    pp2 = mi % 2
                    sl = s_act(Bsl[:, pp2 * MT:(pp2 + 1) * MT], pg[:, 0:MT], AF.Silu,
                               bias=bgcol[:, mi:mi + 1], waits=[mu])
                    gv = v_stt(Bgu[:, mi * MT:(mi + 1) * MT], pu[:, 0:MT],
                               bucol[:, mi:mi + 1], Bsl[:, pp2 * MT:(pp2 + 1) * MT],
                               OP.add, OP.mult, waits=[sl])
                    gevs.append(gv)
                GU = gevs[-1]

                dm = None
                for f in range(32):
                    par = f % 4
                    lw = DMA("gpsimd", Wdn[:, par * 1024:(par + 1) * 1024],
                             sc["down"][f, :, :], slot_hist["dn"][par], sem="D%d" % par)
                    for blk in range(NBLK):
                        pa, pb = (psA, psB)[blk]
                        w_ = [lw, GU] if f == 0 else [lw]
                        t_mm(pa[:, 0:512],
                             Bgu[:, f * MT + blk * 128:f * MT + (blk + 1) * 128],
                             Wdn[:, par * 1024:par * 1024 + 512], f == 0, f == 31,
                             waits=w_)
                        dm = t_mm(pb[:, 0:512],
                                  Bgu[:, f * MT + blk * 128:f * MT + (blk + 1) * 128],
                                  Wdn[:, par * 1024 + 512:(par + 1) * 1024],
                                  f == 0, f == 31)
                    slot_hist["dn"][par] = [dm]
                yv = dm
                for blk in range(NBLK):
                    pa, pb = (psA, psB)[blk]
                    yv = v_tt(Bt3[:, blk * D:blk * D + 512], pa[:, 0:512],
                              Bk[:, blk * D:blk * D + 512], OP.add, waits=[yv])
                    yv = v_tt(Bt3[:, blk * D + 512:(blk + 1) * D], pb[:, 0:512],
                              Bk[:, blk * D + 512:(blk + 1) * D], OP.add, waits=[yv])
                    yv = v_tt(Bt3[:, blk * D:(blk + 1) * D], Bt3[:, blk * D:(blk + 1) * D],
                              Bx[:, blk * D:(blk + 1) * D], OP.add, waits=[yv])

                ol = [yv]
                for blk in range(NBLK):
                    ol = [layer_norm(Bt3[:, blk * D:(blk + 1) * D],
                                     Bt1[:, blk * D:(blk + 1) * D], ol)]
                ol2 = v_cp(Bmb[:, :], Bt1[:, :], ol)
                XFFM = fm_transpose(ol2)

                cp_last = XFFM
                ge = XFFM
                for f in range(128):
                    par = f % 4
                    lw = DMA("sync", Wcp[:, par * 1024:(par + 1) * 1024],
                             sc["wc"][f, :, :], slot_hist["cp"][par], sem="C%d" % par)
                    pz = psZ[f % 2]
                    cq = (f // 4) % 8
                    zm = t_mm(pz[:, 0:MT], Ct[:, f * 128:(f + 1) * 128],
                              Bfm[:, cq * MT:(cq + 1) * MT], True, True,
                              waits=[XFFM, cp_last])
                    pp2 = f % 2
                    ge = s_act(Bha[:, pp2 * MT:(pp2 + 1) * MT], pz[:, 0:MT], AF.Gelu,
                               waits=[zm])
                    for blk in range(NBLK):
                        pa, pb = (psA, psB)[blk]
                        w_ = [ge, lw]
                        t_mm(pa[:, 0:512],
                             Bha[:, pp2 * MT + blk * 128:pp2 * MT + (blk + 1) * 128],
                             Wcp[:, par * 1024:par * 1024 + 512], f == 0, f == 127,
                             waits=w_)
                        cp_last = t_mm(pb[:, 0:512],
                                       Bha[:, pp2 * MT + blk * 128:pp2 * MT + (blk + 1) * 128],
                                       Wcp[:, par * 1024 + 512:(par + 1) * 1024],
                                       f == 0, f == 127)
                    slot_hist["cp"][par] = [cp_last]
                fin = cp_last
                for blk in range(NBLK):
                    pa, pb = (psA, psB)[blk]
                    fin = v_tt(Bt2[:, blk * D:blk * D + 512], pa[:, 0:512],
                               Bt1[:, blk * D:blk * D + 512], OP.add, waits=[fin])
                    fin = v_tt(Bt2[:, blk * D + 512:(blk + 1) * D], pb[:, 0:512],
                               Bt1[:, blk * D + 512:(blk + 1) * D], OP.add, waits=[fin])
                fl = [fin]
                for blk in range(NBLK):
                    fl = [layer_norm(Bt2[:, blk * D:(blk + 1) * D],
                                     Bo[:, blk * D:(blk + 1) * D], fl)]
                st = None
                for blk in range(NBLK):
                    st = DMA("sync", out_d[b, t0 + blk * 128:t0 + (blk + 1) * 128, :],
                             Bo[:, blk * D:(blk + 1) * D], fl, sem="ST")
                prev_x_read[0] = yv
                prev_cos_read[0] = mix
                prev_store[0] = st
                CH = [fl[0], st]

        E("sync", lambda e: e.nop(), [P.tok("ST")])

        @block.sync
        def _(e):
            P.replay("sync", e, sems)

        @block.gpsimd
        def _(e):
            P.replay("gpsimd", e, sems)

        @block.vector
        def _(e):
            P.replay("vector", e, sems)

        @block.scalar
        def _(e):
            P.replay("scalar", e, sems)

        @block.tensor
        def _(e):
            P.replay("tensor", e, sems)

    return nc


def _replay(self, eng, e, sems):
    for fn, ws, inc, inc_n in self.stream[eng]:
        for (s, v) in ws:
            e.wait_ge(sems[s], v)
        ins = fn(e)
        if inc is not None:
            if ins is not None:
                ins.then_inc(sems[inc], inc_n)
            else:
                e.sem_inc(sems[inc], inc_n)

Prog.replay = _replay


def kernel(**inputs):
    x = np.asarray(inputs["L0_attn_x"], dtype=np.float32)
    prep = host_prep(inputs)
    nc = build_nc()
    in_maps = []
    for c in range(NCORES):
        mm = dict(prep)
        mm["x"] = np.ascontiguousarray(x[c * BPC:(c + 1) * BPC])
        in_maps.append(mm)
    res = run_bass_kernel_spmd(nc, in_maps, core_ids=list(range(NCORES)))
    out = np.concatenate([np.asarray(r["out"]) for r in res.results], axis=0)
    return out.astype(np.float32)


# version 03
# speedup vs baseline: 1.2976x; 1.0229x over previous
import math
import numpy as np
import concourse.bass as bass
import concourse.mybir as mybir
from concourse.bass_utils import run_bass_kernel_spmd

F32 = mybir.dt.float32
BF16 = mybir.dt.bfloat16
AF = mybir.ActivationFunctionType
OP = mybir.AluOpType
AX = mybir.AxisListType

D, H, HD, FF, NF = 1024, 16, 64, 4096, 16384
TSEQ, NCORES, BPC = 2048, 8, 2
MT = 256
NBLK = MT // 128
NMT = TSEQ // MT
NCH = D // 128
HMT = 3 + MT
EPS = 1e-5
R1024 = 1.0 / 1024.0


def _split8(v, n):
    out = []
    for _ in range(n):
        c = (np.array(v, np.float32).view(np.uint32) & np.uint32(0xFFFF0000)).view(np.float32)
        out.append(float(c))
        v = v - float(c)
    return out

CW2PI = _split8(2 * math.pi, 6)
_l10 = math.log(10.0)
L10H = float(np.float32(_l10)); L10L = float(np.float32(_l10 - L10H))
_l2 = math.log(2.0)
LN2H = float(np.float32(_l2)); LN2L = float(np.float32(_l2 - LN2H))
PIO2 = float(np.float32(math.pi / 2))


def host_constants():
    c = {}
    t = np.arange(TSEQ, dtype=np.float64)
    idx64 = np.arange(HD, dtype=np.float64) / 63.0
    freqs64 = (np.arange(H, dtype=np.float64)[:, None] * (10.0 ** idx64)[None, :]).reshape(-1)
    ang64 = t[:, None] * freqs64[None, :]
    c["ntab"] = np.asarray(np.rint(ang64 / (2 * np.pi)), dtype=np.float32)
    kd = np.rint(idx64 * np.log2(10.0))
    c["ktab"] = np.ascontiguousarray(np.broadcast_to(kd.astype(np.float32), (128, HD)))
    c["twok"] = np.ascontiguousarray(np.broadcast_to((2.0 ** kd).astype(np.float32), (128, HD)))
    c["tri"] = np.triu(np.ones((128, 128), dtype=np.float32))
    c["shift"] = np.eye(128, k=1, dtype=np.float32)
    c["eye"] = np.eye(128, dtype=np.float32)
    selc = np.zeros((128, 2), dtype=np.float32)
    selc[127, 0] = 1.0
    selc[:, 1] = 1.0
    c["selc"] = selc
    sl = np.zeros((128, 128), dtype=np.float32)
    sl[127, 0] = 1.0
    c["sel_last"] = sl
    return c


def _col8(v):
    return np.ascontiguousarray(np.asarray(v, np.float32).reshape(NCH, 128).T)


def host_prep(inp):
    g = lambda k: np.asarray(inp[k], dtype=np.float32)
    d = {}
    d["qk_wT"] = np.ascontiguousarray(g("L0_attn_qk_w").T)
    d["v_wT"] = np.ascontiguousarray(g("L0_attn_v_w").T)
    d["g_wT"] = np.ascontiguousarray(g("L0_attn_g_w").T)
    d["o_wT"] = np.ascontiguousarray(g("L0_attn_o_w").T)
    for nm, key in (("gate_t4", "L0_attn_gate_w"), ("up_t4", "L0_attn_up_w")):
        wT = g(key).T
        d[nm] = np.ascontiguousarray(wT.reshape(NCH, 128, 32, 128).transpose(2, 0, 1, 3))
    d["down_wT"] = np.ascontiguousarray(g("L0_attn_down_w").T)
    wcT = np.ascontiguousarray(g("L0_ffn_w_cproj").T)
    hh, cc, gg = np.meshgrid(np.arange(4), np.arange(8), np.arange(4), indexing="ij")
    base = ((hh * 4096)[..., None, None]
            + ((cc * 8)[..., None, None] + np.arange(8)[None, None, None, :, None]) * 64
            + (gg * 16)[..., None, None]
            + np.arange(16)[None, None, None, None, :])
    d["wc_Tp"] = np.ascontiguousarray(wcT[base.reshape(-1)])
    mk6 = g("L0_ffn_rail_maker").reshape(4, 8, 8, 16, 16)
    up6 = g("L0_ffn_rail_up_proj").reshape(4, 8, 8, 4, 16, 16)
    mbd = np.zeros((4, 8, 128, 128), dtype=np.float32)
    ubd = np.zeros((4, 8, 4, 128, 128), dtype=np.float32)
    for rho in range(8):
        s = slice(rho * 16, rho * 16 + 16)
        mbd[:, :, s, s] = mk6[:, :, rho].transpose(0, 1, 3, 2)
        ubd[:, :, :, s, s] = up6[:, :, rho].transpose(0, 1, 2, 4, 3)
    d["mk_bd"] = mbd
    d["up_bd"] = np.ascontiguousarray(ubd)
    d["convw_col"] = np.ascontiguousarray(
        g("L0_attn_conv_w")[:, 0, :].reshape(NCH, 128, 4).transpose(1, 0, 2).reshape(128, 32))
    for nm, key in (("ln1w", "L0_attn_ln1_w"), ("ln1b", "L0_attn_ln1_b"),
                    ("convb", "L0_attn_conv_b"), ("ln2w", "L0_attn_ln2_w"),
                    ("ln2b", "L0_attn_ln2_b")):
        d[nm + "_col"] = _col8(g(key))
    for nm, key in (("gb", "L0_attn_g_b"), ("onw", "L0_attn_onorm_w"),
                    ("onb", "L0_attn_onorm_b"), ("gam1", "L0_attn_gamma1"),
                    ("gam2", "L0_attn_gamma2")):
        d[nm + "_row"] = np.ascontiguousarray(g(key).reshape(1, D))
    d["gamma"] = np.ascontiguousarray(g("L0_attn_gamma_attn").reshape(H, 1))
    d.update(host_constants())
    return d



class Prog:
    ENG = ("sync", "gpsimd", "vector", "scalar", "tensor")

    def __init__(self):
        self.stream = {e: [] for e in self.ENG}
        self.count = {}
        self.hw = {e: {} for e in self.ENG}

    def emit(self, eng, fn, waits=(), inc=None, inc_n=1):
        ws = []
        for w in waits:
            if w is None:
                continue
            s, v = w
            if v > 0 and self.hw[eng].get(s, 0) < v:
                self.hw[eng][s] = v
                ws.append((s, v))
        if inc is not None:
            self.count[inc] = self.count.get(inc, 0) + inc_n
        self.stream[eng].append((fn, tuple(ws), inc, inc_n))
        return (inc, self.count[inc]) if inc is not None else None

    def tok(self, sem):
        return (sem, self.count.get(sem, 0))


def build_nc(nmt_per_seq=NMT, nseq=BPC):
    nc = bass.Bass("TRN2", target_bir_lowering=False)
    P = Prog()

    dr = {}
    def din(name, shape, dt=F32):
        dr[name] = nc.dram_tensor(name, list(shape), dt, kind="ExternalInput").ap()

    din("x", (BPC, TSEQ, D))
    for nm in ("qk_wT", "v_wT", "g_wT", "o_wT"):
        din(nm, (D, D))
    din("gate_t4", (32, NCH, 128, 128)); din("up_t4", (32, NCH, 128, 128))
    din("down_wT", (FF, D)); din("wc_Tp", (NF, D))
    din("mk_bd", (4, 8, 128, 128)); din("up_bd", (4, 8, 4, 128, 128))
    din("convw_col", (128, 32))
    for nm in ("ln1w_col", "ln1b_col", "convb_col", "ln2w_col", "ln2b_col"):
        din(nm, (128, NCH))
    for nm in ("gb_row", "onw_row", "onb_row", "gam1_row", "gam2_row"):
        din(nm, (1, D))
    din("gamma", (H, 1)); din("ntab", (TSEQ, D))
    din("ktab", (128, HD)); din("twok", (128, HD))
    din("tri", (128, 128)); din("shift", (128, 128)); din("eye", (128, 128))
    din("selc", (128, 2)); din("sel_last", (128, 128))
    out_d = nc.dram_tensor("out", [BPC, TSEQ, D], F32, kind="ExternalOutput").ap()

    sc = {
        "qk": nc.dram_tensor("s_qk", [NCH, 128, D], BF16).ap(),
        "v": nc.dram_tensor("s_v", [NCH, 128, D], BF16).ap(),
        "g": nc.dram_tensor("s_g", [NCH, 128, D], BF16).ap(),
        "o": nc.dram_tensor("s_o", [NCH, 128, D], BF16).ap(),
        "gate": nc.dram_tensor("s_gate", [32, 128, D], BF16).ap(),
        "up": nc.dram_tensor("s_up", [32, 128, D], BF16).ap(),
        "down": nc.dram_tensor("s_down", [32, 128, D], BF16).ap(),
        "wc": nc.dram_tensor("s_wc", [128, 128, D], BF16).ap(),
        "ct": nc.dram_tensor("s_ct", [128, 128, 128], BF16).ap(),
        "cos": nc.dram_tensor("s_cos", [TSEQ, D], F32).ap(),
    }

    Wct = nc.alloc_sbuf_tensor("Wct", [128, 2 * 4 * 128], BF16)
    Bx2 = nc.alloc_sbuf_tensor("Bx2", [128, 2048], F32)
    dec = nc.alloc_sbuf_tensor("dec", [128, 16 * H], F32)
    rdec = nc.alloc_sbuf_tensor("rdec", [128, 16 * H], F32)
    sob = nc.alloc_sbuf_tensor("sob", [128, D], F32)
    bob = nc.alloc_sbuf_tensor("bob", [128, D], F32)
    g2b = nc.alloc_sbuf_tensor("g2b", [128, D], F32)
    cols = nc.alloc_sbuf_tensor("cols", [128, 48], F32)
    cwcol = nc.alloc_sbuf_tensor("cwcol", [128, 32], F32)
    bgcol = nc.alloc_sbuf_tensor("bgcol", [128, 32], F32)
    bucol = nc.alloc_sbuf_tensor("bucol", [128, 32], F32)
    mmrow = nc.alloc_sbuf_tensor("mmrow", [1, 2304], F32)
    gbhl = nc.alloc_sbuf_tensor("gbhl", [2, D], BF16)
    hl = nc.alloc_sbuf_tensor("hl", [2, D], BF16)
    selc_s = nc.alloc_sbuf_tensor("selc_s", [128, 2], F32)
    sel_last_s = nc.alloc_sbuf_tensor("sel_last_s", [128, 128], F32)
    tri_s = nc.alloc_sbuf_tensor("tri_s", [128, 128], F32)
    shf_s = nc.alloc_sbuf_tensor("shf_s", [128, 128], F32)
    eye_s = nc.alloc_sbuf_tensor("eye_s", [128, 128], F32)
    eyeb = nc.alloc_sbuf_tensor("eyeb", [128, 128], BF16)
    ones2b = nc.alloc_sbuf_tensor("ones2b", [2, 128], BF16)
    gamsb = nc.alloc_sbuf_tensor("gamsb", [H, 1], F32)
    pcol = nc.alloc_sbuf_tensor("pcol", [H, 1], F32)
    ccol = nc.alloc_sbuf_tensor("ccol", [128, 16], F32)
    stat = nc.alloc_sbuf_tensor("stat", [128, 16], F32)
    kn = nc.alloc_sbuf_tensor("kn", [128, 48], F32)
    Bx = nc.alloc_sbuf_tensor("Bx", [128, NBLK * D], F32)
    Bt1 = nc.alloc_sbuf_tensor("Bt1", [128, NBLK * D], F32)
    Bt2 = nc.alloc_sbuf_tensor("Bt2", [128, NBLK * D], F32)
    Bt3 = nc.alloc_sbuf_tensor("Bt3", [128, NBLK * D], F32)
    Bsq = nc.alloc_sbuf_tensor("Bsq", [128, D], F32)
    Bk = nc.alloc_sbuf_tensor("Bk", [128, NBLK * D], F32)
    Bv = nc.alloc_sbuf_tensor("Bv", [128, NBLK * D], F32)
    Bg = nc.alloc_sbuf_tensor("Bg", [128, NBLK * D], F32)
    Bcos = nc.alloc_sbuf_tensor("Bcos", [128, NBLK * D], F32)
    Bhfm = nc.alloc_sbuf_tensor("Bhfm", [128, NCH * HMT], F32)
    Bxc = nc.alloc_sbuf_tensor("Bxc", [128, NCH * MT], BF16)
    Bmb = nc.alloc_sbuf_tensor("Bmb", [128, NBLK * D], BF16)
    Bfm = nc.alloc_sbuf_tensor("Bfm", [128, NCH * MT], BF16)
    Bgu = nc.alloc_sbuf_tensor("Bgu", [128, 32 * MT], BF16)
    Bha = nc.alloc_sbuf_tensor("Bha", [128, 2 * MT], BF16)
    Bsl = nc.alloc_sbuf_tensor("Bsl", [128, 2 * MT], F32)
    Wq = nc.alloc_sbuf_tensor("Wq", [128, 2 * D], BF16)
    Wgu = nc.alloc_sbuf_tensor("Wgu", [128, 4 * 2048], BF16)
    Wdn = nc.alloc_sbuf_tensor("Wdn", [128, 4 * 1024], BF16)
    Wcp = nc.alloc_sbuf_tensor("Wcp", [128, 4 * 1024], BF16)
    ps1 = nc.alloc_psum_tensor("ps1", [128, 512], F32)
    ps2 = nc.alloc_psum_tensor("ps2", [128, 512], F32)
    ps3 = nc.alloc_psum_tensor("ps3", [128, 512], F32)
    ps4 = nc.alloc_psum_tensor("ps4", [128, 512], F32)
    ps5 = nc.alloc_psum_tensor("ps5", [128, 512], F32)
    ps6 = nc.alloc_psum_tensor("ps6", [128, 512], F32)
    psT = nc.alloc_psum_tensor("psT", [128, 2048], BF16)
    SEM_NAMES = (["LD", "WT", "TN", "VC", "SC", "GP", "ST"]
                 + ["Q%d" % i for i in range(2)]
                 + ["G%d" % i for i in range(4)]
                 + ["D%d" % i for i in range(4)]
                 + ["C%d" % i for i in range(4)]
                 + ["X%d" % i for i in range(2)]
                 + ["O%d" % i for i in range(2)]
                 + ["V%d" % i for i in range(2)]
                 + ["U%d" % i for i in range(2)]
                 + ["T%d" % i for i in range(2)])
    sem_h = {n: nc.alloc_semaphore(n) for n in SEM_NAMES}
    with nc.Block() as block:
        sems = sem_h
        psA = (ps1, ps2)
        psB = (ps3, ps4)
        psZ = (ps5, ps6)

        carry = mmrow[0:1, 0:1024]
        ktail = mmrow[0:1, 1024:2048]
        ones1 = mmrow[0:1, 2048:2176]
        e0r = mmrow[0:1, 2176:2304]
        ROW_GB = Bsq[0:1, 0:1024]
        ROW_ONW = Bcos[0:1, 0:1024]
        ROW_ONB = Bcos[0:1, 1024:2048]
        ROW_G2 = Bhfm[0:1, 0:1024]
        rowtmp = Bhfm[0:1, 1024:2048]
        HI32 = Bk[0:1, 0:1024]
        LO32 = Bk[0:1, 1024:2048]
        HL16 = Bxc[0:1, 0:2048]
        ones1c = selc_s[:, 1:2]

        def betaG_ap(mi, a, bnd):
            t_ = Bv if mi < 16 else Bg
            mo = mi % 16
            return t_[0:1, mo * 128 + a:mo * 128 + bnd]

        def betaU_ap(mi, a, bnd):
            t_ = Bk if mi < 16 else Bt3
            mo = mi % 16
            return t_[0:1, mo * 128 + a:mo * 128 + bnd]
        hl32 = Bt2
        dech = Bx
        Wc32 = Bt1
        Wc16 = Bmb
        Bnt = Bv
        Bfrq = Bg
        Bex = Bk[:, 0:HD]
        Bex2 = Bk[:, HD:2 * HD]
        Bex3 = Bk[:, 2 * HD:3 * HD]
        Bo = Bt3

        def E(eng, fn, waits=(), inc=None, inc_n=1):
            return P.emit(eng, fn, waits, inc, inc_n)

        def DMA(eng, dst, src, waits=(), sem="LD"):
            def fn(e, d_=dst, s_=src):
                return e.dma_start(out=d_, in_=s_)
            return E(eng, fn, waits, inc=sem, inc_n=16)

        def v_tt(out, a, b, op, waits=(), inc="VC"):
            return E("vector", lambda e, o=out, x=a, y=b, p=op:
                     e.tensor_tensor(out=o, in0=x, in1=y, op=p), waits, inc)

        def v_ts(out, a, s1, s2, op0, op1=None, waits=(), inc="VC"):
            if op1 is None:
                return E("vector", lambda e, o=out, x=a, u=s1, p=op0:
                         e.tensor_scalar(out=o, in0=x, scalar1=u, scalar2=None, op0=p),
                         waits, inc)
            return E("vector", lambda e, o=out, x=a, u=s1, w=s2, p=op0, q=op1:
                     e.tensor_scalar(out=o, in0=x, scalar1=u, scalar2=w, op0=p, op1=q),
                     waits, inc)

        def v_stt(out, a, s1, b, op0, op1, waits=(), inc="VC"):
            return E("vector", lambda e, o=out, x=a, u=s1, y=b, p=op0, q=op1:
                     e.scalar_tensor_tensor(out=o, in0=x, scalar=u, in1=y, op0=p, op1=q),
                     waits, inc)

        def v_cp(out, a, waits=(), inc="VC"):
            return E("vector", lambda e, o=out, x=a: e.tensor_copy(out=o, in_=x), waits, inc)

        def v_ms(out, c, waits=(), inc="VC"):
            return E("vector", lambda e, o=out, k=c: e.memset(o, k), waits, inc)

        def v_red(out, a, waits=(), inc="VC"):
            return E("vector", lambda e, o=out, x=a:
                     e.tensor_reduce(out=o, in_=x, axis=AX.X, op=OP.add), waits, inc)

        def s_act(out, a, func, bias=0.0, scale=1.0, waits=(), inc="SC"):
            return E("scalar", lambda e, o=out, x=a, f=func, b=bias, s=scale:
                     e.activation(out=o, in_=x, func=f, bias=b, scale=s), waits, inc)

        def s_cp(out, a, waits=(), inc="SC"):
            return E("scalar", lambda e, o=out, x=a: e.copy(out=o, in_=x), waits, inc)

        def t_mm(psum, lhsT, rhs, start, stop, waits=(), inc="TN"):
            return E("tensor", lambda e, o=psum, l=lhsT, r=rhs, a=start, b=stop:
                     e.matmul(o, l, r, start=a, stop=b), waits, inc)

        def t_tr(psum, src, ident, waits=(), inc="TN"):
            return E("tensor", lambda e, o=psum, x=src, i=ident:
                     e.transpose(o, x, i), waits, inc)

        def recip_pos(dst, src, tmp, waits):
            t1 = s_act(tmp, src, AF.Ln, waits=waits)
            t2 = s_act(dst, tmp, AF.Exp, scale=-1.0, waits=[t1])
            t3 = v_tt(tmp, src, dst, OP.mult, waits=[t2])
            t4 = v_ts(tmp, tmp, -1.0, 2.0, OP.mult, OP.add, waits=[t3])
            return v_tt(dst, dst, tmp, OP.mult, waits=[t4])

        def rsqrt_pos(dst, src, tmp, waits):
            t1 = s_act(tmp, src, AF.Ln, waits=waits)
            t2 = s_act(dst, tmp, AF.Exp, scale=-0.5, waits=[t1])
            t3 = v_tt(tmp, dst, dst, OP.mult, waits=[t2])
            t4 = v_tt(tmp, tmp, src, OP.mult, waits=[t3])
            t5 = v_ts(tmp, tmp, -0.5, 1.5, OP.mult, OP.add, waits=[t4])
            return v_tt(dst, dst, tmp, OP.mult, waits=[t5])

        for dst, src in (
            (tri_s[:, :], dr["tri"][:, :]), (shf_s[:, :], dr["shift"][:, :]),
            (eye_s[:, :], dr["eye"][:, :]),
            (cols[:, 0:8], dr["ln1w_col"][:, :]), (cols[:, 8:16], dr["ln1b_col"][:, :]),
            (cols[:, 16:24], dr["convb_col"][:, :]), (cols[:, 24:32], dr["ln2w_col"][:, :]),
            (cols[:, 32:40], dr["ln2b_col"][:, :]),
            (ROW_GB, dr["gb_row"][:, :]), (ROW_ONW, dr["onw_row"][:, :]),
            (ROW_ONB, dr["onb_row"][:, :]), (ROW_G2, dr["gam2_row"][:, :]),
            (rowtmp, dr["gam1_row"][:, :]),
            (selc_s[:, :], dr["selc"][:, :]), (sel_last_s[:, :], dr["sel_last"][:, :]),
            (gamsb[:, :], dr["gamma"][:, :]),
            (cwcol[:, :], dr["convw_col"][:, :]),
            (Bex2, dr["ktab"][:, :]), (Bex3, dr["twok"][:, :]),
        ):
            DMA("sync", dst, src)
        W0 = [P.tok("LD")]

        ch = v_ms(ones1, 1.0, W0)
        ch = v_ms(e0r, 0.0, [ch])
        ch = v_ms(mmrow[0:1, 2176:2177], 1.0, [ch])
        ch = v_ms(ones2b[:, :], 1.0, [ch])
        ch = v_ms(carry, 0.0, [ch])
        ch = v_ms(ktail, 0.0, [ch])
        ch = v_cp(eyeb[:, :], eye_s[:, :], [ch])

        ch = v_ms(dech[0:H, 0:1], 1.0, [ch])
        ch = v_cp(pcol[:, :], gamsb[:, :], [ch])
        n = 1
        while n < TSEQ:
            nn = min(n, TSEQ - n)
            ch = v_ts(dech[0:H, n:n + nn], dech[0:H, 0:nn], pcol[0:H, 0:1], None,
                      OP.mult, waits=[ch])
            n += nn
            if n < TSEQ:
                ch = v_tt(pcol[:, :], pcol[:, :], pcol[:, :], OP.mult, waits=[ch])
        for tb in range(16):
            tt = t_tr(ps1[:, 0:H], dech[0:H, tb * 128:(tb + 1) * 128],
                      eye_s[0:H, 0:H], [ch])
            ch = s_cp(dec[:, tb * H:(tb + 1) * H], ps1[:, 0:H], [tt])
        ch = v_ts(rdec[:, :], dec[:, :], 1e-6, None, OP.add, waits=[ch])
        ch = recip_pos(Bt1[:, 0:16 * H], rdec[:, :], Bt2[:, 0:16 * H], [ch])
        ch = v_cp(rdec[:, :], Bt1[:, 0:16 * H], [ch])

        ch = E("gpsimd", lambda e: e.iota(Bex, [[1, HD]], channel_multiplier=0,
               allow_small_or_imprecise_dtypes=True), [ch], inc="GP")
        ch = v_ms(stat[:, 0:1], 63.0, [ch])
        ch = recip_pos(stat[:, 1:2], stat[:, 0:1], stat[:, 2:3], [ch])
        ch = v_ts(Bex, Bex, stat[:, 1:2], None, OP.mult, waits=[ch])
        ch = v_ts(Bfrq[:, 0:HD], Bex, L10H, None, OP.mult, waits=[ch])
        ch = v_stt(Bfrq[:, 0:HD], Bex2, -LN2H, Bfrq[:, 0:HD], OP.mult, OP.add, [ch])
        ch = v_ts(Bfrq[:, HD:2 * HD], Bex, L10L, None, OP.mult, waits=[ch])
        ch = v_tt(Bfrq[:, 0:HD], Bfrq[:, 0:HD], Bfrq[:, HD:2 * HD], OP.add, waits=[ch])
        ch = v_stt(Bfrq[:, 0:HD], Bex2, -LN2L, Bfrq[:, 0:HD], OP.mult, OP.add, [ch])
        ch = v_ms(ccol[:, 0:2], 1.0, [ch])
        for j in range(2, 11):
            ch = v_ms(stat[:, 3:4], float(j), [ch])
            ch = recip_pos(stat[:, 4:5], stat[:, 3:4], stat[:, 5:6], [ch])
            ch = v_ts(ccol[:, j:j + 1], ccol[:, j - 1:j], stat[:, 4:5], None, OP.mult, waits=[ch])
        ch = v_ts(Bex2, Bfrq[:, 0:HD], 0.0, ccol[:, 10:11], OP.mult, OP.add, [ch])
        for j in range(9, -1, -1):
            ch = v_tt(Bex2, Bex2, Bfrq[:, 0:HD], OP.mult, waits=[ch])
            ch = v_ts(Bex2, Bex2, ccol[:, j:j + 1], None, OP.add, waits=[ch])
        ch = v_tt(Bex2, Bex2, Bex3, OP.mult, waits=[ch])
        for h in range(H - 1, -1, -1):
            ch = v_ts(Bfrq[:, h * HD:(h + 1) * HD], Bex2, float(h), None, OP.mult, waits=[ch])

        for tb in range(16):
            l1 = DMA("sync", Bnt[:, 0:D], dr["ntab"][tb * 128:(tb + 1) * 128, :], [ch])
            ch = E("gpsimd", lambda e, b=tb: e.iota(stat[:, 6:7], [[1, 1]], base=b * 128,
                   channel_multiplier=1, allow_small_or_imprecise_dtypes=True),
                   [ch], inc="GP")
            ch = v_ts(Bt1[:, 0:D], Bfrq[:, 0:D], stat[:, 6:7], None, OP.mult, waits=[ch])
            for ci in CW2PI:
                ch = v_stt(Bt1[:, 0:D], Bnt[:, 0:D], -ci, Bt1[:, 0:D], OP.mult, OP.add,
                           [ch, l1])
            ch = s_act(Bt1[:, 0:D], Bt1[:, 0:D], AF.Abs, waits=[ch])
            ch = v_ts(Bt2[:, 0:D], Bt1[:, 0:D], -1.0, PIO2, OP.mult, OP.add, waits=[ch])
            ch = s_act(Bt2[:, 0:D], Bt2[:, 0:D], AF.Sin, waits=[ch])
            ch = DMA("sync", sc["cos"][tb * 128:(tb + 1) * 128, :], Bt2[:, 0:D], [ch],
                     sem="ST")

        def bcast_row(dst128, src_row, w):
            w = v_cp(Bxc[0:1, 0:1024], src_row, [w])
            w = v_cp(HI32, Bxc[0:1, 0:1024], [w])
            w = v_tt(LO32, src_row, HI32, OP.subtract, waits=[w])
            w = v_cp(Bxc[0:1, 1024:2048], LO32, [w])
            w = DMA("gpsimd", hl[0:2, :], HL16, [w], sem="WT")
            for sl in range(2):
                pp = (ps1, ps2)[sl]
                m1 = t_mm(pp[:, 0:512], ones2b[:, :], hl[:, sl * 512:(sl + 1) * 512],
                          True, True, waits=[w])
                w = s_cp(dst128[:, sl * 512:(sl + 1) * 512], pp[:, 0:512], [m1])
            return w

        ch = v_tt(rowtmp, rowtmp, ROW_ONW, OP.mult, waits=[ch])
        ch = bcast_row(sob, rowtmp, ch)
        ch = DMA("sync", rowtmp, dr["gam1_row"][:, :], [ch])
        ch = v_tt(rowtmp, rowtmp, ROW_ONB, OP.mult, waits=[ch])
        ch = bcast_row(bob, rowtmp, ch)
        ch = bcast_row(g2b, ROW_G2, ch)
        ch = v_cp(Bxc[0:1, 0:1024], ROW_GB, [ch])
        ch = v_cp(HI32, Bxc[0:1, 0:1024], [ch])
        ch = v_tt(LO32, ROW_GB, HI32, OP.subtract, waits=[ch])
        ch = v_cp(Bxc[0:1, 1024:2048], LO32, [ch])
        ch = DMA("gpsimd", gbhl[0:2, :], HL16, [ch], sem="WT")

        conv_specs = []
        for nm in ("qk", "v", "g", "o"):
            for c in range(NCH):
                conv_specs.append(("plain", dr[nm + "_wT"][c * 128:(c + 1) * 128, :],
                                   sc[nm][c, :, :], None))
        for f in range(32):
            conv_specs.append(("g2", dr["down_wT"][f * 128:(f + 1) * 128, :],
                               sc["down"][f, :, :], None))
        for f in range(128):
            conv_specs.append(("plain", dr["wc_Tp"][f * 128:(f + 1) * 128, :],
                               sc["wc"][f, :, :], None))
        for wi, nm in (("gate_t4", "gate"), ("up_t4", "up")):
            for m in range(32):
                conv_specs.append(("gu", dr[wi][m, :, :, :].rearrange("c p w -> p c w"),
                                   sc[nm][m, :, :], (nm, m)))
        slot_tok = {0: ch, 1: ch}
        prev_v = ch
        prev_beta = ch
        beta_last = {"gate": None, "up": None}
        for i, (mode, src, dst, extra) in enumerate(conv_specs):
            par = i % 2
            wdst = Wc32[:, par * D:(par + 1) * D]
            if mode == "gu":
                wdst = wdst.rearrange("p (c w) -> p c w", w=128)
            lw = DMA("gpsimd", wdst, src, [slot_tok[par]], sem="V%d" % par)
            if mode == "g2":
                tv = v_tt(Wc16[:, par * D:(par + 1) * D], Wc32[:, par * D:(par + 1) * D],
                          g2b[:, :], OP.mult, waits=[lw, prev_v])
            elif mode == "gu":
                tv = prev_v
                for c in range(NCH):
                    tv = v_ts(Wc16[:, par * D + c * 128:par * D + (c + 1) * 128],
                              Wc32[:, par * D + c * 128:par * D + (c + 1) * 128],
                              cols[:, 24 + c:25 + c], None, OP.mult, waits=[lw, tv])
            else:
                tv = v_cp(Wc16[:, par * D:(par + 1) * D], Wc32[:, par * D:(par + 1) * D],
                          [lw, prev_v])
            st = DMA("gpsimd", dst, Wc16[:, par * D:(par + 1) * D], [tv],
                     sem="U%d" % par)
            new_tok = st
            if mode == "gu":
                nm_, m_ = extra
                pm = None
                for c in range(NCH):
                    pm = t_mm(ps3[0:1, 0:128], cols[:, 32 + c:33 + c],
                              Wc32[:, par * D + c * 128:par * D + (c + 1) * 128],
                              c == 0, c == 7,
                              waits=[lw, prev_beta] if c == 0 else [])
                bfn = betaG_ap if nm_ == "gate" else betaU_ap
                be = s_cp(bfn(m_, 0, 128), ps3[0:1, 0:128], [pm])
                prev_beta = be
                beta_last[nm_] = be
                new_tok = be
            slot_tok[par] = new_tok
            prev_v = tv
        CONV_END = [slot_tok[0], slot_tok[1], prev_beta]

        ch2 = prev_beta
        for m in range(32):
            tt = t_tr(ps4[:, 0:1], betaG_ap(m, 0, 128), eye_s[0:1, 0:1], [ch2])
            ch2 = s_cp(bgcol[:, m:m + 1], ps4[:, 0:1], [tt])
        for m in range(32):
            tt = t_tr(ps4[:, 0:1], betaU_ap(m, 0, 128), eye_s[0:1, 0:1], [ch2])
            ch2 = s_cp(bucol[:, m:m + 1], ps4[:, 0:1], [tt])

        ctok = [ch2] + CONV_END
        last_c = ch2
        for hq in range(4):
            for cq in range(8):
                l1 = DMA("gpsimd", Wc32[:, 0:128], dr["mk_bd"][hq, cq, :, :],
                         [last_c] + ctok, sem="V0")
                t1 = v_cp(Wc16[:, 0:128], Wc32[:, 0:128], [l1, last_c])
                for gq in range(4):
                    l2 = DMA("gpsimd", Wc32[:, 128:256], dr["up_bd"][hq, cq, gq, :, :],
                             [last_c], sem="V1")
                    t2 = v_cp(Wc16[:, 128:256], Wc32[:, 128:256], [l2, t1])
                    mm = t_mm(ps3[:, 0:128], Wc16[:, 0:128], Wc16[:, 128:256],
                              True, True, waits=[t2, last_c])
                    ti = ((hq * 8) + cq) * 4 + gq
                    last_c = s_cp(Wc16[:, 256:384], ps3[:, 0:128], [mm])
                    last_c = DMA("gpsimd", sc["ct"][ti, :, :], Wc16[:, 256:384],
                                 [last_c], sem="U0")
        PRO_DONE = [last_c, P.tok("WT"), P.tok("ST"), P.tok("VC"), P.tok("SC"),
                    P.tok("TN"), P.tok("GP"), P.tok("V0"), P.tok("V1"),
                    P.tok("U0"), P.tok("U1"), P.tok("LD")]

        def layer_norm(src, dst, w):
            w1 = v_tt(Bsq[:, 0:D], src, src, OP.mult, waits=w)
            w2 = v_red(stat[:, 0:1], src, waits=w)
            w3 = v_red(stat[:, 1:2], Bsq[:, 0:D], waits=[w1])
            w4 = v_ts(stat[:, 2:3], stat[:, 0:1], R1024, None, OP.mult, waits=[w2])
            w5 = v_ts(stat[:, 3:4], stat[:, 1:2], R1024, None, OP.mult, waits=[w3])
            w6 = v_tt(stat[:, 4:5], stat[:, 2:3], stat[:, 2:3], OP.mult, waits=[w4])
            w7 = v_tt(stat[:, 5:6], stat[:, 3:4], stat[:, 4:5], OP.subtract,
                      waits=[w5, w6])
            w8 = v_ts(stat[:, 5:6], stat[:, 5:6], EPS, None, OP.add, waits=[w7])
            w9 = rsqrt_pos(stat[:, 6:7], stat[:, 5:6], stat[:, 7:8], [w8])
            return v_ts(dst, src, stat[:, 2:3], stat[:, 6:7], OP.subtract, OP.mult,
                        waits=[w9])

        slot_hist = {"q": [list(PRO_DONE)] * 2, "gu": [list(PRO_DONE)] * 4,
                     "dn": [list(PRO_DONE)] * 4, "cp": [list(PRO_DONE)] * 4}
        xread_tok = [PRO_DONE[0], PRO_DONE[0]]
        cosread_tok = [PRO_DONE[0]]
        psT_evac = [PRO_DONE[0], PRO_DONE[0]]
        store_tok = [PRO_DONE[0]]
        fl_tok = [PRO_DONE[0]]
        fin_tok = [PRO_DONE[0]]
        hv = Bhfm[:, :].rearrange("p (c w) -> p c w", w=HMT)
        xcv = Bxc[:, :].rearrange("p (c w) -> p c w", w=MT)
        cacc = Bg[:, 0:NCH * MT].rearrange("p (c w) -> p c w", w=MT)

        def layer_norm(src_, dst, w, co=0, sq=None):
            sqb = Bsq[:, 0:D] if sq is None else sq
            w1 = v_tt(sqb, src_, src_, OP.mult, waits=w)
            w2 = v_red(stat[:, co + 0:co + 1], src_, waits=w)
            w3 = v_red(stat[:, co + 1:co + 2], sqb, waits=[w1])
            w4 = v_ts(stat[:, co + 2:co + 3], stat[:, co + 0:co + 1], R1024, None,
                      OP.mult, waits=[w2])
            w5 = v_ts(stat[:, co + 3:co + 4], stat[:, co + 1:co + 2], R1024, None,
                      OP.mult, waits=[w3])
            w6 = v_tt(stat[:, co + 4:co + 5], stat[:, co + 2:co + 3],
                      stat[:, co + 2:co + 3], OP.mult, waits=[w4])
            w7 = v_tt(stat[:, co + 5:co + 6], stat[:, co + 3:co + 4],
                      stat[:, co + 4:co + 5], OP.subtract, waits=[w5, w6])
            w8 = v_ts(stat[:, co + 5:co + 6], stat[:, co + 5:co + 6], EPS, None,
                      OP.add, waits=[w7])
            w9 = rsqrt_pos(stat[:, co + 6:co + 7], stat[:, co + 5:co + 6],
                           stat[:, co + 7:co + 8], [w8])
            return v_ts(dst, src_, stat[:, co + 2:co + 3], stat[:, co + 6:co + 7],
                        OP.subtract, OP.mult, waits=[w9])

        def front_gen(b, m, prev, pub):
            gi = b * nmt_per_seq + m
            par = gi % 2
            BxP = (Bx, Bx2)[par]
            pub["xpar"] = par
            lxx, lcc = [], []
            for blk in range(NBLK):
                lxx.append(DMA("sync", BxP[:, blk * D:(blk + 1) * D],
                               dr["x"][b, m * MT + blk * 128:m * MT + (blk + 1) * 128, :],
                               [xread_tok[par]], sem="X%d" % blk))
                lcc.append(DMA("sync", Bcos[:, blk * D:(blk + 1) * D],
                               sc["cos"][m * MT + blk * 128:m * MT + (blk + 1) * 128, :],
                               [cosread_tok[0], prev["mix"]], sem="O%d" % blk))
            pub["lcos"] = lcc
            yield
            if m == 0:
                z0 = v_ms(carry, 0.0, [prev["mix"]])
                z0 = v_ms(ktail, 0.0, [z0])
                hcp = v_ms(hv[:, :, 0:3], 0.0, [z0, prev["XC"]])
            else:
                hcp = v_cp(hv[:, :, 0:3], hv[:, :, MT:MT + 3], [prev["XC"]])
            pub["hz"] = hcp
            yield
            ln1 = []
            for blk in range(NBLK):
                t = layer_norm(BxP[:, blk * D:(blk + 1) * D],
                               Bv[:, blk * D:(blk + 1) * D],
                               [lxx[blk], prev["MIXFM"]] + ([ln1[-1]] if ln1 else []),
                               co=8, sq=Bg[:, 0:D])
                ln1.append(t)
                yield
            ev = hcp
            for blk in range(NBLK):
                w = v_cp(Bxc[:, 0:1024], Bv[:, blk * D:(blk + 1) * D],
                         [ln1[blk], prev["mix"]])
                w = v_cp(Bg[:, D:2 * D], Bxc[:, 0:1024], [w])
                w = v_tt(Bg[:, 0:D], Bv[:, blk * D:(blk + 1) * D], Bg[:, D:2 * D],
                         OP.subtract, waits=[w])
                w = v_cp(Bxc[:, 1024:2048], Bg[:, 0:D], [w])
                yield
                for c in range(NCH):
                    t1 = t_tr(psT[:, 0:128], Bxc[:, c * 128:(c + 1) * 128],
                              eyeb[:, :], waits=[w, psT_evac[0]])
                    t2 = t_tr(psT[:, 1024:1152], Bxc[:, 1024 + c * 128:1024 + (c + 1) * 128],
                              eyeb[:, :], waits=[w, psT_evac[1]])
                    hs = Bhfm[:, c * HMT + 3 + blk * 128:c * HMT + 3 + (blk + 1) * 128]
                    e1 = s_cp(hs, psT[:, 0:128], [t1, ev])
                    ev = v_tt(hs, hs, psT[:, 1024:1152], OP.add, waits=[t2, e1])
                    psT_evac[0] = e1
                    psT_evac[1] = ev
                    if c % 2 == 1:
                        yield
            cv = ev
            for c in range(NCH):
                cv = v_ts(hv[:, c:c + 1, 3:3 + MT], hv[:, c:c + 1, 3:3 + MT],
                          cols[:, c:c + 1], cols[:, 8 + c:9 + c], OP.mult, OP.add,
                          waits=[cv])
                if c % 4 == 3:
                    yield
            for c in range(NCH):
                cv = v_ts(cacc[:, c:c + 1, :], hv[:, c:c + 1, 0:MT],
                          cwcol[:, c * 4:c * 4 + 1], cols[:, 16 + c:17 + c],
                          OP.mult, OP.add, waits=[cv])
                if c % 4 == 3:
                    yield
            for k in range(1, 4):
                for c in range(NCH):
                    dst = xcv[:, c:c + 1, :] if k == 3 else cacc[:, c:c + 1, :]
                    cv = v_stt(dst, hv[:, c:c + 1, k:k + MT],
                               cwcol[:, c * 4 + k:c * 4 + k + 1],
                               cacc[:, c:c + 1, :], OP.mult, OP.add, waits=[cv])
                    if c % 4 == 3:
                        yield
            pub["XC"] = cv
            pub["done"] = True

        def drain(fg, n=1):
            if fg is None:
                return
            for _ in range(n):
                try:
                    next(fg)
                except StopIteration:
                    return

        def fm_transpose(w0):
            ev_ = w0
            for blk in range(NBLK):
                for c in range(NCH):
                    bk = (blk * NCH + c) % 2
                    po = bk * 1024
                    tt = t_tr(psT[:, po:po + 128],
                              Bmb[:, blk * D + c * 128:blk * D + (c + 1) * 128],
                              eyeb[:, :], waits=[w0, psT_evac[bk]])
                    ev_ = s_cp(Bfm[:, c * MT + blk * 128:c * MT + (blk + 1) * 128],
                               psT[:, po:po + 128], [tt])
                    psT_evac[bk] = ev_
            return ev_

        mts = [(b, m) for b in range(nseq) for m in range(nmt_per_seq)]
        ft = {}
        fg = front_gen(0, 0, {"mix": PRO_DONE[0], "XC": PRO_DONE[0],
                              "MIXFM": PRO_DONE[0]}, ft)
        drain(fg, 1000)

        for mi_, (b, m) in enumerate(mts):
            t0 = m * MT
            gb0 = t0 // 128
            par = ft["xpar"]
            BxP = (Bx, Bx2)[par]
            XC = ft["XC"]
            lcos = ft["lcos"]

            pair_cycle = [(ps1, ps2), (ps3, ps4), (ps5, ps6)]
            pair_evac = {0: fin_tok[0], 1: fin_tok[0], 2: fin_tok[0]}
            last_ev = fl_tok[0]
            for ti, (nm, dstb) in enumerate((("qk", Bk), ("v", Bv), ("g", Bg))):
                mend = {}
                for c in range(NCH):
                    wpar = c % 2
                    lw = DMA("gpsimd", Wq[:, wpar * D:(wpar + 1) * D], sc[nm][c, :, :],
                             slot_hist["q"][wpar], sem="Q%d" % wpar)
                    for blk in range(NBLK):
                        pr = (ti * NBLK + blk) % 3
                        pa, pb = pair_cycle[pr]
                        w_ = [lw, XC, pair_evac[pr], fin_tok[0]] if c == 0 else [lw]
                        last = (c == 7) and nm != "g"
                        t_mm(pa[:, 0:512],
                             Bxc[:, c * MT + blk * 128:c * MT + (blk + 1) * 128],
                             Wq[:, wpar * D:wpar * D + 512], c == 0, last, waits=w_)
                        mend[blk] = t_mm(pb[:, 0:512],
                                         Bxc[:, c * MT + blk * 128:c * MT + (blk + 1) * 128],
                                         Wq[:, wpar * D + 512:(wpar + 1) * D],
                                         c == 0, last)
                    slot_hist["q"][wpar] = [mend[NBLK - 1]]
                if nm == "g":
                    for blk in range(NBLK):
                        pr = (ti * NBLK + blk) % 3
                        pa, pb = pair_cycle[pr]
                        t_mm(pa[:, 0:512], ones2b[:, :], gbhl[:, 0:512], False, True)
                        mend[blk] = t_mm(pb[:, 0:512], ones2b[:, :],
                                         gbhl[:, 512:1024], False, True)
                for blk in range(NBLK):
                    pr = (ti * NBLK + blk) % 3
                    pa, pb = pair_cycle[pr]
                    if nm == "g":
                        e1 = s_act(dstb[:, blk * D:blk * D + 512], pa[:, 0:512],
                                   AF.Sigmoid, waits=[mend[blk], last_ev])
                        e2 = s_act(dstb[:, blk * D + 512:(blk + 1) * D], pb[:, 0:512],
                                   AF.Sigmoid, waits=[e1])
                    else:
                        e1 = s_cp(dstb[:, blk * D:blk * D + 512], pa[:, 0:512],
                                  [mend[blk], last_ev])
                        e2 = s_cp(dstb[:, blk * D + 512:(blk + 1) * D], pb[:, 0:512],
                                  [e1])
                    pair_evac[pr] = e2
                    last_ev = e2
            QKVG = last_ev

            kk = QKVG
            for blk in range(NBLK):
                kk = v_tt(Bsq[:, 0:D], Bk[:, blk * D:(blk + 1) * D],
                          Bk[:, blk * D:(blk + 1) * D], OP.mult, waits=[kk])
                kk = v_red(kn[:, 32:48],
                           Bsq[:, 0:D].rearrange("p (h w) -> p h w", w=HD),
                           waits=[kk])
                kk = v_ts(kn[:, 32:48], kn[:, 32:48], -1e-24, None, OP.add, waits=[kk])
                kk = s_act(kn[:, 32:48], kn[:, 32:48], AF.Relu, waits=[kk])
                kk = v_ts(kn[:, 32:48], kn[:, 32:48], 1e-24, None, OP.add, waits=[kk])
                kk = rsqrt_pos(kn[:, 0:16], kn[:, 32:48], kn[:, 16:32], [kk])
                for h in range(H):
                    kk = v_ts(Bk[:, blk * D + h * HD:blk * D + (h + 1) * HD],
                              Bk[:, blk * D + h * HD:blk * D + (h + 1) * HD],
                              kn[:, h:h + 1], None, OP.mult, waits=[kk])
            mix = kk
            cu_prev = kk

            for blk in range(NBLK):
                gbi = gb0 + blk
                m1 = t_mm(ps5[:, 0:512], shf_s[:, :], Bk[:, blk * D:blk * D + 512],
                          True, False, waits=[mix])
                m2 = t_mm(ps6[:, 0:512], shf_s[:, :],
                          Bk[:, blk * D + 512:(blk + 1) * D], True, False)
                if blk == 0:
                    m1 = t_mm(ps5[:, 0:512], e0r, ktail[0:1, 0:512], False, True)
                    m2 = t_mm(ps6[:, 0:512], e0r, ktail[0:1, 512:1024], False, True)
                else:
                    m1 = t_mm(ps5[:, 0:512], sel_last_s[:, :], Bk[:, 0:512],
                              False, True)
                    m2 = t_mm(ps6[:, 0:512], sel_last_s[:, :], Bk[:, 512:1024],
                              False, True)
                s1 = v_tt(Bt1[:, blk * D:blk * D + 512],
                          Bcos[:, blk * D:blk * D + 512], ps5[:, 0:512], OP.add,
                          waits=[m1, lcos[blk]])
                s1 = v_tt(Bt1[:, blk * D + 512:(blk + 1) * D],
                          Bcos[:, blk * D + 512:(blk + 1) * D], ps6[:, 0:512],
                          OP.add, waits=[m2, s1])
                s2 = v_tt(Bg[:, blk * D:(blk + 1) * D], Bv[:, blk * D:(blk + 1) * D],
                          Bg[:, blk * D:(blk + 1) * D], OP.mult, waits=[s1])
                s3 = v_tt(Bt1[:, blk * D:(blk + 1) * D], Bt1[:, blk * D:(blk + 1) * D],
                          Bg[:, blk * D:(blk + 1) * D], OP.mult, waits=[s2])
                for h in range(H):
                    s3 = v_ts(Bt1[:, blk * D + h * HD:blk * D + (h + 1) * HD],
                              Bt1[:, blk * D + h * HD:blk * D + (h + 1) * HD],
                              dec[:, gbi * H + h:gbi * H + h + 1], None, OP.mult,
                              waits=[s3])
                c1 = t_mm(ps1[:, 0:512], tri_s[:, :], Bt1[:, blk * D:blk * D + 512],
                          True, False, waits=[s3])
                c1 = t_mm(ps1[:, 0:512], ones1, carry[0:1, 0:512], False, True)
                c2 = t_mm(ps2[:, 0:512], tri_s[:, :],
                          Bt1[:, blk * D + 512:(blk + 1) * D], True, False)
                c2 = t_mm(ps2[:, 0:512], ones1, carry[0:1, 512:1024], False, True)
                b1 = t_mm(ps3[0:1, 0:512], ones1c, Bt1[:, blk * D:blk * D + 512],
                          True, True, waits=[s3, cu_prev])
                b2 = t_mm(ps4[0:1, 0:512], ones1c,
                          Bt1[:, blk * D + 512:(blk + 1) * D], True, True)
                cu = v_tt(carry[0:1, 0:512], carry[0:1, 0:512], ps3[0:1, 0:512],
                          OP.add, waits=[c1, b1])
                cu = v_tt(carry[0:1, 512:1024], carry[0:1, 512:1024],
                          ps4[0:1, 0:512], OP.add, waits=[c2, b2, cu])
                cu_prev = cu
                mv = cu
                for h in range(H):
                    pp = ps1 if h < 8 else ps2
                    off = (h % 8) * HD
                    mv = v_ts(Bv[:, blk * D + h * HD:blk * D + (h + 1) * HD],
                              pp[:, off:off + HD],
                              rdec[:, gbi * H + h:gbi * H + h + 1], None, OP.mult,
                              waits=[c1, c2, mv])
                mix = mv
            kt1 = t_mm(ps3[0:1, 0:512], selc_s[:, 0:1], Bk[:, D:D + 512],
                       True, True, waits=[mix])
            kt2 = t_mm(ps4[0:1, 0:512], selc_s[:, 0:1], Bk[:, D + 512:2 * D],
                       True, True)
            ktv = s_cp(ktail[0:1, 0:512], ps3[0:1, 0:512], [kt1])
            ktv = s_cp(ktail[0:1, 512:1024], ps4[0:1, 0:512], [ktv, kt2])
            mix = v_tt(Bv[:, :], Bv[:, :], Bk[:, :], OP.mult, waits=[mix, ktv])
            mix = v_cp(Bmb[:, :], Bv[:, :], [mix])
            MIXFM = fm_transpose(mix)

            mend = None
            for c in range(NCH):
                wpar = c % 2
                lw = DMA("gpsimd", Wq[:, wpar * D:(wpar + 1) * D], sc["o"][c, :, :],
                         slot_hist["q"][wpar] + [MIXFM], sem="Q%d" % wpar)
                for blk in range(NBLK):
                    pa, pb = (psA, psB)[blk]
                    w_ = [lw, MIXFM] if c == 0 else [lw]
                    t_mm(pa[:, 0:512],
                         Bfm[:, c * MT + blk * 128:c * MT + (blk + 1) * 128],
                         Wq[:, wpar * D:wpar * D + 512], c == 0, c == 7, waits=w_)
                    mend = t_mm(pb[:, 0:512],
                                Bfm[:, c * MT + blk * 128:c * MT + (blk + 1) * 128],
                                Wq[:, wpar * D + 512:(wpar + 1) * D], c == 0, c == 7)
                slot_hist["q"][wpar] = [mend]
            ev = mend
            for blk in range(NBLK):
                pa, pb = (psA, psB)[blk]
                ev = s_cp(Bt2[:, blk * D:blk * D + 512], pa[:, 0:512],
                          [ev, fl_tok[0]])
                ev = s_cp(Bt2[:, blk * D + 512:(blk + 1) * D], pb[:, 0:512], [ev])
            on = [ev]
            for blk in range(NBLK):
                on = [layer_norm(Bt2[:, blk * D:(blk + 1) * D],
                                 Bt2[:, blk * D:(blk + 1) * D], on)]
            x2 = on[0]
            for blk in range(NBLK):
                x2 = v_tt(Bt3[:, blk * D:(blk + 1) * D], Bt2[:, blk * D:(blk + 1) * D],
                          sob[:, :], OP.mult, waits=[x2, store_tok[0]])
                x2 = v_tt(Bt3[:, blk * D:(blk + 1) * D], Bt3[:, blk * D:(blk + 1) * D],
                          bob[:, :], OP.add, waits=[x2])
                x2 = v_tt(Bk[:, blk * D:(blk + 1) * D], Bt3[:, blk * D:(blk + 1) * D],
                          BxP[:, blk * D:(blk + 1) * D], OP.add, waits=[x2])

            l2t = [x2]
            for blk in range(NBLK):
                l2t = [layer_norm(Bk[:, blk * D:(blk + 1) * D],
                                  Bt2[:, blk * D:(blk + 1) * D], l2t)]
            l2c = v_cp(Bmb[:, :], Bt2[:, :], l2t)
            H2FM = fm_transpose(l2c)

            nxt = mts[mi_ + 1] if mi_ + 1 < len(mts) else None
            ft_next = {}
            fg = None
            if nxt is not None:
                fg = front_gen(nxt[0], nxt[1],
                               {"mix": mix, "XC": XC, "MIXFM": MIXFM}, ft_next)

            gevs = []
            for mi in range(32):
                wpar = mi % 4
                lg = DMA("gpsimd", Wgu[:, wpar * 2048:wpar * 2048 + 1024],
                         sc["gate"][mi, :, :], slot_hist["gu"][wpar], sem="G%d" % wpar)
                lu = DMA("gpsimd", Wgu[:, wpar * 2048 + 1024:(wpar + 1) * 2048],
                         sc["up"][mi, :, :], slot_hist["gu"][wpar], sem="G%d" % wpar)
                pg = (ps5, ps3)[mi % 2]
                pu = (ps6, ps4)[mi % 2]
                mg = mu = None
                for c in range(NCH):
                    w_ = [lg, lu, H2FM]
                    if c == 0 and mi >= 2:
                        w_.append(gevs[mi - 2])
                    mg = t_mm(pg[:, 0:MT],
                              Wgu[:, wpar * 2048 + c * 128:wpar * 2048 + (c + 1) * 128],
                              Bfm[:, c * MT:(c + 1) * MT], c == 0, c == 7,
                              waits=w_ if c == 0 else [lg])
                    mu = t_mm(pu[:, 0:MT],
                              Wgu[:, wpar * 2048 + 1024 + c * 128:wpar * 2048 + 1024 + (c + 1) * 128],
                              Bfm[:, c * MT:(c + 1) * MT], c == 0, c == 7)
                slot_hist["gu"][wpar] = [mu]
                pp2 = mi % 2
                sl = s_act(Bsl[:, pp2 * MT:(pp2 + 1) * MT], pg[:, 0:MT], AF.Silu,
                           bias=bgcol[:, mi:mi + 1], waits=[mu])
                gv = v_stt(Bgu[:, mi * MT:(mi + 1) * MT], pu[:, 0:MT],
                           bucol[:, mi:mi + 1], Bsl[:, pp2 * MT:(pp2 + 1) * MT],
                           OP.add, OP.mult, waits=[sl])
                gevs.append(gv)
                drain(fg)
            GU = gevs[-1]

            dm = None
            for f in range(32):
                wpar = f % 4
                lw = DMA("gpsimd", Wdn[:, wpar * 1024:(wpar + 1) * 1024],
                         sc["down"][f, :, :], slot_hist["dn"][wpar], sem="D%d" % wpar)
                for blk in range(NBLK):
                    pa, pb = (psA, psB)[blk]
                    w_ = [lw, GU] if f == 0 else [lw]
                    t_mm(pa[:, 0:512],
                         Bgu[:, f * MT + blk * 128:f * MT + (blk + 1) * 128],
                         Wdn[:, wpar * 1024:wpar * 1024 + 512], f == 0, f == 31,
                         waits=w_)
                    dm = t_mm(pb[:, 0:512],
                              Bgu[:, f * MT + blk * 128:f * MT + (blk + 1) * 128],
                              Wdn[:, wpar * 1024 + 512:(wpar + 1) * 1024],
                              f == 0, f == 31)
                slot_hist["dn"][wpar] = [dm]
                drain(fg)
            yv = dm
            for blk in range(NBLK):
                pa, pb = (psA, psB)[blk]
                yv = v_tt(Bt3[:, blk * D:blk * D + 512], pa[:, 0:512],
                          Bk[:, blk * D:blk * D + 512], OP.add, waits=[yv])
                yv = v_tt(Bt3[:, blk * D + 512:(blk + 1) * D], pb[:, 0:512],
                          Bk[:, blk * D + 512:(blk + 1) * D], OP.add, waits=[yv])
                yv = v_tt(Bt3[:, blk * D:(blk + 1) * D], Bt3[:, blk * D:(blk + 1) * D],
                          BxP[:, blk * D:(blk + 1) * D], OP.add, waits=[yv])
            xread_tok[par] = yv

            ol = [yv]
            for blk in range(NBLK):
                ol = [layer_norm(Bt3[:, blk * D:(blk + 1) * D],
                                 Bt1[:, blk * D:(blk + 1) * D], ol)]
            ol2 = v_cp(Bmb[:, :], Bt1[:, :], ol)
            XFFM = fm_transpose(ol2)

            cp_last = XFFM
            ge = XFFM
            zms = [XFFM, XFFM]
            for f in range(128):
                wpar = f % 4
                lw = DMA("sync", Wcp[:, wpar * 1024:(wpar + 1) * 1024],
                         sc["wc"][f, :, :], slot_hist["cp"][wpar], sem="C%d" % wpar)
                if f % 4 == 0:
                    gslot = (f // 4) % 2
                    lct = DMA("gpsimd",
                              Wct[:, gslot * 512:(gslot + 1) * 512].rearrange(
                                  "p (t w) -> p t w", w=128),
                              sc["ct"][f:f + 4, :, :].rearrange("t p w -> p t w"),
                              [zms[gslot]], sem="T%d" % gslot)
                pz = psZ[f % 2]
                cq = (f // 4) % 8
                zm = t_mm(pz[:, 0:MT],
                          Wct[:, gslot * 512 + (f % 4) * 128:gslot * 512 + (f % 4 + 1) * 128],
                          Bfm[:, cq * MT:(cq + 1) * MT], True, True,
                          waits=[XFFM, cp_last, lct])
                if f % 4 == 3:
                    zms[gslot] = zm
                pp2 = f % 2
                ge = s_act(Bha[:, pp2 * MT:(pp2 + 1) * MT], pz[:, 0:MT], AF.Gelu,
                           waits=[zm])
                for blk in range(NBLK):
                    pa, pb = (psA, psB)[blk]
                    t_mm(pa[:, 0:512],
                         Bha[:, pp2 * MT + blk * 128:pp2 * MT + (blk + 1) * 128],
                         Wcp[:, wpar * 1024:wpar * 1024 + 512], f == 0, f == 127,
                         waits=[ge, lw])
                    cp_last = t_mm(pb[:, 0:512],
                                   Bha[:, pp2 * MT + blk * 128:pp2 * MT + (blk + 1) * 128],
                                   Wcp[:, wpar * 1024 + 512:(wpar + 1) * 1024],
                                   f == 0, f == 127)
                slot_hist["cp"][wpar] = [cp_last]
                if f % 2 == 0:
                    drain(fg)
            drain(fg, 1000)

            fin = cp_last
            for blk in range(NBLK):
                pa, pb = (psA, psB)[blk]
                fin = v_tt(Bt2[:, blk * D:blk * D + 512], pa[:, 0:512],
                           Bt1[:, blk * D:blk * D + 512], OP.add, waits=[fin])
                fin = v_tt(Bt2[:, blk * D + 512:(blk + 1) * D], pb[:, 0:512],
                           Bt1[:, blk * D + 512:(blk + 1) * D], OP.add, waits=[fin])
            fl = [fin]
            for blk in range(NBLK):
                fl = [layer_norm(Bt2[:, blk * D:(blk + 1) * D],
                                 Bo[:, blk * D:(blk + 1) * D], fl)]
            for blk in range(NBLK):
                st = DMA("sync", out_d[b, t0 + blk * 128:t0 + (blk + 1) * 128, :],
                         Bo[:, blk * D:(blk + 1) * D], fl, sem="ST")
            store_tok[0] = st
            fl_tok[0] = fl[0]
            fin_tok[0] = fin
            cosread_tok[0] = mix
            ft = ft_next

        E("sync", lambda e: e.nop(), [P.tok("ST")])

        @block.sync
        def _(e):
            P.replay("sync", e, sems)

        @block.gpsimd
        def _(e):
            P.replay("gpsimd", e, sems)

        @block.vector
        def _(e):
            P.replay("vector", e, sems)

        @block.scalar
        def _(e):
            P.replay("scalar", e, sems)

        @block.tensor
        def _(e):
            P.replay("tensor", e, sems)

    return nc


def _replay(self, eng, e, sems):
    for fn, ws, inc, inc_n in self.stream[eng]:
        for (s, v) in ws:
            e.wait_ge(sems[s], v)
        ins = fn(e)
        if inc is not None:
            if ins is not None:
                ins.then_inc(sems[inc], inc_n)
            else:
                e.sem_inc(sems[inc], inc_n)

Prog.replay = _replay


def kernel(**inputs):
    x = np.asarray(inputs["L0_attn_x"], dtype=np.float32)
    prep = host_prep(inputs)
    nc = build_nc()
    in_maps = []
    for c in range(NCORES):
        mm = dict(prep)
        mm["x"] = np.ascontiguousarray(x[c * BPC:(c + 1) * BPC])
        in_maps.append(mm)
    res = run_bass_kernel_spmd(nc, in_maps, core_ids=list(range(NCORES)))
    out = np.concatenate([np.asarray(r["out"]) for r in res.results], axis=0)
    return out.astype(np.float32)


# version 05
# speedup vs baseline: 1.8558x; 1.4302x over previous
import math
import numpy as np
import concourse.bass as bass
import concourse.mybir as mybir
from concourse.bass_utils import run_bass_kernel_spmd

F32 = mybir.dt.float32
BF16 = mybir.dt.bfloat16
AF = mybir.ActivationFunctionType
OP = mybir.AluOpType
AX = mybir.AxisListType

D, H, HD, FF, NF = 1024, 16, 64, 4096, 16384
TSEQ, NCORES, BPC = 2048, 8, 2
MT = 256
NBLK = MT // 128
NMT = TSEQ // MT
NCH = D // 128
HMT = 3 + MT
EPS = 1e-5
R1024 = 1.0 / 1024.0


def _split8(v, n):
    out = []
    for _ in range(n):
        c = (np.array(v, np.float32).view(np.uint32) & np.uint32(0xFFFF0000)).view(np.float32)
        out.append(float(c))
        v = v - float(c)
    return out

CW2PI = _split8(2 * math.pi, 6)
_l10 = math.log(10.0)
L10H = float(np.float32(_l10)); L10L = float(np.float32(_l10 - L10H))
_l2 = math.log(2.0)
LN2H = float(np.float32(_l2)); LN2L = float(np.float32(_l2 - LN2H))
PIO2 = float(np.float32(math.pi / 2))


def host_constants():
    c = {}
    t = np.arange(TSEQ, dtype=np.float64)
    idx64 = np.arange(HD, dtype=np.float64) / 63.0
    freqs64 = (np.arange(H, dtype=np.float64)[:, None] * (10.0 ** idx64)[None, :]).reshape(-1)
    ang64 = t[:, None] * freqs64[None, :]
    c["ntab"] = np.asarray(np.rint(ang64 / (2 * np.pi)), dtype=np.float32)
    kd = np.rint(idx64 * np.log2(10.0))
    c["ktab"] = np.ascontiguousarray(np.broadcast_to(kd.astype(np.float32), (128, HD)))
    c["twok"] = np.ascontiguousarray(np.broadcast_to((2.0 ** kd).astype(np.float32), (128, HD)))
    c["tri"] = np.triu(np.ones((128, 128), dtype=np.float32))
    c["shift"] = np.eye(128, k=1, dtype=np.float32)
    c["eye"] = np.eye(128, dtype=np.float32)
    selc = np.zeros((128, 2), dtype=np.float32)
    selc[127, 0] = 1.0
    selc[:, 1] = 1.0
    c["selc"] = selc
    sl = np.zeros((128, 128), dtype=np.float32)
    sl[127, 0] = 1.0
    c["sel_last"] = sl
    return c


def _col8(v):
    return np.ascontiguousarray(np.asarray(v, np.float32).reshape(NCH, 128).T)


def host_prep(inp):
    g = lambda k: np.asarray(inp[k], dtype=np.float32)
    d = {}
    d["qk_wT"] = np.ascontiguousarray(g("L0_attn_qk_w").T)
    d["v_wT"] = np.ascontiguousarray(g("L0_attn_v_w").T)
    d["g_wT"] = np.ascontiguousarray(g("L0_attn_g_w").T)
    d["o_wT"] = np.ascontiguousarray(g("L0_attn_o_w").T)
    for nm, key in (("gate_t4", "L0_attn_gate_w"), ("up_t4", "L0_attn_up_w")):
        wT = g(key).T
        d[nm] = np.ascontiguousarray(wT.reshape(NCH, 128, 32, 128).transpose(2, 0, 1, 3))
    d["down_wT"] = np.ascontiguousarray(g("L0_attn_down_w").T)
    wcT = np.ascontiguousarray(g("L0_ffn_w_cproj").T)
    hh, cc, gg = np.meshgrid(np.arange(4), np.arange(8), np.arange(4), indexing="ij")
    base = ((hh * 4096)[..., None, None]
            + ((cc * 8)[..., None, None] + np.arange(8)[None, None, None, :, None]) * 64
            + (gg * 16)[..., None, None]
            + np.arange(16)[None, None, None, None, :])
    d["wc_Tp"] = np.ascontiguousarray(wcT[base.reshape(-1)])
    mk6 = g("L0_ffn_rail_maker").reshape(4, 8, 8, 16, 16)
    up6 = g("L0_ffn_rail_up_proj").reshape(4, 8, 8, 4, 16, 16)
    mbd = np.zeros((4, 8, 128, 128), dtype=np.float32)
    ubd = np.zeros((4, 8, 4, 128, 128), dtype=np.float32)
    for rho in range(8):
        s = slice(rho * 16, rho * 16 + 16)
        mbd[:, :, s, s] = mk6[:, :, rho].transpose(0, 1, 3, 2)
        ubd[:, :, :, s, s] = up6[:, :, rho].transpose(0, 1, 2, 4, 3)
    d["mk_bd"] = mbd
    d["up_bd"] = np.ascontiguousarray(ubd)
    d["convw_col"] = np.ascontiguousarray(
        g("L0_attn_conv_w")[:, 0, :].reshape(NCH, 128, 4).transpose(1, 0, 2).reshape(128, 32))
    for nm, key in (("ln1w", "L0_attn_ln1_w"), ("ln1b", "L0_attn_ln1_b"),
                    ("convb", "L0_attn_conv_b"), ("ln2w", "L0_attn_ln2_w"),
                    ("ln2b", "L0_attn_ln2_b")):
        d[nm + "_col"] = _col8(g(key))
    for nm, key in (("gb", "L0_attn_g_b"), ("onw", "L0_attn_onorm_w"),
                    ("onb", "L0_attn_onorm_b"), ("gam1", "L0_attn_gamma1"),
                    ("gam2", "L0_attn_gamma2")):
        d[nm + "_row"] = np.ascontiguousarray(g(key).reshape(1, D))
    d["gamma"] = np.ascontiguousarray(g("L0_attn_gamma_attn").reshape(H, 1))
    d.update(host_constants())
    return d



class Prog:
    ENG = ("sync", "gpsimd", "vector", "scalar", "tensor")

    def __init__(self):
        self.stream = {e: [] for e in self.ENG}
        self.count = {}
        self.hw = {e: {} for e in self.ENG}

    def emit(self, eng, fn, waits=(), inc=None, inc_n=1):
        ws = []
        for w in waits:
            if w is None:
                continue
            s, v = w
            if v > 0 and self.hw[eng].get(s, 0) < v:
                self.hw[eng][s] = v
                ws.append((s, v))
        if inc is not None:
            self.count[inc] = self.count.get(inc, 0) + inc_n
        self.stream[eng].append((fn, tuple(ws), inc, inc_n))
        return (inc, self.count[inc]) if inc is not None else None

    def tok(self, sem):
        return (sem, self.count.get(sem, 0))


def build_nc(nmt_per_seq=NMT, nseq=BPC):
    nc = bass.Bass("TRN2", target_bir_lowering=False)
    P = Prog()

    dr = {}
    def din(name, shape, dt=F32):
        dr[name] = nc.dram_tensor(name, list(shape), dt, kind="ExternalInput").ap()

    din("x", (BPC, TSEQ, D))
    for nm in ("qk_wT", "v_wT", "g_wT", "o_wT"):
        din(nm, (D, D))
    din("gate_t4", (32, NCH, 128, 128)); din("up_t4", (32, NCH, 128, 128))
    din("down_wT", (FF, D)); din("wc_Tp", (NF, D))
    din("mk_bd", (4, 8, 128, 128)); din("up_bd", (4, 8, 4, 128, 128))
    din("convw_col", (128, 32))
    for nm in ("ln1w_col", "ln1b_col", "convb_col", "ln2w_col", "ln2b_col"):
        din(nm, (128, NCH))
    for nm in ("gb_row", "onw_row", "onb_row", "gam1_row", "gam2_row"):
        din(nm, (1, D))
    din("gamma", (H, 1)); din("ntab", (TSEQ, D))
    din("ktab", (128, HD)); din("twok", (128, HD))
    din("tri", (128, 128)); din("shift", (128, 128)); din("eye", (128, 128))
    din("selc", (128, 2)); din("sel_last", (128, 128))
    out_d = nc.dram_tensor("out", [BPC, TSEQ, D], F32, kind="ExternalOutput").ap()

    sc = {
        "qk": nc.dram_tensor("s_qk", [NCH, 128, D], BF16).ap(),
        "v": nc.dram_tensor("s_v", [NCH, 128, D], BF16).ap(),
        "g": nc.dram_tensor("s_g", [NCH, 128, D], BF16).ap(),
        "o": nc.dram_tensor("s_o", [NCH, 128, D], BF16).ap(),
        "gate": nc.dram_tensor("s_gate", [32, 128, D], BF16).ap(),
        "up": nc.dram_tensor("s_up", [32, 128, D], BF16).ap(),
        "down": nc.dram_tensor("s_down", [32, 128, D], BF16).ap(),
        "wc": nc.dram_tensor("s_wc", [128, 128, D], BF16).ap(),
        "ct": nc.dram_tensor("s_ct", [128, 128, 128], BF16).ap(),
        "cos": nc.dram_tensor("s_cos", [TSEQ, D], F32).ap(),
    }

    Wct = nc.alloc_sbuf_tensor("Wct", [128, 2 * 4 * 128], BF16)
    Bx2 = nc.alloc_sbuf_tensor("Bx2", [128, 2048], F32)
    dec = nc.alloc_sbuf_tensor("dec", [128, 16 * H], F32)
    rdec = nc.alloc_sbuf_tensor("rdec", [128, 16 * H], F32)
    sob = nc.alloc_sbuf_tensor("sob", [128, D], F32)
    bob = nc.alloc_sbuf_tensor("bob", [128, D], F32)
    g2b = nc.alloc_sbuf_tensor("g2b", [128, D], F32)
    cols = nc.alloc_sbuf_tensor("cols", [128, 48], F32)
    cwcol = nc.alloc_sbuf_tensor("cwcol", [128, 32], F32)
    bgcol = nc.alloc_sbuf_tensor("bgcol", [128, 32], F32)
    bucol = nc.alloc_sbuf_tensor("bucol", [128, 32], F32)
    mmrow = nc.alloc_sbuf_tensor("mmrow", [1, 2304], F32)
    gbhl = nc.alloc_sbuf_tensor("gbhl", [2, D], BF16)
    hl = nc.alloc_sbuf_tensor("hl", [2, D], BF16)
    selc_s = nc.alloc_sbuf_tensor("selc_s", [128, 2], F32)
    sel_last_s = nc.alloc_sbuf_tensor("sel_last_s", [128, 128], F32)
    tri_s = nc.alloc_sbuf_tensor("tri_s", [128, 128], F32)
    shf_s = nc.alloc_sbuf_tensor("shf_s", [128, 128], F32)
    eye_s = nc.alloc_sbuf_tensor("eye_s", [128, 128], F32)
    eyeb = nc.alloc_sbuf_tensor("eyeb", [128, 128], BF16)
    ones2b = nc.alloc_sbuf_tensor("ones2b", [2, 128], BF16)
    gamsb = nc.alloc_sbuf_tensor("gamsb", [H, 1], F32)
    pcol = nc.alloc_sbuf_tensor("pcol", [H, 1], F32)
    ccol = nc.alloc_sbuf_tensor("ccol", [128, 16], F32)
    stat = nc.alloc_sbuf_tensor("stat", [128, 16], F32)
    kn = nc.alloc_sbuf_tensor("kn", [128, 48], F32)
    Bx = nc.alloc_sbuf_tensor("Bx", [128, NBLK * D], F32)
    Bt1 = nc.alloc_sbuf_tensor("Bt1", [128, NBLK * D], F32)
    Bt2 = nc.alloc_sbuf_tensor("Bt2", [128, NBLK * D], F32)
    Bt3 = nc.alloc_sbuf_tensor("Bt3", [128, NBLK * D], F32)
    Bsq = nc.alloc_sbuf_tensor("Bsq", [128, D], F32)
    Bk = nc.alloc_sbuf_tensor("Bk", [128, NBLK * D], F32)
    Bv = nc.alloc_sbuf_tensor("Bv", [128, NBLK * D], F32)
    Bg = nc.alloc_sbuf_tensor("Bg", [128, NBLK * D], F32)
    Bcos = nc.alloc_sbuf_tensor("Bcos", [128, NBLK * D], F32)
    Bhfm = nc.alloc_sbuf_tensor("Bhfm", [128, NCH * HMT], F32)
    Bxc = nc.alloc_sbuf_tensor("Bxc", [128, NCH * MT], BF16)
    Bmb = nc.alloc_sbuf_tensor("Bmb", [128, NBLK * D], BF16)
    Bfm = nc.alloc_sbuf_tensor("Bfm", [128, NCH * MT], BF16)
    Bgu = nc.alloc_sbuf_tensor("Bgu", [128, 32 * MT], BF16)
    Bha = nc.alloc_sbuf_tensor("Bha", [128, 2 * MT], BF16)
    Bsl = nc.alloc_sbuf_tensor("Bsl", [128, 2 * MT], F32)
    Wq = nc.alloc_sbuf_tensor("Wq", [128, 2 * D], BF16)
    Wgu = nc.alloc_sbuf_tensor("Wgu", [128, 4 * 2048], BF16)
    Wdn = nc.alloc_sbuf_tensor("Wdn", [128, 4 * 1024], BF16)
    Wcp = nc.alloc_sbuf_tensor("Wcp", [128, 4 * 1024], BF16)
    ps1 = nc.alloc_psum_tensor("ps1", [128, 512], F32)
    ps2 = nc.alloc_psum_tensor("ps2", [128, 512], F32)
    ps3 = nc.alloc_psum_tensor("ps3", [128, 512], F32)
    ps4 = nc.alloc_psum_tensor("ps4", [128, 512], F32)
    ps5 = nc.alloc_psum_tensor("ps5", [128, 512], F32)
    ps6 = nc.alloc_psum_tensor("ps6", [128, 512], F32)
    psT = nc.alloc_psum_tensor("psT", [128, 2048], BF16)
    SEM_NAMES = (["LD", "WT", "TN", "VC", "SC", "GP", "ST"]
                 + ["Q%d" % i for i in range(2)]
                 + ["G%d" % i for i in range(4)]
                 + ["D%d" % i for i in range(4)]
                 + ["C%d" % i for i in range(4)]
                 + ["X%d" % i for i in range(2)]
                 + ["O%d" % i for i in range(2)]
                 + ["V%d" % i for i in range(4)]
                 + ["U%d" % i for i in range(4)]
                 + ["T%d" % i for i in range(2)])
    sem_h = {n: nc.alloc_semaphore(n) for n in SEM_NAMES}
    with nc.Block() as block:
        sems = sem_h
        psA = (ps1, ps2)
        psB = (ps3, ps4)
        psZ = (ps5, ps6)

        carry = mmrow[0:1, 0:1024]
        ktail = mmrow[0:1, 1024:2048]
        ones1 = mmrow[0:1, 2048:2176]
        e0r = mmrow[0:1, 2176:2304]
        ROW_GB = Bsq[0:1, 0:1024]
        ROW_ONW = Bcos[0:1, 0:1024]
        ROW_ONB = Bcos[0:1, 1024:2048]
        ROW_G2 = Bhfm[0:1, 0:1024]
        rowtmp = Bhfm[0:1, 1024:2048]
        HI32 = Bk[0:1, 0:1024]
        LO32 = Bk[0:1, 1024:2048]
        HL16 = Bxc[0:1, 0:2048]
        ones1c = selc_s[:, 1:2]

        def betaG_ap(mi, a, bnd):
            t_ = Bv if mi < 16 else Bg
            mo = mi % 16
            return t_[0:1, mo * 128 + a:mo * 128 + bnd]

        def betaU_ap(mi, a, bnd):
            t_ = Bk if mi < 16 else Bt3
            mo = mi % 16
            return t_[0:1, mo * 128 + a:mo * 128 + bnd]
        hl32 = Bt2
        dech = Bx
        Wc32 = Bt1
        Wc16 = Bmb
        Bnt = Bv
        Bfrq = Bg
        Bex = Bk[:, 0:HD]
        Bex2 = Bk[:, HD:2 * HD]
        Bex3 = Bk[:, 2 * HD:3 * HD]
        Bo = Bt3

        def E(eng, fn, waits=(), inc=None, inc_n=1):
            return P.emit(eng, fn, waits, inc, inc_n)

        def DMA(eng, dst, src, waits=(), sem="LD"):
            def fn(e, d_=dst, s_=src):
                return e.dma_start(out=d_, in_=s_)
            return E(eng, fn, waits, inc=sem, inc_n=16)

        def v_tt(out, a, b, op, waits=(), inc="VC"):
            return E("vector", lambda e, o=out, x=a, y=b, p=op:
                     e.tensor_tensor(out=o, in0=x, in1=y, op=p), waits, inc)

        def v_ts(out, a, s1, s2, op0, op1=None, waits=(), inc="VC"):
            if op1 is None:
                return E("vector", lambda e, o=out, x=a, u=s1, p=op0:
                         e.tensor_scalar(out=o, in0=x, scalar1=u, scalar2=None, op0=p),
                         waits, inc)
            return E("vector", lambda e, o=out, x=a, u=s1, w=s2, p=op0, q=op1:
                     e.tensor_scalar(out=o, in0=x, scalar1=u, scalar2=w, op0=p, op1=q),
                     waits, inc)

        def v_stt(out, a, s1, b, op0, op1, waits=(), inc="VC"):
            return E("vector", lambda e, o=out, x=a, u=s1, y=b, p=op0, q=op1:
                     e.scalar_tensor_tensor(out=o, in0=x, scalar=u, in1=y, op0=p, op1=q),
                     waits, inc)

        def v_cp(out, a, waits=(), inc="VC"):
            return E("vector", lambda e, o=out, x=a: e.tensor_copy(out=o, in_=x), waits, inc)

        def v_ms(out, c, waits=(), inc="VC"):
            return E("vector", lambda e, o=out, k=c: e.memset(o, k), waits, inc)

        def g_tt(out, a, b, op, waits=(), inc="GP"):
            return E("gpsimd", lambda e, o=out, x=a, y=b, p=op:
                     e.tensor_tensor(out=o, in0=x, in1=y, op=p), waits, inc)

        def g_ts(out, a, s1, op0, waits=(), inc="GP"):
            return E("gpsimd", lambda e, o=out, x=a, u=s1, p=op0:
                     e.tensor_scalar(out=o, in0=x, scalar1=u, scalar2=None, op0=p),
                     waits, inc)

        def v_sq_accum(out, a, acc, waits=(), inc="VC"):
            return E("vector", lambda e, o=out, x=a, q=acc:
                     e.scalar_tensor_tensor(out=o, in0=x, scalar=1.0, in1=x,
                                            op0=OP.mult, op1=OP.mult, accum_out=q),
                     waits, inc)

        def v_red(out, a, waits=(), inc="VC"):
            return E("vector", lambda e, o=out, x=a:
                     e.tensor_reduce(out=o, in_=x, axis=AX.X, op=OP.add), waits, inc)

        def s_act(out, a, func, bias=0.0, scale=1.0, waits=(), inc="SC"):
            return E("scalar", lambda e, o=out, x=a, f=func, b=bias, s=scale:
                     e.activation(out=o, in_=x, func=f, bias=b, scale=s), waits, inc)

        def s_cp(out, a, waits=(), inc="SC"):
            return E("scalar", lambda e, o=out, x=a: e.copy(out=o, in_=x), waits, inc)

        def t_mm(psum, lhsT, rhs, start, stop, waits=(), inc="TN"):
            return E("tensor", lambda e, o=psum, l=lhsT, r=rhs, a=start, b=stop:
                     e.matmul(o, l, r, start=a, stop=b), waits, inc)

        def t_tr(psum, src, ident, waits=(), inc="TN"):
            return E("tensor", lambda e, o=psum, x=src, i=ident:
                     e.transpose(o, x, i), waits, inc)

        def recip_pos(dst, src, tmp, waits):
            t1 = s_act(tmp, src, AF.Ln, waits=waits)
            t2 = s_act(dst, tmp, AF.Exp, scale=-1.0, waits=[t1])
            t3 = v_tt(tmp, src, dst, OP.mult, waits=[t2])
            t4 = v_ts(tmp, tmp, -1.0, 2.0, OP.mult, OP.add, waits=[t3])
            return v_tt(dst, dst, tmp, OP.mult, waits=[t4])

        def rsqrt_pos(dst, src, tmp, waits):
            t1 = s_act(tmp, src, AF.Ln, waits=waits)
            t2 = s_act(dst, tmp, AF.Exp, scale=-0.5, waits=[t1])
            t3 = v_tt(tmp, dst, dst, OP.mult, waits=[t2])
            t4 = v_tt(tmp, tmp, src, OP.mult, waits=[t3])
            t5 = v_ts(tmp, tmp, -0.5, 1.5, OP.mult, OP.add, waits=[t4])
            return v_tt(dst, dst, tmp, OP.mult, waits=[t5])

        for dst, src in (
            (tri_s[:, :], dr["tri"][:, :]), (shf_s[:, :], dr["shift"][:, :]),
            (eye_s[:, :], dr["eye"][:, :]),
            (cols[:, 0:8], dr["ln1w_col"][:, :]), (cols[:, 8:16], dr["ln1b_col"][:, :]),
            (cols[:, 16:24], dr["convb_col"][:, :]), (cols[:, 24:32], dr["ln2w_col"][:, :]),
            (cols[:, 32:40], dr["ln2b_col"][:, :]),
            (ROW_GB, dr["gb_row"][:, :]), (ROW_ONW, dr["onw_row"][:, :]),
            (ROW_ONB, dr["onb_row"][:, :]), (ROW_G2, dr["gam2_row"][:, :]),
            (rowtmp, dr["gam1_row"][:, :]),
            (selc_s[:, :], dr["selc"][:, :]), (sel_last_s[:, :], dr["sel_last"][:, :]),
            (gamsb[:, :], dr["gamma"][:, :]),
            (cwcol[:, :], dr["convw_col"][:, :]),
            (Bex2, dr["ktab"][:, :]), (Bex3, dr["twok"][:, :]),
        ):
            DMA("sync", dst, src)
        W0 = [P.tok("LD")]

        ch = v_ms(ones1, 1.0, W0)
        ch = v_ms(e0r, 0.0, [ch])
        ch = v_ms(mmrow[0:1, 2176:2177], 1.0, [ch])
        ch = v_ms(ones2b[:, :], 1.0, [ch])
        ch = v_ms(carry, 0.0, [ch])
        ch = v_ms(ktail, 0.0, [ch])
        ch = v_cp(eyeb[:, :], eye_s[:, :], [ch])

        ch = v_ms(dech[0:H, 0:1], 1.0, [ch])
        ch = v_cp(pcol[:, :], gamsb[:, :], [ch])
        n = 1
        while n < TSEQ:
            nn = min(n, TSEQ - n)
            ch = v_ts(dech[0:H, n:n + nn], dech[0:H, 0:nn], pcol[0:H, 0:1], None,
                      OP.mult, waits=[ch])
            n += nn
            if n < TSEQ:
                ch = v_tt(pcol[:, :], pcol[:, :], pcol[:, :], OP.mult, waits=[ch])
        for tb in range(16):
            tt = t_tr(ps1[:, 0:H], dech[0:H, tb * 128:(tb + 1) * 128],
                      eye_s[0:H, 0:H], [ch])
            ch = s_cp(dec[:, tb * H:(tb + 1) * H], ps1[:, 0:H], [tt])
        ch = v_ts(rdec[:, :], dec[:, :], 1e-6, None, OP.add, waits=[ch])
        ch = recip_pos(Bt1[:, 0:16 * H], rdec[:, :], Bt2[:, 0:16 * H], [ch])
        ch = v_cp(rdec[:, :], Bt1[:, 0:16 * H], [ch])

        ch = E("gpsimd", lambda e: e.iota(Bex, [[1, HD]], channel_multiplier=0,
               allow_small_or_imprecise_dtypes=True), [ch], inc="GP")
        ch = v_ms(stat[:, 0:1], 63.0, [ch])
        ch = recip_pos(stat[:, 1:2], stat[:, 0:1], stat[:, 2:3], [ch])
        ch = v_ts(Bex, Bex, stat[:, 1:2], None, OP.mult, waits=[ch])
        ch = v_ts(Bfrq[:, 0:HD], Bex, L10H, None, OP.mult, waits=[ch])
        ch = v_stt(Bfrq[:, 0:HD], Bex2, -LN2H, Bfrq[:, 0:HD], OP.mult, OP.add, [ch])
        ch = v_ts(Bfrq[:, HD:2 * HD], Bex, L10L, None, OP.mult, waits=[ch])
        ch = v_tt(Bfrq[:, 0:HD], Bfrq[:, 0:HD], Bfrq[:, HD:2 * HD], OP.add, waits=[ch])
        ch = v_stt(Bfrq[:, 0:HD], Bex2, -LN2L, Bfrq[:, 0:HD], OP.mult, OP.add, [ch])
        ch = v_ms(ccol[:, 0:2], 1.0, [ch])
        for j in range(2, 11):
            ch = v_ms(stat[:, 3:4], float(j), [ch])
            ch = recip_pos(stat[:, 4:5], stat[:, 3:4], stat[:, 5:6], [ch])
            ch = v_ts(ccol[:, j:j + 1], ccol[:, j - 1:j], stat[:, 4:5], None, OP.mult, waits=[ch])
        ch = v_ts(Bex2, Bfrq[:, 0:HD], 0.0, ccol[:, 10:11], OP.mult, OP.add, [ch])
        for j in range(9, -1, -1):
            ch = v_tt(Bex2, Bex2, Bfrq[:, 0:HD], OP.mult, waits=[ch])
            ch = v_ts(Bex2, Bex2, ccol[:, j:j + 1], None, OP.add, waits=[ch])
        ch = v_tt(Bex2, Bex2, Bex3, OP.mult, waits=[ch])
        for h in range(H - 1, -1, -1):
            ch = v_ts(Bfrq[:, h * HD:(h + 1) * HD], Bex2, float(h), None, OP.mult, waits=[ch])

        for tb in range(16):
            l1 = DMA("sync", Bnt[:, 0:D], dr["ntab"][tb * 128:(tb + 1) * 128, :], [ch])
            ch = E("gpsimd", lambda e, b=tb: e.iota(stat[:, 6:7], [[1, 1]], base=b * 128,
                   channel_multiplier=1, allow_small_or_imprecise_dtypes=True),
                   [ch], inc="GP")
            ch = v_ts(Bt1[:, 0:D], Bfrq[:, 0:D], stat[:, 6:7], None, OP.mult, waits=[ch])
            for ci in CW2PI:
                ch = v_stt(Bt1[:, 0:D], Bnt[:, 0:D], -ci, Bt1[:, 0:D], OP.mult, OP.add,
                           [ch, l1])
            ch = s_act(Bt1[:, 0:D], Bt1[:, 0:D], AF.Abs, waits=[ch])
            ch = v_ts(Bt2[:, 0:D], Bt1[:, 0:D], -1.0, PIO2, OP.mult, OP.add, waits=[ch])
            ch = s_act(Bt2[:, 0:D], Bt2[:, 0:D], AF.Sin, waits=[ch])
            ch = DMA("sync", sc["cos"][tb * 128:(tb + 1) * 128, :], Bt2[:, 0:D], [ch],
                     sem="ST")

        def bcast_row(dst128, src_row, w):
            w = v_cp(Bxc[0:1, 0:1024], src_row, [w])
            w = v_cp(HI32, Bxc[0:1, 0:1024], [w])
            w = v_tt(LO32, src_row, HI32, OP.subtract, waits=[w])
            w = v_cp(Bxc[0:1, 1024:2048], LO32, [w])
            w = DMA("gpsimd", hl[0:2, :], HL16, [w], sem="WT")
            for sl in range(2):
                pp = (ps1, ps2)[sl]
                m1 = t_mm(pp[:, 0:512], ones2b[:, :], hl[:, sl * 512:(sl + 1) * 512],
                          True, True, waits=[w])
                w = s_cp(dst128[:, sl * 512:(sl + 1) * 512], pp[:, 0:512], [m1])
            return w

        ch = v_tt(rowtmp, rowtmp, ROW_ONW, OP.mult, waits=[ch])
        ch = bcast_row(sob, rowtmp, ch)
        ch = DMA("sync", rowtmp, dr["gam1_row"][:, :], [ch])
        ch = v_tt(rowtmp, rowtmp, ROW_ONB, OP.mult, waits=[ch])
        ch = bcast_row(bob, rowtmp, ch)
        ch = bcast_row(g2b, ROW_G2, ch)
        ch = v_cp(Bxc[0:1, 0:1024], ROW_GB, [ch])
        ch = v_cp(HI32, Bxc[0:1, 0:1024], [ch])
        ch = v_tt(LO32, ROW_GB, HI32, OP.subtract, waits=[ch])
        ch = v_cp(Bxc[0:1, 1024:2048], LO32, [ch])
        ch = DMA("gpsimd", gbhl[0:2, :], HL16, [ch], sem="WT")

        conv_specs = []
        for nm in ("qk", "v", "g", "o"):
            for c in range(NCH):
                conv_specs.append(("plain", dr[nm + "_wT"][c * 128:(c + 1) * 128, :],
                                   sc[nm][c, :, :], None))
        for f in range(32):
            conv_specs.append(("g2", dr["down_wT"][f * 128:(f + 1) * 128, :],
                               sc["down"][f, :, :], None))
        for f in range(128):
            conv_specs.append(("plain", dr["wc_Tp"][f * 128:(f + 1) * 128, :],
                               sc["wc"][f, :, :], None))
        for wi, nm in (("gate_t4", "gate"), ("up_t4", "up")):
            for m in range(32):
                conv_specs.append(("gu", dr[wi][m, :, :, :].rearrange("c p w -> p c w"),
                                   sc[nm][m, :, :], (nm, m)))
        slot_tok = {0: [ch], 1: [ch], 2: [ch], 3: [ch]}
        prev_v = ch
        prev_beta = ch
        cv32 = [Wc32[:, 0:D], Wc32[:, D:2 * D], Bt2[:, 0:D], Bt2[:, D:2 * D]]
        cv16 = [Wc16[:, 0:D], Wc16[:, D:2 * D], Bxc[:, 0:D], Bxc[:, D:2 * D]]
        for i, (mode, src, dst, extra) in enumerate(conv_specs):
            par = i % 4
            wdst = cv32[par]
            if mode == "gu":
                wdst = wdst.rearrange("p (c w) -> p c w", w=128)
            q_ = "gpsimd" if par < 2 else "sync"
            lw = DMA(q_, wdst, src, slot_tok[par], sem="V%d" % par)
            if mode == "g2":
                tv = v_tt(cv16[par], cv32[par], g2b[:, :], OP.mult,
                          waits=[lw, prev_v])
            elif mode == "gu":
                tv = prev_v
                for c in range(NCH):
                    tv = v_ts(cv16[par][:, c * 128:(c + 1) * 128],
                              cv32[par][:, c * 128:(c + 1) * 128],
                              cols[:, 24 + c:25 + c], None, OP.mult, waits=[lw, tv])
            else:
                tv = v_cp(cv16[par], cv32[par], [lw, prev_v])
            st = DMA(q_, dst, cv16[par], [tv], sem="U%d" % par)
            new_tok = [st]
            if mode == "gu":
                nm_, m_ = extra
                pm = None
                for c in range(NCH):
                    pm = t_mm(ps3[0:1, 0:128], cols[:, 32 + c:33 + c],
                              cv32[par][:, c * 128:(c + 1) * 128],
                              c == 0, c == 7,
                              waits=[lw, prev_beta] if c == 0 else [])
                bfn = betaG_ap if nm_ == "gate" else betaU_ap
                be = s_cp(bfn(m_, 0, 128), ps3[0:1, 0:128], [pm])
                prev_beta = be
                new_tok = [st, be]
            slot_tok[par] = new_tok
            prev_v = tv
        CONV_END = slot_tok[0] + slot_tok[1] + slot_tok[2] + slot_tok[3] + [prev_beta]

        ch2 = prev_beta
        for m in range(32):
            tt = t_tr(ps4[:, 0:1], betaG_ap(m, 0, 128), eye_s[0:1, 0:1], [ch2])
            ch2 = s_cp(bgcol[:, m:m + 1], ps4[:, 0:1], [tt])
        for m in range(32):
            tt = t_tr(ps4[:, 0:1], betaU_ap(m, 0, 128), eye_s[0:1, 0:1], [ch2])
            ch2 = s_cp(bucol[:, m:m + 1], ps4[:, 0:1], [tt])

        ctok = [ch2] + CONV_END
        psC = [ps3, ps4, ps5, ps6]
        mk_tok = [ctok, ctok]
        up_tok = [list(ctok)] * 4
        ev_tok = [list(ctok)] * 4
        st_tok = [list(ctok)] * 4
        mm_tok = {}
        for hq in range(4):
            for cq in range(8):
                hc = hq * 8 + cq
                mp = hc % 2
                l1 = DMA("gpsimd", Wc32[:, mp * 128:(mp + 1) * 128],
                         dr["mk_bd"][hq, cq, :, :], mk_tok[mp], sem="V%d" % mp)
                t1 = v_cp(Wc16[:, 1536 + mp * 128:1536 + (mp + 1) * 128],
                          Wc32[:, mp * 128:(mp + 1) * 128], [l1])
                last_mm = None
                for gq in range(4):
                    ti = hc * 4 + gq
                    sp = ti % 4
                    l2 = DMA("sync", Wc32[:, 256 + sp * 128:256 + (sp + 1) * 128],
                             dr["up_bd"][hq, cq, gq, :, :], up_tok[sp],
                             sem="V%d" % (2 + sp % 2))
                    t2 = v_cp(Wc16[:, 256 + sp * 128:256 + (sp + 1) * 128],
                              Wc32[:, 256 + sp * 128:256 + (sp + 1) * 128], [l2])
                    mm = t_mm(psC[sp][:, 0:128],
                              Wc16[:, 1536 + mp * 128:1536 + (mp + 1) * 128],
                              Wc16[:, 256 + sp * 128:256 + (sp + 1) * 128],
                              True, True, waits=[t1, t2] + ev_tok[sp])
                    up_tok[sp] = [mm]
                    ev = s_cp(Wc16[:, 768 + sp * 128:768 + (sp + 1) * 128],
                              psC[sp][:, 0:128], [mm] + st_tok[sp])
                    ev_tok[sp] = [ev]
                    stt_ = DMA("gpsimd", sc["ct"][ti, :, :],
                               Wc16[:, 768 + sp * 128:768 + (sp + 1) * 128],
                               [ev], sem="U%d" % (sp % 2))
                    st_tok[sp] = [stt_]
                    last_mm = mm
                mk_tok[mp] = [last_mm]
        last_c = ev_tok[3][0]
        CT_ST = [P.tok("U0"), P.tok("U1")]
        PRO_DONE = [last_c, P.tok("WT"), P.tok("ST"), P.tok("VC"), P.tok("SC"),
                    P.tok("TN"), P.tok("GP"), P.tok("V0"), P.tok("V1"),
                    P.tok("V2"), P.tok("V3"), P.tok("U0"), P.tok("U1"),
                    P.tok("U2"), P.tok("U3"), P.tok("LD")]

        def layer_norm(src, dst, w):
            w1 = v_tt(Bsq[:, 0:D], src, src, OP.mult, waits=w)
            w2 = v_red(stat[:, 0:1], src, waits=w)
            w3 = v_red(stat[:, 1:2], Bsq[:, 0:D], waits=[w1])
            w4 = v_ts(stat[:, 2:3], stat[:, 0:1], R1024, None, OP.mult, waits=[w2])
            w5 = v_ts(stat[:, 3:4], stat[:, 1:2], R1024, None, OP.mult, waits=[w3])
            w6 = v_tt(stat[:, 4:5], stat[:, 2:3], stat[:, 2:3], OP.mult, waits=[w4])
            w7 = v_tt(stat[:, 5:6], stat[:, 3:4], stat[:, 4:5], OP.subtract,
                      waits=[w5, w6])
            w8 = v_ts(stat[:, 5:6], stat[:, 5:6], EPS, None, OP.add, waits=[w7])
            w9 = rsqrt_pos(stat[:, 6:7], stat[:, 5:6], stat[:, 7:8], [w8])
            return v_ts(dst, src, stat[:, 2:3], stat[:, 6:7], OP.subtract, OP.mult,
                        waits=[w9])

        slot_hist = {"q": [list(PRO_DONE)] * 2, "gu": [list(PRO_DONE)] * 4,
                     "dn": [list(PRO_DONE)] * 4, "cp": [list(PRO_DONE)] * 4}
        xread_tok = [PRO_DONE[0], PRO_DONE[0]]
        cosread_tok = [PRO_DONE[0]]
        psT_evac = [PRO_DONE[0], PRO_DONE[0]]
        store_tok = [PRO_DONE[0]]
        fl_tok = [PRO_DONE[0]]
        fin_tok = [PRO_DONE[0]]
        hv = Bhfm[:, :].rearrange("p (c w) -> p c w", w=HMT)
        xcv = Bxc[:, :].rearrange("p (c w) -> p c w", w=MT)
        cacc = Bg[:, 0:NCH * MT].rearrange("p (c w) -> p c w", w=MT)

        def layer_norm(src_, dst, w, co=0, sq=None):
            sqb = Bsq[:, 0:D] if sq is None else sq
            w2 = v_red(stat[:, co + 0:co + 1], src_, waits=w)
            w3 = v_sq_accum(sqb, src_, stat[:, co + 1:co + 2], waits=[w2])
            w4 = v_ts(stat[:, co + 2:co + 3], stat[:, co + 0:co + 1], R1024, None,
                      OP.mult, waits=[w2])
            w5 = v_ts(stat[:, co + 3:co + 4], stat[:, co + 1:co + 2], R1024, None,
                      OP.mult, waits=[w3])
            w6 = v_tt(stat[:, co + 4:co + 5], stat[:, co + 2:co + 3],
                      stat[:, co + 2:co + 3], OP.mult, waits=[w4])
            w7 = v_tt(stat[:, co + 5:co + 6], stat[:, co + 3:co + 4],
                      stat[:, co + 4:co + 5], OP.subtract, waits=[w5, w6])
            w8 = v_ts(stat[:, co + 5:co + 6], stat[:, co + 5:co + 6], EPS, None,
                      OP.add, waits=[w7])
            w9 = rsqrt_pos(stat[:, co + 6:co + 7], stat[:, co + 5:co + 6],
                           stat[:, co + 7:co + 8], [w8])
            return v_ts(dst, src_, stat[:, co + 2:co + 3], stat[:, co + 6:co + 7],
                        OP.subtract, OP.mult, waits=[w9])

        def front_gen(b, m, prev, pub):
            gi = b * nmt_per_seq + m
            par = gi % 2
            BxP = (Bx, Bx2)[par]
            pub["xpar"] = par
            lxx, lcc = [], []
            for blk in range(NBLK):
                lxx.append(DMA("sync", BxP[:, blk * D:(blk + 1) * D],
                               dr["x"][b, m * MT + blk * 128:m * MT + (blk + 1) * 128, :],
                               [xread_tok[par]], sem="X%d" % blk))
                lcc.append(DMA("sync", Bcos[:, blk * D:(blk + 1) * D],
                               sc["cos"][m * MT + blk * 128:m * MT + (blk + 1) * 128, :],
                               [cosread_tok[0], prev["mix"]], sem="O%d" % blk))
            pub["lcos"] = lcc
            yield
            if m == 0:
                z0 = v_ms(carry, 0.0, [prev["mix"]])
                z0 = v_ms(ktail, 0.0, [z0])
                hcp = v_ms(hv[:, :, 0:3], 0.0, [z0, prev["XC"]])
            else:
                hcp = v_cp(hv[:, :, 0:3], hv[:, :, MT:MT + 3], [prev["XC"]])
            pub["hz"] = hcp
            yield
            ln1 = []
            for blk in range(NBLK):
                t = layer_norm(BxP[:, blk * D:(blk + 1) * D],
                               Bv[:, blk * D:(blk + 1) * D],
                               [lxx[blk], prev["MIXFM"]] + ([ln1[-1]] if ln1 else []),
                               co=8, sq=Bg[:, 0:D])
                ln1.append(t)
                yield
            ev = hcp
            for blk in range(NBLK):
                w = v_cp(Bxc[:, 0:1024], Bv[:, blk * D:(blk + 1) * D],
                         [ln1[blk], prev["mix"]])
                w = v_cp(Bg[:, D:2 * D], Bxc[:, 0:1024], [w])
                w = v_tt(Bg[:, 0:D], Bv[:, blk * D:(blk + 1) * D], Bg[:, D:2 * D],
                         OP.subtract, waits=[w])
                w = v_cp(Bxc[:, 1024:2048], Bg[:, 0:D], [w])
                yield
                for c in range(NCH):
                    t1 = t_tr(psT[:, 0:128], Bxc[:, c * 128:(c + 1) * 128],
                              eyeb[:, :], waits=[w, psT_evac[0]])
                    t2 = t_tr(psT[:, 1024:1152], Bxc[:, 1024 + c * 128:1024 + (c + 1) * 128],
                              eyeb[:, :], waits=[w, psT_evac[1]])
                    hs = Bhfm[:, c * HMT + 3 + blk * 128:c * HMT + 3 + (blk + 1) * 128]
                    e1 = s_cp(hs, psT[:, 0:128], [t1, ev])
                    ev = v_tt(hs, hs, psT[:, 1024:1152], OP.add, waits=[t2, e1])
                    psT_evac[0] = e1
                    psT_evac[1] = ev
                    if c % 2 == 1:
                        yield
            cv = ev
            for c in range(NCH):
                hs = Bhfm[:, c * HMT + 3:c * HMT + 3 + MT]
                cv = v_ts(hs, hs, cols[:, c:c + 1], cols[:, 8 + c:9 + c],
                          OP.mult, OP.add, waits=[cv])
                if c % 4 == 3:
                    yield
            for c in range(NCH):
                cv = v_ts(Bg[:, c * MT:(c + 1) * MT], Bhfm[:, c * HMT:c * HMT + MT],
                          cwcol[:, c * 4:c * 4 + 1], cols[:, 16 + c:17 + c],
                          OP.mult, OP.add, waits=[cv])
                if c % 4 == 3:
                    yield
            for k in range(1, 4):
                for c in range(NCH):
                    dst = (Bxc[:, c * MT:(c + 1) * MT] if k == 3
                           else Bg[:, c * MT:(c + 1) * MT])
                    cv = v_stt(dst, Bhfm[:, c * HMT + k:c * HMT + k + MT],
                               cwcol[:, c * 4 + k:c * 4 + k + 1],
                               Bg[:, c * MT:(c + 1) * MT], OP.mult, OP.add,
                               waits=[cv])
                    if c % 4 == 3:
                        yield
            pub["XC"] = cv
            pub["done"] = True

        def drain(fg, n=1):
            if fg is None:
                return
            for _ in range(n):
                try:
                    next(fg)
                except StopIteration:
                    return

        def fm_transpose(w0):
            ev_ = w0
            for blk in range(NBLK):
                for c in range(NCH):
                    bk = (blk * NCH + c) % 2
                    po = bk * 1024
                    tt = t_tr(psT[:, po:po + 128],
                              Bmb[:, blk * D + c * 128:blk * D + (c + 1) * 128],
                              eyeb[:, :], waits=[w0, psT_evac[bk]])
                    ev_ = s_cp(Bfm[:, c * MT + blk * 128:c * MT + (blk + 1) * 128],
                               psT[:, po:po + 128], [tt])
                    psT_evac[bk] = ev_
            return ev_

        mts = [(b, m) for b in range(nseq) for m in range(nmt_per_seq)]
        ft = {}
        fg = front_gen(0, 0, {"mix": PRO_DONE[0], "XC": PRO_DONE[0],
                              "MIXFM": PRO_DONE[0]}, ft)
        drain(fg, 1000)

        for mi_, (b, m) in enumerate(mts):
            t0 = m * MT
            gb0 = t0 // 128
            par = ft["xpar"]
            BxP = (Bx, Bx2)[par]
            XC = ft["XC"]
            lcos = ft["lcos"]

            pair_cycle = [(ps1, ps2), (ps3, ps4), (ps5, ps6)]
            pair_evac = {0: fin_tok[0], 1: fin_tok[0], 2: fin_tok[0]}
            last_ev = fl_tok[0]
            for ti, (nm, dstb) in enumerate((("qk", Bk), ("v", Bv), ("g", Bg))):
                mend = {}
                for c in range(NCH):
                    wpar = c % 2
                    lw = DMA("gpsimd", Wq[:, wpar * D:(wpar + 1) * D], sc[nm][c, :, :],
                             slot_hist["q"][wpar], sem="Q%d" % wpar)
                    for blk in range(NBLK):
                        pr = (ti * NBLK + blk) % 3
                        pa, pb = pair_cycle[pr]
                        w_ = [lw, XC, pair_evac[pr], fin_tok[0]] if c == 0 else [lw]
                        last = (c == 7) and nm != "g"
                        t_mm(pa[:, 0:512],
                             Bxc[:, c * MT + blk * 128:c * MT + (blk + 1) * 128],
                             Wq[:, wpar * D:wpar * D + 512], c == 0, last, waits=w_)
                        mend[blk] = t_mm(pb[:, 0:512],
                                         Bxc[:, c * MT + blk * 128:c * MT + (blk + 1) * 128],
                                         Wq[:, wpar * D + 512:(wpar + 1) * D],
                                         c == 0, last)
                    slot_hist["q"][wpar] = [mend[NBLK - 1]]
                if nm == "g":
                    for blk in range(NBLK):
                        pr = (ti * NBLK + blk) % 3
                        pa, pb = pair_cycle[pr]
                        t_mm(pa[:, 0:512], ones2b[:, :], gbhl[:, 0:512], False, True)
                        mend[blk] = t_mm(pb[:, 0:512], ones2b[:, :],
                                         gbhl[:, 512:1024], False, True)
                for blk in range(NBLK):
                    pr = (ti * NBLK + blk) % 3
                    pa, pb = pair_cycle[pr]
                    if nm == "g":
                        e1 = s_act(dstb[:, blk * D:blk * D + 512], pa[:, 0:512],
                                   AF.Sigmoid, waits=[mend[blk], last_ev])
                        e2 = s_act(dstb[:, blk * D + 512:(blk + 1) * D], pb[:, 0:512],
                                   AF.Sigmoid, waits=[e1])
                    else:
                        e1 = s_cp(dstb[:, blk * D:blk * D + 512], pa[:, 0:512],
                                  [mend[blk], last_ev])
                        e2 = s_cp(dstb[:, blk * D + 512:(blk + 1) * D], pb[:, 0:512],
                                  [e1])
                    pair_evac[pr] = e2
                    last_ev = e2
            QKVG = last_ev

            kk = QKVG
            for blk in range(NBLK):
                kk = v_tt(Bsq[:, 0:D], Bk[:, blk * D:(blk + 1) * D],
                          Bk[:, blk * D:(blk + 1) * D], OP.mult, waits=[kk])
                kk = v_red(kn[:, 32:48],
                           Bsq[:, 0:D].rearrange("p (h w) -> p h w", w=HD),
                           waits=[kk])
                kk = v_ts(kn[:, 32:48], kn[:, 32:48], -1e-24, None, OP.add, waits=[kk])
                kk = s_act(kn[:, 32:48], kn[:, 32:48], AF.Relu, waits=[kk])
                kk = v_ts(kn[:, 32:48], kn[:, 32:48], 1e-24, None, OP.add, waits=[kk])
                kk = rsqrt_pos(kn[:, 0:16], kn[:, 32:48], kn[:, 16:32], [kk])
                for h in range(H):
                    kk = v_ts(Bk[:, blk * D + h * HD:blk * D + (h + 1) * HD],
                              Bk[:, blk * D + h * HD:blk * D + (h + 1) * HD],
                              kn[:, h:h + 1], None, OP.mult, waits=[kk])
            mix = kk
            mixg = kk
            cu_prev = kk

            for blk in range(NBLK):
                gbi = gb0 + blk
                m1 = t_mm(ps5[:, 0:512], shf_s[:, :], Bk[:, blk * D:blk * D + 512],
                          True, False, waits=[mix] + ([mixg] if blk > 0 else []))
                m2 = t_mm(ps6[:, 0:512], shf_s[:, :],
                          Bk[:, blk * D + 512:(blk + 1) * D], True, False)
                if blk == 0:
                    m1 = t_mm(ps5[:, 0:512], e0r, ktail[0:1, 0:512], False, True)
                    m2 = t_mm(ps6[:, 0:512], e0r, ktail[0:1, 512:1024], False, True)
                else:
                    m1 = t_mm(ps5[:, 0:512], sel_last_s[:, :], Bk[:, 0:512],
                              False, True)
                    m2 = t_mm(ps6[:, 0:512], sel_last_s[:, :], Bk[:, 512:1024],
                              False, True)
                s1 = v_tt(Bt1[:, blk * D:blk * D + 512],
                          Bcos[:, blk * D:blk * D + 512], ps5[:, 0:512], OP.add,
                          waits=[m1, lcos[blk]])
                s1 = v_tt(Bt1[:, blk * D + 512:(blk + 1) * D],
                          Bcos[:, blk * D + 512:(blk + 1) * D], ps6[:, 0:512],
                          OP.add, waits=[m2, s1])
                s2 = g_tt(Bg[:, blk * D:(blk + 1) * D], Bv[:, blk * D:(blk + 1) * D],
                          Bg[:, blk * D:(blk + 1) * D], OP.mult, waits=[s1])
                s3 = v_tt(Bt1[:, blk * D:(blk + 1) * D], Bt1[:, blk * D:(blk + 1) * D],
                          Bg[:, blk * D:(blk + 1) * D], OP.mult, waits=[s2])
                for h in range(H):
                    s3 = v_ts(Bt1[:, blk * D + h * HD:blk * D + (h + 1) * HD],
                              Bt1[:, blk * D + h * HD:blk * D + (h + 1) * HD],
                              dec[:, gbi * H + h:gbi * H + h + 1], None, OP.mult,
                              waits=[s3])
                c1 = t_mm(ps1[:, 0:512], tri_s[:, :], Bt1[:, blk * D:blk * D + 512],
                          True, False, waits=[s3])
                c1 = t_mm(ps1[:, 0:512], ones1, carry[0:1, 0:512], False, True)
                c2 = t_mm(ps2[:, 0:512], tri_s[:, :],
                          Bt1[:, blk * D + 512:(blk + 1) * D], True, False)
                c2 = t_mm(ps2[:, 0:512], ones1, carry[0:1, 512:1024], False, True)
                b1 = t_mm(ps3[0:1, 0:512], ones1c, Bt1[:, blk * D:blk * D + 512],
                          True, True, waits=[s3, cu_prev])
                b2 = t_mm(ps4[0:1, 0:512], ones1c,
                          Bt1[:, blk * D + 512:(blk + 1) * D], True, True)
                cu = v_tt(carry[0:1, 0:512], carry[0:1, 0:512], ps3[0:1, 0:512],
                          OP.add, waits=[c1, b1])
                cu = v_tt(carry[0:1, 512:1024], carry[0:1, 512:1024],
                          ps4[0:1, 0:512], OP.add, waits=[c2, b2, cu])
                cu_prev = cu
                mv = cu
                for h in range(H):
                    pp = ps1 if h < 8 else ps2
                    off = (h % 8) * HD
                    mv = v_ts(Bv[:, blk * D + h * HD:blk * D + (h + 1) * HD],
                              pp[:, off:off + HD],
                              rdec[:, gbi * H + h:gbi * H + h + 1], None, OP.mult,
                              waits=[c1, c2, mv])
                mix = mv
            kt1 = t_mm(ps3[0:1, 0:512], selc_s[:, 0:1], Bk[:, D:D + 512],
                       True, True, waits=[mix, mixg])
            kt2 = t_mm(ps4[0:1, 0:512], selc_s[:, 0:1], Bk[:, D + 512:2 * D],
                       True, True)
            ktv = s_cp(ktail[0:1, 0:512], ps3[0:1, 0:512], [kt1])
            ktv = s_cp(ktail[0:1, 512:1024], ps4[0:1, 0:512], [ktv, kt2])
            mix = v_tt(Bv[:, :], Bv[:, :], Bk[:, :], OP.mult, waits=[mix, mixg, ktv])
            mix = E("scalar", lambda e: e.copy(out=Bmb[:, :], in_=Bv[:, :]),
                    [mix], inc="SC")
            MIXFM = fm_transpose(mix)

            mend = None
            for c in range(NCH):
                wpar = c % 2
                lw = DMA("gpsimd", Wq[:, wpar * D:(wpar + 1) * D], sc["o"][c, :, :],
                         slot_hist["q"][wpar], sem="Q%d" % wpar)
                for blk in range(NBLK):
                    pa, pb = (psA, psB)[blk]
                    w_ = [lw, MIXFM] if c == 0 else [lw]
                    t_mm(pa[:, 0:512],
                         Bfm[:, c * MT + blk * 128:c * MT + (blk + 1) * 128],
                         Wq[:, wpar * D:wpar * D + 512], c == 0, c == 7, waits=w_)
                    mend = t_mm(pb[:, 0:512],
                                Bfm[:, c * MT + blk * 128:c * MT + (blk + 1) * 128],
                                Wq[:, wpar * D + 512:(wpar + 1) * D], c == 0, c == 7)
                slot_hist["q"][wpar] = [mend]
            ev = mend
            for blk in range(NBLK):
                pa, pb = (psA, psB)[blk]
                ev = s_cp(Bt2[:, blk * D:blk * D + 512], pa[:, 0:512],
                          [ev, fl_tok[0]])
                ev = s_cp(Bt2[:, blk * D + 512:(blk + 1) * D], pb[:, 0:512], [ev])
            on = [ev]
            for blk in range(NBLK):
                on = [layer_norm(Bt2[:, blk * D:(blk + 1) * D],
                                 Bt2[:, blk * D:(blk + 1) * D], on)]
            x2 = on[0]
            for blk in range(NBLK):
                x2 = v_tt(Bt3[:, blk * D:(blk + 1) * D], Bt2[:, blk * D:(blk + 1) * D],
                          sob[:, :], OP.mult, waits=[x2, store_tok[0]])
                x2 = v_tt(Bt3[:, blk * D:(blk + 1) * D], Bt3[:, blk * D:(blk + 1) * D],
                          bob[:, :], OP.add, waits=[x2])
                x2 = v_tt(Bk[:, blk * D:(blk + 1) * D], Bt3[:, blk * D:(blk + 1) * D],
                          BxP[:, blk * D:(blk + 1) * D], OP.add, waits=[x2])

            l2t = [x2]
            for blk in range(NBLK):
                l2t = [layer_norm(Bk[:, blk * D:(blk + 1) * D],
                                  Bt2[:, blk * D:(blk + 1) * D], l2t)]
            l2c = v_cp(Bmb[:, :], Bt2[:, :], l2t)
            H2FM = fm_transpose(l2c)

            nxt = mts[mi_ + 1] if mi_ + 1 < len(mts) else None
            ft_next = {}
            fg = None
            if nxt is not None:
                fg = front_gen(nxt[0], nxt[1],
                               {"mix": mix, "XC": XC, "MIXFM": MIXFM}, ft_next)

            gevs = []
            for mi in range(32):
                wpar = mi % 4
                lg = DMA("gpsimd", Wgu[:, wpar * 2048:wpar * 2048 + 1024],
                         sc["gate"][mi, :, :], slot_hist["gu"][wpar], sem="G%d" % wpar)
                lu = DMA("gpsimd", Wgu[:, wpar * 2048 + 1024:(wpar + 1) * 2048],
                         sc["up"][mi, :, :], slot_hist["gu"][wpar], sem="G%d" % wpar)
                pg = (ps5, ps3)[mi % 2]
                pu = (ps6, ps4)[mi % 2]
                mg = mu = None
                for c in range(NCH):
                    w_ = [lg, lu, H2FM]
                    if c == 0 and mi >= 2:
                        w_.append(gevs[mi - 2])
                    mg = t_mm(pg[:, 0:MT],
                              Wgu[:, wpar * 2048 + c * 128:wpar * 2048 + (c + 1) * 128],
                              Bfm[:, c * MT:(c + 1) * MT], c == 0, c == 7,
                              waits=w_ if c == 0 else [lg])
                    mu = t_mm(pu[:, 0:MT],
                              Wgu[:, wpar * 2048 + 1024 + c * 128:wpar * 2048 + 1024 + (c + 1) * 128],
                              Bfm[:, c * MT:(c + 1) * MT], c == 0, c == 7)
                slot_hist["gu"][wpar] = [mu]
                pp2 = mi % 2
                sl = s_act(Bsl[:, pp2 * MT:(pp2 + 1) * MT], pg[:, 0:MT], AF.Silu,
                           bias=bgcol[:, mi:mi + 1], waits=[mu])
                gv = v_stt(Bgu[:, mi * MT:(mi + 1) * MT], pu[:, 0:MT],
                           bucol[:, mi:mi + 1], Bsl[:, pp2 * MT:(pp2 + 1) * MT],
                           OP.add, OP.mult, waits=[sl])
                gevs.append(gv)
                drain(fg)
            GU = gevs[-1]

            dm = None
            for f in range(32):
                wpar = f % 4
                lw = DMA("gpsimd", Wdn[:, wpar * 1024:(wpar + 1) * 1024],
                         sc["down"][f, :, :], slot_hist["dn"][wpar], sem="D%d" % wpar)
                for blk in range(NBLK):
                    pa, pb = (psA, psB)[blk]
                    w_ = [lw, GU] if f == 0 else [lw]
                    t_mm(pa[:, 0:512],
                         Bgu[:, f * MT + blk * 128:f * MT + (blk + 1) * 128],
                         Wdn[:, wpar * 1024:wpar * 1024 + 512], f == 0, f == 31,
                         waits=w_)
                    dm = t_mm(pb[:, 0:512],
                              Bgu[:, f * MT + blk * 128:f * MT + (blk + 1) * 128],
                              Wdn[:, wpar * 1024 + 512:(wpar + 1) * 1024],
                              f == 0, f == 31)
                slot_hist["dn"][wpar] = [dm]
                drain(fg)
            yv = dm
            for blk in range(NBLK):
                pa, pb = (psA, psB)[blk]
                yv = v_tt(Bt3[:, blk * D:blk * D + 512], pa[:, 0:512],
                          Bk[:, blk * D:blk * D + 512], OP.add, waits=[yv])
                yv = v_tt(Bt3[:, blk * D + 512:(blk + 1) * D], pb[:, 0:512],
                          Bk[:, blk * D + 512:(blk + 1) * D], OP.add, waits=[yv])
                yv = v_tt(Bt3[:, blk * D:(blk + 1) * D], Bt3[:, blk * D:(blk + 1) * D],
                          BxP[:, blk * D:(blk + 1) * D], OP.add, waits=[yv])
            xread_tok[par] = yv

            ol = [yv]
            for blk in range(NBLK):
                ol = [layer_norm(Bt3[:, blk * D:(blk + 1) * D],
                                 Bt1[:, blk * D:(blk + 1) * D], ol)]
            ol2 = v_cp(Bmb[:, :], Bt1[:, :], ol)
            XFFM = fm_transpose(ol2)

            cp_last = XFFM
            zms = [XFFM, XFFM]
            ges = {}
            lcts = {}

            def emit_z(f):
                if f % 4 == 0:
                    gs = (f // 4) % 2
                    lcts[gs] = DMA("gpsimd",
                                   Wct[:, gs * 512:(gs + 1) * 512].rearrange(
                                       "p (t w) -> p t w", w=128),
                                   sc["ct"][f:f + 4, :, :].rearrange("t p w -> p t w"),
                                   [zms[gs]], sem="T%d" % gs)
                gs = (f // 4) % 2
                pz = psZ[f % 2]
                cq = (f // 4) % 8
                gprev = ges.get(f - 2)
                zm = t_mm(pz[:, 0:MT],
                          Wct[:, gs * 512 + (f % 4) * 128:gs * 512 + (f % 4 + 1) * 128],
                          Bfm[:, cq * MT:(cq + 1) * MT], True, True,
                          waits=[XFFM, lcts[gs]] + ([gprev] if gprev else []))
                if f % 4 == 3:
                    zms[gs] = zm
                ges[f] = s_act(Bha[:, (f % 2) * MT:(f % 2 + 1) * MT], pz[:, 0:MT],
                               AF.Gelu, waits=[zm])

            emit_z(0)
            for f in range(128):
                wpar = f % 4
                lw = DMA("sync", Wcp[:, wpar * 1024:(wpar + 1) * 1024],
                         sc["wc"][f, :, :], slot_hist["cp"][wpar], sem="C%d" % wpar)
                if f + 1 < 128:
                    emit_z(f + 1)
                pp2 = f % 2
                for blk in range(NBLK):
                    pa, pb = (psA, psB)[blk]
                    t_mm(pa[:, 0:512],
                         Bha[:, pp2 * MT + blk * 128:pp2 * MT + (blk + 1) * 128],
                         Wcp[:, wpar * 1024:wpar * 1024 + 512], f == 0, f == 127,
                         waits=[ges[f], lw])
                    cp_last = t_mm(pb[:, 0:512],
                                   Bha[:, pp2 * MT + blk * 128:pp2 * MT + (blk + 1) * 128],
                                   Wcp[:, wpar * 1024 + 512:(wpar + 1) * 1024],
                                   f == 0, f == 127)
                slot_hist["cp"][wpar] = [cp_last]
                if f % 2 == 0:
                    drain(fg)
            drain(fg, 1000)

            fin = cp_last
            for blk in range(NBLK):
                pa, pb = (psA, psB)[blk]
                fin = v_tt(Bt2[:, blk * D:blk * D + 512], pa[:, 0:512],
                           Bt1[:, blk * D:blk * D + 512], OP.add, waits=[fin])
                fin = v_tt(Bt2[:, blk * D + 512:(blk + 1) * D], pb[:, 0:512],
                           Bt1[:, blk * D + 512:(blk + 1) * D], OP.add, waits=[fin])
            fl = [fin]
            for blk in range(NBLK):
                fl = [layer_norm(Bt2[:, blk * D:(blk + 1) * D],
                                 Bo[:, blk * D:(blk + 1) * D], fl)]
            for blk in range(NBLK):
                st = DMA("sync", out_d[b, t0 + blk * 128:t0 + (blk + 1) * 128, :],
                         Bo[:, blk * D:(blk + 1) * D], fl, sem="ST")
            store_tok[0] = st
            fl_tok[0] = fl[0]
            fin_tok[0] = fin
            cosread_tok[0] = mix
            ft = ft_next

        E("sync", lambda e: e.nop(), [P.tok("ST")])

        @block.sync
        def _(e):
            P.replay("sync", e, sems)

        @block.gpsimd
        def _(e):
            P.replay("gpsimd", e, sems)

        @block.vector
        def _(e):
            P.replay("vector", e, sems)

        @block.scalar
        def _(e):
            P.replay("scalar", e, sems)

        @block.tensor
        def _(e):
            P.replay("tensor", e, sems)

    return nc


def _replay(self, eng, e, sems):
    for fn, ws, inc, inc_n in self.stream[eng]:
        for (s, v) in ws:
            e.wait_ge(sems[s], v)
        ins = fn(e)
        if inc is not None:
            if ins is not None:
                ins.then_inc(sems[inc], inc_n)
            else:
                e.sem_inc(sems[inc], inc_n)

Prog.replay = _replay


def kernel(**inputs):
    x = np.asarray(inputs["L0_attn_x"], dtype=np.float32)
    prep = host_prep(inputs)
    nc = build_nc()
    in_maps = []
    for c in range(NCORES):
        mm = dict(prep)
        mm["x"] = np.ascontiguousarray(x[c * BPC:(c + 1) * BPC])
        in_maps.append(mm)
    res = run_bass_kernel_spmd(nc, in_maps, core_ids=list(range(NCORES)))
    out = np.concatenate([np.asarray(r["out"]) for r in res.results], axis=0)
    return out.astype(np.float32)


# version 08 (final)
# speedup vs baseline: 1.9293x; 1.0396x over previous
import math
import numpy as np
import concourse.bass as bass
import concourse.mybir as mybir
from concourse.bass_utils import run_bass_kernel_spmd

F32 = mybir.dt.float32
BF16 = mybir.dt.bfloat16
AF = mybir.ActivationFunctionType
OP = mybir.AluOpType
AX = mybir.AxisListType

D, H, HD, FF, NF = 1024, 16, 64, 4096, 16384
TSEQ, NCORES, BPC = 2048, 8, 2
MT = 256
NBLK = MT // 128
NMT = TSEQ // MT
NCH = D // 128
HMT = 3 + MT
EPS = 1e-5
R1024 = 1.0 / 1024.0


def _split8(v, n):
    out = []
    for _ in range(n):
        c = (np.array(v, np.float32).view(np.uint32) & np.uint32(0xFFFF0000)).view(np.float32)
        out.append(float(c))
        v = v - float(c)
    return out

CW2PI = _split8(2 * math.pi, 6)
_l10 = math.log(10.0)
L10H = float(np.float32(_l10)); L10L = float(np.float32(_l10 - L10H))
_l2 = math.log(2.0)
LN2H = float(np.float32(_l2)); LN2L = float(np.float32(_l2 - LN2H))
PIO2 = float(np.float32(math.pi / 2))


def host_constants():
    c = {}
    t = np.arange(TSEQ, dtype=np.float64)
    idx64 = np.arange(HD, dtype=np.float64) / 63.0
    freqs64 = (np.arange(H, dtype=np.float64)[:, None] * (10.0 ** idx64)[None, :]).reshape(-1)
    ang64 = t[:, None] * freqs64[None, :]
    c["ntab"] = np.asarray(np.rint(ang64 / (2 * np.pi)), dtype=np.float32)
    kd = np.rint(idx64 * np.log2(10.0))
    c["ktab"] = np.ascontiguousarray(np.broadcast_to(kd.astype(np.float32), (128, HD)))
    c["twok"] = np.ascontiguousarray(np.broadcast_to((2.0 ** kd).astype(np.float32), (128, HD)))
    c["tri"] = np.triu(np.ones((128, 128), dtype=np.float32))
    c["shift"] = np.eye(128, k=1, dtype=np.float32)
    c["eye"] = np.eye(128, dtype=np.float32)
    selc = np.zeros((128, 2), dtype=np.float32)
    selc[127, 0] = 1.0
    selc[:, 1] = 1.0
    c["selc"] = selc
    sl = np.zeros((128, 128), dtype=np.float32)
    sl[127, 0] = 1.0
    c["sel_last"] = sl
    return c


def _col8(v):
    return np.ascontiguousarray(np.asarray(v, np.float32).reshape(NCH, 128).T)


def host_prep(inp):
    g = lambda k: np.asarray(inp[k], dtype=np.float32)
    d = {}
    d["qk_wT"] = np.ascontiguousarray(g("L0_attn_qk_w").T)
    d["v_wT"] = np.ascontiguousarray(g("L0_attn_v_w").T)
    d["g_wT"] = np.ascontiguousarray(g("L0_attn_g_w").T)
    d["o_wT"] = np.ascontiguousarray(g("L0_attn_o_w").T)
    for nm, key in (("gate_t4", "L0_attn_gate_w"), ("up_t4", "L0_attn_up_w")):
        wT = g(key).T
        d[nm] = np.ascontiguousarray(wT.reshape(NCH, 128, 32, 128).transpose(2, 0, 1, 3))
    d["down_wT"] = np.ascontiguousarray(g("L0_attn_down_w").T)
    wcT = np.ascontiguousarray(g("L0_ffn_w_cproj").T)
    hh, cc, gg = np.meshgrid(np.arange(4), np.arange(8), np.arange(4), indexing="ij")
    base = ((hh * 4096)[..., None, None]
            + ((cc * 8)[..., None, None] + np.arange(8)[None, None, None, :, None]) * 64
            + (gg * 16)[..., None, None]
            + np.arange(16)[None, None, None, None, :])
    d["wc_Tp"] = np.ascontiguousarray(wcT[base.reshape(-1)])
    mk6 = g("L0_ffn_rail_maker").reshape(4, 8, 8, 16, 16)
    up6 = g("L0_ffn_rail_up_proj").reshape(4, 8, 8, 4, 16, 16)
    mbd = np.zeros((4, 8, 128, 128), dtype=np.float32)
    ubd = np.zeros((4, 8, 4, 128, 128), dtype=np.float32)
    for rho in range(8):
        s = slice(rho * 16, rho * 16 + 16)
        mbd[:, :, s, s] = mk6[:, :, rho].transpose(0, 1, 3, 2)
        ubd[:, :, :, s, s] = up6[:, :, rho].transpose(0, 1, 2, 4, 3)
    d["mk_bd"] = mbd
    d["up_bd"] = np.ascontiguousarray(ubd)
    d["convw_col"] = np.ascontiguousarray(
        g("L0_attn_conv_w")[:, 0, :].reshape(NCH, 128, 4).transpose(1, 0, 2).reshape(128, 32))
    for nm, key in (("ln1w", "L0_attn_ln1_w"), ("ln1b", "L0_attn_ln1_b"),
                    ("convb", "L0_attn_conv_b"), ("ln2w", "L0_attn_ln2_w"),
                    ("ln2b", "L0_attn_ln2_b")):
        d[nm + "_col"] = _col8(g(key))
    for nm, key in (("gb", "L0_attn_g_b"), ("onw", "L0_attn_onorm_w"),
                    ("onb", "L0_attn_onorm_b"), ("gam1", "L0_attn_gamma1"),
                    ("gam2", "L0_attn_gamma2")):
        d[nm + "_row"] = np.ascontiguousarray(g(key).reshape(1, D))
    d["gamma"] = np.ascontiguousarray(g("L0_attn_gamma_attn").reshape(H, 1))
    d.update(host_constants())
    return d



class Prog:
    ENG = ("sync", "gpsimd", "vector", "scalar", "tensor")

    def __init__(self):
        self.stream = {e: [] for e in self.ENG}
        self.count = {}
        self.hw = {e: {} for e in self.ENG}

    def emit(self, eng, fn, waits=(), inc=None, inc_n=1):
        ws = []
        flat = []
        for w in waits:
            if isinstance(w, list):
                flat.extend(w)
            else:
                flat.append(w)
        for w in flat:
            if w is None:
                continue
            s, v = w
            if v > 0 and self.hw[eng].get(s, 0) < v:
                self.hw[eng][s] = v
                ws.append((s, v))
        if inc is not None:
            self.count[inc] = self.count.get(inc, 0) + inc_n
        self.stream[eng].append((fn, tuple(ws), inc, inc_n))
        return (inc, self.count[inc]) if inc is not None else None

    def tok(self, sem):
        return (sem, self.count.get(sem, 0))


def build_nc(nmt_per_seq=NMT, nseq=BPC):
    nc = bass.Bass("TRN2", target_bir_lowering=False)
    P = Prog()

    dr = {}
    def din(name, shape, dt=F32):
        dr[name] = nc.dram_tensor(name, list(shape), dt, kind="ExternalInput").ap()

    din("x", (BPC, TSEQ, D))
    for nm in ("qk_wT", "v_wT", "g_wT", "o_wT"):
        din(nm, (D, D))
    din("gate_t4", (32, NCH, 128, 128)); din("up_t4", (32, NCH, 128, 128))
    din("down_wT", (FF, D)); din("wc_Tp", (NF, D))
    din("mk_bd", (4, 8, 128, 128)); din("up_bd", (4, 8, 4, 128, 128))
    din("convw_col", (128, 32))
    for nm in ("ln1w_col", "ln1b_col", "convb_col", "ln2w_col", "ln2b_col"):
        din(nm, (128, NCH))
    for nm in ("gb_row", "onw_row", "onb_row", "gam1_row", "gam2_row"):
        din(nm, (1, D))
    din("gamma", (H, 1)); din("ntab", (TSEQ, D))
    din("ktab", (128, HD)); din("twok", (128, HD))
    din("tri", (128, 128)); din("shift", (128, 128)); din("eye", (128, 128))
    din("selc", (128, 2)); din("sel_last", (128, 128))
    out_d = nc.dram_tensor("out", [BPC, TSEQ, D], F32, kind="ExternalOutput").ap()

    sc = {
        "qk": nc.dram_tensor("s_qk", [NCH, 128, D], BF16).ap(),
        "v": nc.dram_tensor("s_v", [NCH, 128, D], BF16).ap(),
        "g": nc.dram_tensor("s_g", [NCH, 128, D], BF16).ap(),
        "o": nc.dram_tensor("s_o", [NCH, 128, D], BF16).ap(),
        "gate": nc.dram_tensor("s_gate", [32, 128, D], BF16).ap(),
        "up": nc.dram_tensor("s_up", [32, 128, D], BF16).ap(),
        "down": nc.dram_tensor("s_down", [32, 128, D], BF16).ap(),
        "wc": nc.dram_tensor("s_wc", [128, 128, D], BF16).ap(),
        "ct": nc.dram_tensor("s_ct", [128, 128, 128], BF16).ap(),
        "cos": nc.dram_tensor("s_cos", [TSEQ, D], F32).ap(),
    }

    Wct = nc.alloc_sbuf_tensor("Wct", [128, 2 * 4 * 128], BF16)
    Bx2 = nc.alloc_sbuf_tensor("Bx2", [128, 2048], F32)
    dec = nc.alloc_sbuf_tensor("dec", [128, 16 * H], F32)
    rdec = nc.alloc_sbuf_tensor("rdec", [128, 16 * H], F32)
    sob = nc.alloc_sbuf_tensor("sob", [128, D], F32)
    bob = nc.alloc_sbuf_tensor("bob", [128, D], F32)
    g2b = nc.alloc_sbuf_tensor("g2b", [128, D], F32)
    cols = nc.alloc_sbuf_tensor("cols", [128, 48], F32)
    cwcol = nc.alloc_sbuf_tensor("cwcol", [128, 32], F32)
    bgcol = nc.alloc_sbuf_tensor("bgcol", [128, 32], F32)
    bucol = nc.alloc_sbuf_tensor("bucol", [128, 32], F32)
    mmrow = nc.alloc_sbuf_tensor("mmrow", [1, 2304], F32)
    gbhl = nc.alloc_sbuf_tensor("gbhl", [2, D], BF16)
    hl = nc.alloc_sbuf_tensor("hl", [2, D], BF16)
    selc_s = nc.alloc_sbuf_tensor("selc_s", [128, 2], F32)
    sel_last_s = nc.alloc_sbuf_tensor("sel_last_s", [128, 128], F32)
    tri_s = nc.alloc_sbuf_tensor("tri_s", [128, 128], F32)
    shf_s = nc.alloc_sbuf_tensor("shf_s", [128, 128], F32)
    eye_s = nc.alloc_sbuf_tensor("eye_s", [128, 128], F32)
    eyeb = nc.alloc_sbuf_tensor("eyeb", [128, 128], BF16)
    ones2b = nc.alloc_sbuf_tensor("ones2b", [2, 128], BF16)
    gamsb = nc.alloc_sbuf_tensor("gamsb", [H, 1], F32)
    pcol = nc.alloc_sbuf_tensor("pcol", [H, 1], F32)
    ccol = nc.alloc_sbuf_tensor("ccol", [128, 16], F32)
    stat = nc.alloc_sbuf_tensor("stat", [128, 24], F32)
    kn = nc.alloc_sbuf_tensor("kn", [128, 48], F32)
    Bx = nc.alloc_sbuf_tensor("Bx", [128, NBLK * D], F32)
    Bt1 = nc.alloc_sbuf_tensor("Bt1", [128, NBLK * D], F32)
    Bt2 = nc.alloc_sbuf_tensor("Bt2", [128, NBLK * D], F32)
    Bt3 = nc.alloc_sbuf_tensor("Bt3", [128, NBLK * D], F32)
    Bsq = nc.alloc_sbuf_tensor("Bsq", [128, D], F32)
    Bsq2 = nc.alloc_sbuf_tensor("Bsq2", [128, D], F32)
    Bsq3 = nc.alloc_sbuf_tensor("Bsq3", [128, D], F32)
    Bk = nc.alloc_sbuf_tensor("Bk", [128, NBLK * D], F32)
    Bv = nc.alloc_sbuf_tensor("Bv", [128, NBLK * D], F32)
    Bg = nc.alloc_sbuf_tensor("Bg", [128, NBLK * D], F32)
    Bcos = nc.alloc_sbuf_tensor("Bcos", [128, NBLK * D], F32)
    Bhfm = nc.alloc_sbuf_tensor("Bhfm", [128, NCH * HMT], F32)
    Bxc = nc.alloc_sbuf_tensor("Bxc", [128, NCH * MT], BF16)
    Bmb = nc.alloc_sbuf_tensor("Bmb", [128, NBLK * D], BF16)
    Bfm = nc.alloc_sbuf_tensor("Bfm", [128, NCH * MT], BF16)
    Bgu = nc.alloc_sbuf_tensor("Bgu", [128, 32 * MT], BF16)
    Bha = nc.alloc_sbuf_tensor("Bha", [128, 2 * MT], BF16)
    Bsl = nc.alloc_sbuf_tensor("Bsl", [128, 2 * MT], F32)
    Wq = nc.alloc_sbuf_tensor("Wq", [128, 2 * D], BF16)
    Wgu = nc.alloc_sbuf_tensor("Wgu", [128, 4 * 2048], BF16)
    Wdn = nc.alloc_sbuf_tensor("Wdn", [128, 4 * 1024], BF16)
    Wcp = nc.alloc_sbuf_tensor("Wcp", [128, 4 * 1024], BF16)
    ps1 = nc.alloc_psum_tensor("ps1", [128, 512], F32)
    ps2 = nc.alloc_psum_tensor("ps2", [128, 512], F32)
    ps3 = nc.alloc_psum_tensor("ps3", [128, 512], F32)
    ps4 = nc.alloc_psum_tensor("ps4", [128, 512], F32)
    ps5 = nc.alloc_psum_tensor("ps5", [128, 512], F32)
    ps6 = nc.alloc_psum_tensor("ps6", [128, 512], F32)
    psT = nc.alloc_psum_tensor("psT", [128, 2048], BF16)
    SEM_NAMES = (["LD", "WT", "TN", "VC", "SC", "GP", "ST"]
                 + ["Q%d" % i for i in range(2)]
                 + ["G%d" % i for i in range(4)]
                 + ["D%d" % i for i in range(4)]
                 + ["C%d" % i for i in range(4)]
                 + ["X%d" % i for i in range(2)]
                 + ["O%d" % i for i in range(2)]
                 + ["V%d" % i for i in range(4)]
                 + ["U%d" % i for i in range(4)]
                 + ["T%d" % i for i in range(2)])
    sem_h = {n: nc.alloc_semaphore(n) for n in SEM_NAMES}
    with nc.Block() as block:
        sems = sem_h
        psA = (ps1, ps2)
        psB = (ps3, ps4)
        psZ = (ps5, ps6)

        carry = mmrow[0:1, 0:1024]
        ktail = mmrow[0:1, 1024:2048]
        ones1 = mmrow[0:1, 2048:2176]
        e0r = mmrow[0:1, 2176:2304]
        ROW_GB = Bsq[0:1, 0:1024]
        ROW_ONW = Bcos[0:1, 0:1024]
        ROW_ONB = Bcos[0:1, 1024:2048]
        ROW_G2 = Bhfm[0:1, 0:1024]
        rowtmp = Bhfm[0:1, 1024:2048]
        HI32 = Bk[0:1, 0:1024]
        LO32 = Bk[0:1, 1024:2048]
        HL16 = Bxc[0:1, 0:2048]
        ones1c = selc_s[:, 1:2]

        def betaG_ap(mi, a, bnd):
            t_ = Bv if mi < 16 else Bg
            mo = mi % 16
            return t_[0:1, mo * 128 + a:mo * 128 + bnd]

        def betaU_ap(mi, a, bnd):
            t_ = Bk if mi < 16 else Bt3
            mo = mi % 16
            return t_[0:1, mo * 128 + a:mo * 128 + bnd]
        hl32 = Bt2
        dech = Bx
        Wc32 = Bt1
        Wc16 = Bmb
        Bnt = Bv
        Bfrq = Bg
        Bex = Bk[:, 0:HD]
        Bex2 = Bk[:, HD:2 * HD]
        Bex3 = Bk[:, 2 * HD:3 * HD]
        Bo = Bt3

        def E(eng, fn, waits=(), inc=None, inc_n=1):
            return P.emit(eng, fn, waits, inc, inc_n)

        def DMA(eng, dst, src, waits=(), sem="LD"):
            def fn(e, d_=dst, s_=src):
                return e.dma_start(out=d_, in_=s_)
            return E(eng, fn, waits, inc=sem, inc_n=16)

        def v_tt(out, a, b, op, waits=(), inc="VC"):
            return E("vector", lambda e, o=out, x=a, y=b, p=op:
                     e.tensor_tensor(out=o, in0=x, in1=y, op=p), waits, inc)

        def v_ts(out, a, s1, s2, op0, op1=None, waits=(), inc="VC"):
            if op1 is None:
                return E("vector", lambda e, o=out, x=a, u=s1, p=op0:
                         e.tensor_scalar(out=o, in0=x, scalar1=u, scalar2=None, op0=p),
                         waits, inc)
            return E("vector", lambda e, o=out, x=a, u=s1, w=s2, p=op0, q=op1:
                     e.tensor_scalar(out=o, in0=x, scalar1=u, scalar2=w, op0=p, op1=q),
                     waits, inc)

        def v_stt(out, a, s1, b, op0, op1, waits=(), inc="VC"):
            return E("vector", lambda e, o=out, x=a, u=s1, y=b, p=op0, q=op1:
                     e.scalar_tensor_tensor(out=o, in0=x, scalar=u, in1=y, op0=p, op1=q),
                     waits, inc)

        def v_cp(out, a, waits=(), inc="VC"):
            return E("vector", lambda e, o=out, x=a: e.tensor_copy(out=o, in_=x), waits, inc)

        def v_ms(out, c, waits=(), inc="VC"):
            return E("vector", lambda e, o=out, k=c: e.memset(o, k), waits, inc)

        def g_tt(out, a, b, op, waits=(), inc="GP"):
            return E("gpsimd", lambda e, o=out, x=a, y=b, p=op:
                     e.tensor_tensor(out=o, in0=x, in1=y, op=p), waits, inc)

        def g_cp(out, a, waits=(), inc="GP"):
            return E("gpsimd", lambda e, o=out, x=a: e.tensor_copy(out=o, in_=x),
                     waits, inc)

        def g_ts(out, a, s1, op0, waits=(), inc="GP"):
            return E("gpsimd", lambda e, o=out, x=a, u=s1, p=op0:
                     e.tensor_scalar(out=o, in0=x, scalar1=u, scalar2=None, op0=p),
                     waits, inc)

        def g_ts2(out, a, s1, s2, op0, op1, waits=(), inc="GP"):
            return E("gpsimd", lambda e, o=out, x=a, u=s1, w=s2, p=op0, q=op1:
                     e.tensor_scalar(out=o, in0=x, scalar1=u, scalar2=w, op0=p, op1=q),
                     waits, inc)

        def g_red(out, a, waits=(), inc="GP"):
            return E("gpsimd", lambda e, o=out, x=a:
                     e.tensor_reduce(out=o, in_=x, axis=AX.X, op=OP.add), waits, inc)

        def g_sq_accum(out, a, acc, waits=(), inc="GP"):
            return E("gpsimd", lambda e, o=out, x=a, q=acc:
                     e.scalar_tensor_tensor(out=o, in0=x, scalar=1.0, in1=x,
                                            op0=OP.mult, op1=OP.mult, accum_out=q),
                     waits, inc)

        def v_sq_accum(out, a, acc, waits=(), inc="VC"):
            return E("vector", lambda e, o=out, x=a, q=acc:
                     e.scalar_tensor_tensor(out=o, in0=x, scalar=1.0, in1=x,
                                            op0=OP.mult, op1=OP.mult, accum_out=q),
                     waits, inc)

        def v_red(out, a, waits=(), inc="VC"):
            return E("vector", lambda e, o=out, x=a:
                     e.tensor_reduce(out=o, in_=x, axis=AX.X, op=OP.add), waits, inc)

        def s_act(out, a, func, bias=0.0, scale=1.0, waits=(), inc="SC"):
            return E("scalar", lambda e, o=out, x=a, f=func, b=bias, s=scale:
                     e.activation(out=o, in_=x, func=f, bias=b, scale=s), waits, inc)

        def s_cp(out, a, waits=(), inc="SC"):
            return E("scalar", lambda e, o=out, x=a: e.copy(out=o, in_=x), waits, inc)

        def t_mm(psum, lhsT, rhs, start, stop, waits=(), inc="TN"):
            return E("tensor", lambda e, o=psum, l=lhsT, r=rhs, a=start, b=stop:
                     e.matmul(o, l, r, start=a, stop=b), waits, inc)

        def t_tr(psum, src, ident, waits=(), inc="TN"):
            return E("tensor", lambda e, o=psum, x=src, i=ident:
                     e.transpose(o, x, i), waits, inc)

        def recip_pos(dst, src, tmp, waits):
            t1 = s_act(tmp, src, AF.Ln, waits=waits)
            t2 = s_act(dst, tmp, AF.Exp, scale=-1.0, waits=[t1])
            t3 = v_tt(tmp, src, dst, OP.mult, waits=[t2])
            t4 = v_ts(tmp, tmp, -1.0, 2.0, OP.mult, OP.add, waits=[t3])
            return v_tt(dst, dst, tmp, OP.mult, waits=[t4])

        def rsqrt_pos(dst, src, tmp, waits):
            t1 = s_act(tmp, src, AF.Ln, waits=waits)
            t2 = s_act(dst, tmp, AF.Exp, scale=-0.5, waits=[t1])
            t3 = v_tt(tmp, dst, dst, OP.mult, waits=[t2])
            t4 = v_tt(tmp, tmp, src, OP.mult, waits=[t3])
            t5 = v_ts(tmp, tmp, -0.5, 1.5, OP.mult, OP.add, waits=[t4])
            return v_tt(dst, dst, tmp, OP.mult, waits=[t5])

        for dst, src in (
            (tri_s[:, :], dr["tri"][:, :]), (shf_s[:, :], dr["shift"][:, :]),
            (eye_s[:, :], dr["eye"][:, :]),
            (cols[:, 0:8], dr["ln1w_col"][:, :]), (cols[:, 8:16], dr["ln1b_col"][:, :]),
            (cols[:, 16:24], dr["convb_col"][:, :]), (cols[:, 24:32], dr["ln2w_col"][:, :]),
            (cols[:, 32:40], dr["ln2b_col"][:, :]),
            (ROW_GB, dr["gb_row"][:, :]), (ROW_ONW, dr["onw_row"][:, :]),
            (ROW_ONB, dr["onb_row"][:, :]), (ROW_G2, dr["gam2_row"][:, :]),
            (rowtmp, dr["gam1_row"][:, :]),
            (selc_s[:, :], dr["selc"][:, :]), (sel_last_s[:, :], dr["sel_last"][:, :]),
            (gamsb[:, :], dr["gamma"][:, :]),
            (cwcol[:, :], dr["convw_col"][:, :]),
            (Bex2, dr["ktab"][:, :]), (Bex3, dr["twok"][:, :]),
        ):
            DMA("sync", dst, src)
        W0 = [P.tok("LD")]

        ch = v_ms(ones1, 1.0, W0)
        ch = v_ms(e0r, 0.0, [ch])
        ch = v_ms(mmrow[0:1, 2176:2177], 1.0, [ch])
        ch = v_ms(ones2b[:, :], 1.0, [ch])
        ch = v_ms(carry, 0.0, [ch])
        ch = v_ms(ktail, 0.0, [ch])
        ch = v_cp(eyeb[:, :], eye_s[:, :], [ch])

        ch = v_ms(dech[0:H, 0:1], 1.0, [ch])
        ch = v_cp(pcol[:, :], gamsb[:, :], [ch])
        n = 1
        while n < TSEQ:
            nn = min(n, TSEQ - n)
            ch = v_ts(dech[0:H, n:n + nn], dech[0:H, 0:nn], pcol[0:H, 0:1], None,
                      OP.mult, waits=[ch])
            n += nn
            if n < TSEQ:
                ch = v_tt(pcol[:, :], pcol[:, :], pcol[:, :], OP.mult, waits=[ch])
        for tb in range(16):
            tt = t_tr(ps1[:, 0:H], dech[0:H, tb * 128:(tb + 1) * 128],
                      eye_s[0:H, 0:H], [ch])
            ch = s_cp(dec[:, tb * H:(tb + 1) * H], ps1[:, 0:H], [tt])
        ch = v_ts(rdec[:, :], dec[:, :], 1e-6, None, OP.add, waits=[ch])
        ch = recip_pos(Bt1[:, 0:16 * H], rdec[:, :], Bt2[:, 0:16 * H], [ch])
        ch = v_cp(rdec[:, :], Bt1[:, 0:16 * H], [ch])

        ch = E("gpsimd", lambda e: e.iota(Bex, [[1, HD]], channel_multiplier=0,
               allow_small_or_imprecise_dtypes=True), [ch], inc="GP")
        ch = v_ms(stat[:, 0:1], 63.0, [ch])
        ch = recip_pos(stat[:, 1:2], stat[:, 0:1], stat[:, 2:3], [ch])
        ch = v_ts(Bex, Bex, stat[:, 1:2], None, OP.mult, waits=[ch])
        ch = v_ts(Bfrq[:, 0:HD], Bex, L10H, None, OP.mult, waits=[ch])
        ch = v_stt(Bfrq[:, 0:HD], Bex2, -LN2H, Bfrq[:, 0:HD], OP.mult, OP.add, [ch])
        ch = v_ts(Bfrq[:, HD:2 * HD], Bex, L10L, None, OP.mult, waits=[ch])
        ch = v_tt(Bfrq[:, 0:HD], Bfrq[:, 0:HD], Bfrq[:, HD:2 * HD], OP.add, waits=[ch])
        ch = v_stt(Bfrq[:, 0:HD], Bex2, -LN2L, Bfrq[:, 0:HD], OP.mult, OP.add, [ch])
        ch = v_ms(ccol[:, 0:2], 1.0, [ch])
        for j in range(2, 11):
            ch = v_ms(stat[:, 3:4], float(j), [ch])
            ch = recip_pos(stat[:, 4:5], stat[:, 3:4], stat[:, 5:6], [ch])
            ch = v_ts(ccol[:, j:j + 1], ccol[:, j - 1:j], stat[:, 4:5], None, OP.mult, waits=[ch])
        ch = v_ts(Bex2, Bfrq[:, 0:HD], 0.0, ccol[:, 10:11], OP.mult, OP.add, [ch])
        for j in range(9, -1, -1):
            ch = v_tt(Bex2, Bex2, Bfrq[:, 0:HD], OP.mult, waits=[ch])
            ch = v_ts(Bex2, Bex2, ccol[:, j:j + 1], None, OP.add, waits=[ch])
        ch = v_tt(Bex2, Bex2, Bex3, OP.mult, waits=[ch])
        for h in range(H - 1, -1, -1):
            ch = v_ts(Bfrq[:, h * HD:(h + 1) * HD], Bex2, float(h), None, OP.mult, waits=[ch])

        for tb in range(16):
            l1 = DMA("sync", Bnt[:, 0:D], dr["ntab"][tb * 128:(tb + 1) * 128, :], [ch])
            ch = E("gpsimd", lambda e, b=tb: e.iota(stat[:, 6:7], [[1, 1]], base=b * 128,
                   channel_multiplier=1, allow_small_or_imprecise_dtypes=True),
                   [ch], inc="GP")
            ch = v_ts(Bt1[:, 0:D], Bfrq[:, 0:D], stat[:, 6:7], None, OP.mult, waits=[ch])
            for ci in CW2PI:
                ch = v_stt(Bt1[:, 0:D], Bnt[:, 0:D], -ci, Bt1[:, 0:D], OP.mult, OP.add,
                           [ch, l1])
            ch = s_act(Bt1[:, 0:D], Bt1[:, 0:D], AF.Abs, waits=[ch])
            ch = v_ts(Bt2[:, 0:D], Bt1[:, 0:D], -1.0, PIO2, OP.mult, OP.add, waits=[ch])
            ch = s_act(Bt2[:, 0:D], Bt2[:, 0:D], AF.Sin, waits=[ch])
            ch = DMA("sync", sc["cos"][tb * 128:(tb + 1) * 128, :], Bt2[:, 0:D], [ch],
                     sem="ST")

        def bcast_row(dst128, src_row, w):
            w = v_cp(Bxc[0:1, 0:1024], src_row, [w])
            w = v_cp(HI32, Bxc[0:1, 0:1024], [w])
            w = v_tt(LO32, src_row, HI32, OP.subtract, waits=[w])
            w = v_cp(Bxc[0:1, 1024:2048], LO32, [w])
            w = DMA("gpsimd", hl[0:2, :], HL16, [w], sem="WT")
            for sl in range(2):
                pp = (ps1, ps2)[sl]
                m1 = t_mm(pp[:, 0:512], ones2b[:, :], hl[:, sl * 512:(sl + 1) * 512],
                          True, True, waits=[w])
                w = s_cp(dst128[:, sl * 512:(sl + 1) * 512], pp[:, 0:512], [m1])
            return w

        ch = v_tt(rowtmp, rowtmp, ROW_ONW, OP.mult, waits=[ch])
        ch = bcast_row(sob, rowtmp, ch)
        ch = DMA("sync", rowtmp, dr["gam1_row"][:, :], [ch])
        ch = v_tt(rowtmp, rowtmp, ROW_ONB, OP.mult, waits=[ch])
        ch = bcast_row(bob, rowtmp, ch)
        ch = bcast_row(g2b, ROW_G2, ch)
        ch = v_cp(Bxc[0:1, 0:1024], ROW_GB, [ch])
        ch = v_cp(HI32, Bxc[0:1, 0:1024], [ch])
        ch = v_tt(LO32, ROW_GB, HI32, OP.subtract, waits=[ch])
        ch = v_cp(Bxc[0:1, 1024:2048], LO32, [ch])
        ch = DMA("gpsimd", gbhl[0:2, :], HL16, [ch], sem="WT")

        conv_specs = []
        for nm in ("qk", "v", "g", "o"):
            for c in range(NCH):
                conv_specs.append(("plain", dr[nm + "_wT"][c * 128:(c + 1) * 128, :],
                                   sc[nm][c, :, :], None))
        for f in range(32):
            conv_specs.append(("g2", dr["down_wT"][f * 128:(f + 1) * 128, :],
                               sc["down"][f, :, :], None))
        for f in range(128):
            conv_specs.append(("plain", dr["wc_Tp"][f * 128:(f + 1) * 128, :],
                               sc["wc"][f, :, :], None))
        for wi, nm in (("gate_t4", "gate"), ("up_t4", "up")):
            for m in range(32):
                conv_specs.append(("gu", dr[wi][m, :, :, :].rearrange("c p w -> p c w"),
                                   sc[nm][m, :, :], (nm, m)))
        slot_tok = {0: [ch], 1: [ch], 2: [ch], 3: [ch]}
        prev_v = ch
        prev_beta = ch
        cv32 = [Wc32[:, 0:D], Wc32[:, D:2 * D], Bt2[:, 0:D], Bt2[:, D:2 * D]]
        cv16 = [Wc16[:, 0:D], Wc16[:, D:2 * D], Bxc[:, 0:D], Bxc[:, D:2 * D]]
        for i, (mode, src, dst, extra) in enumerate(conv_specs):
            par = i % 4
            wdst = cv32[par]
            if mode == "gu":
                wdst = wdst.rearrange("p (c w) -> p c w", w=128)
            q_ = "gpsimd" if par < 2 else "sync"
            lw = DMA(q_, wdst, src, slot_tok[par], sem="V%d" % par)
            if mode == "g2":
                tv = v_tt(cv16[par], cv32[par], g2b[:, :], OP.mult,
                          waits=[lw, prev_v])
            elif mode == "gu":
                tv = prev_v
                for c in range(NCH):
                    tv = v_ts(cv16[par][:, c * 128:(c + 1) * 128],
                              cv32[par][:, c * 128:(c + 1) * 128],
                              cols[:, 24 + c:25 + c], None, OP.mult, waits=[lw, tv])
            else:
                tv = v_cp(cv16[par], cv32[par], [lw, prev_v])
            st = DMA(q_, dst, cv16[par], [tv], sem="U%d" % par)
            new_tok = [st]
            if mode == "gu":
                nm_, m_ = extra
                pm = None
                for c in range(NCH):
                    pm = t_mm(ps3[0:1, 0:128], cols[:, 32 + c:33 + c],
                              cv32[par][:, c * 128:(c + 1) * 128],
                              c == 0, c == 7,
                              waits=[lw, prev_beta] if c == 0 else [])
                bfn = betaG_ap if nm_ == "gate" else betaU_ap
                be = s_cp(bfn(m_, 0, 128), ps3[0:1, 0:128], [pm])
                prev_beta = be
                new_tok = [st, be]
            slot_tok[par] = new_tok
            prev_v = tv
        CONV_END = slot_tok[0] + slot_tok[1] + slot_tok[2] + slot_tok[3] + [prev_beta]

        ch2 = prev_beta
        for m in range(32):
            tt = t_tr(ps4[:, 0:1], betaG_ap(m, 0, 128), eye_s[0:1, 0:1], [ch2])
            ch2 = s_cp(bgcol[:, m:m + 1], ps4[:, 0:1], [tt])
        for m in range(32):
            tt = t_tr(ps4[:, 0:1], betaU_ap(m, 0, 128), eye_s[0:1, 0:1], [ch2])
            ch2 = s_cp(bucol[:, m:m + 1], ps4[:, 0:1], [tt])

        ctok = [ch2] + CONV_END
        psC = [ps3, ps4, ps5, ps6]
        mk_tok = [ctok, ctok]
        up_tok = [list(ctok)] * 4
        ev_tok = [list(ctok)] * 4
        st_tok = [list(ctok)] * 4
        mm_tok = {}
        for hq in range(4):
            for cq in range(8):
                hc = hq * 8 + cq
                mp = hc % 2
                l1 = DMA("gpsimd", Wc32[:, mp * 128:(mp + 1) * 128],
                         dr["mk_bd"][hq, cq, :, :], mk_tok[mp], sem="V%d" % mp)
                t1 = v_cp(Wc16[:, 1536 + mp * 128:1536 + (mp + 1) * 128],
                          Wc32[:, mp * 128:(mp + 1) * 128], [l1])
                last_mm = None
                for gq in range(4):
                    ti = hc * 4 + gq
                    sp = ti % 4
                    l2 = DMA("sync", Wc32[:, 256 + sp * 128:256 + (sp + 1) * 128],
                             dr["up_bd"][hq, cq, gq, :, :], up_tok[sp],
                             sem="V%d" % (2 + sp % 2))
                    t2 = v_cp(Wc16[:, 256 + sp * 128:256 + (sp + 1) * 128],
                              Wc32[:, 256 + sp * 128:256 + (sp + 1) * 128], [l2])
                    mm = t_mm(psC[sp][:, 0:128],
                              Wc16[:, 1536 + mp * 128:1536 + (mp + 1) * 128],
                              Wc16[:, 256 + sp * 128:256 + (sp + 1) * 128],
                              True, True, waits=[t1, t2] + ev_tok[sp])
                    up_tok[sp] = [mm]
                    ev = s_cp(Wc16[:, 768 + sp * 128:768 + (sp + 1) * 128],
                              psC[sp][:, 0:128], [mm] + st_tok[sp])
                    ev_tok[sp] = [ev]
                    stt_ = DMA("gpsimd", sc["ct"][ti, :, :],
                               Wc16[:, 768 + sp * 128:768 + (sp + 1) * 128],
                               [ev], sem="U%d" % (sp % 2))
                    st_tok[sp] = [stt_]
                    last_mm = mm
                mk_tok[mp] = [last_mm]
        last_c = ev_tok[3][0]
        CT_ST = [P.tok("U0"), P.tok("U1")]
        PRO_DONE = [last_c, P.tok("WT"), P.tok("ST"), P.tok("VC"), P.tok("SC"),
                    P.tok("TN"), P.tok("GP"), P.tok("V0"), P.tok("V1"),
                    P.tok("V2"), P.tok("V3"), P.tok("U0"), P.tok("U1"),
                    P.tok("U2"), P.tok("U3"), P.tok("LD")]

        def layer_norm(src, dst, w):
            w1 = v_tt(Bsq[:, 0:D], src, src, OP.mult, waits=w)
            w2 = v_red(stat[:, 0:1], src, waits=w)
            w3 = v_red(stat[:, 1:2], Bsq[:, 0:D], waits=[w1])
            w4 = v_ts(stat[:, 2:3], stat[:, 0:1], R1024, None, OP.mult, waits=[w2])
            w5 = v_ts(stat[:, 3:4], stat[:, 1:2], R1024, None, OP.mult, waits=[w3])
            w6 = v_tt(stat[:, 4:5], stat[:, 2:3], stat[:, 2:3], OP.mult, waits=[w4])
            w7 = v_tt(stat[:, 5:6], stat[:, 3:4], stat[:, 4:5], OP.subtract,
                      waits=[w5, w6])
            w8 = v_ts(stat[:, 5:6], stat[:, 5:6], EPS, None, OP.add, waits=[w7])
            w9 = rsqrt_pos(stat[:, 6:7], stat[:, 5:6], stat[:, 7:8], [w8])
            return v_ts(dst, src, stat[:, 2:3], stat[:, 6:7], OP.subtract, OP.mult,
                        waits=[w9])

        slot_hist = {"q": [list(PRO_DONE)] * 2, "gu": [list(PRO_DONE)] * 4,
                     "dn": [list(PRO_DONE)] * 4, "cp": [list(PRO_DONE)] * 4}
        xread_tok = [PRO_DONE[0], PRO_DONE[0]]
        cosread_tok = [PRO_DONE[0]]
        psT_evac = [PRO_DONE[0], PRO_DONE[0]]
        store_tok = [PRO_DONE[0]]
        fl_tok = [PRO_DONE[0]]
        fin_tok = [PRO_DONE[0]]
        hv = Bhfm[:, :].rearrange("p (c w) -> p c w", w=HMT)
        xcv = Bxc[:, :].rearrange("p (c w) -> p c w", w=MT)
        cacc = Bg[:, 0:NCH * MT].rearrange("p (c w) -> p c w", w=MT)

        def layer_norm(src_, dst, w, co=0, sq=None):
            sqb = Bsq[:, 0:D] if sq is None else sq
            w2 = v_red(stat[:, co + 0:co + 1], src_, waits=w)
            w3 = v_sq_accum(sqb, src_, stat[:, co + 1:co + 2], waits=[w2])
            w4 = v_ts(stat[:, co + 2:co + 3], stat[:, co + 0:co + 1], R1024, None,
                      OP.mult, waits=[w2])
            w5 = v_ts(stat[:, co + 3:co + 4], stat[:, co + 1:co + 2], R1024, None,
                      OP.mult, waits=[w3])
            w6 = v_tt(stat[:, co + 4:co + 5], stat[:, co + 2:co + 3],
                      stat[:, co + 2:co + 3], OP.mult, waits=[w4])
            w7 = v_tt(stat[:, co + 5:co + 6], stat[:, co + 3:co + 4],
                      stat[:, co + 4:co + 5], OP.subtract, waits=[w5, w6])
            w8 = v_ts(stat[:, co + 5:co + 6], stat[:, co + 5:co + 6], EPS, None,
                      OP.add, waits=[w7])
            w9 = rsqrt_pos(stat[:, co + 6:co + 7], stat[:, co + 5:co + 6],
                           stat[:, co + 7:co + 8], [w8])
            return v_ts(dst, src_, stat[:, co + 2:co + 3], stat[:, co + 6:co + 7],
                        OP.subtract, OP.mult, waits=[w9])

        def ln_passes(src_, w, co):
            sqb = (Bsq if co == 0 else Bsq3)[:, 0:D]
            w2 = v_red(stat[:, co + 0:co + 1], src_, waits=w)
            w3 = v_sq_accum(sqb, src_, stat[:, co + 1:co + 2], waits=[w2])
            return w2, w3

        def ln_finish(src_, dst, co, w2, w3):
            w4 = g_ts(stat[:, co + 2:co + 3], stat[:, co + 0:co + 1], R1024,
                      OP.mult, waits=[w2])
            w5 = g_ts(stat[:, co + 3:co + 4], stat[:, co + 1:co + 2], R1024,
                      OP.mult, waits=[w3])
            w6 = g_tt(stat[:, co + 4:co + 5], stat[:, co + 2:co + 3],
                      stat[:, co + 2:co + 3], OP.mult, waits=[w4])
            w7 = g_tt(stat[:, co + 5:co + 6], stat[:, co + 3:co + 4],
                      stat[:, co + 4:co + 5], OP.subtract, waits=[w5, w6])
            w8 = g_ts(stat[:, co + 5:co + 6], stat[:, co + 5:co + 6], EPS,
                      OP.add, waits=[w7])
            t1 = s_act(stat[:, co + 7:co + 8], stat[:, co + 5:co + 6], AF.Ln,
                       waits=[w8])
            t2 = s_act(stat[:, co + 6:co + 7], stat[:, co + 7:co + 8], AF.Exp,
                       scale=-0.5, waits=[t1])
            t3 = g_tt(stat[:, co + 7:co + 8], stat[:, co + 6:co + 7],
                      stat[:, co + 6:co + 7], OP.mult, waits=[t2])
            t4 = g_tt(stat[:, co + 7:co + 8], stat[:, co + 7:co + 8],
                      stat[:, co + 5:co + 6], OP.mult, waits=[t3])
            t5 = g_ts2(stat[:, co + 7:co + 8], stat[:, co + 7:co + 8], -0.5, 1.5,
                       OP.mult, OP.add, waits=[t4])
            w9 = g_tt(stat[:, co + 6:co + 7], stat[:, co + 6:co + 7],
                      stat[:, co + 7:co + 8], OP.mult, waits=[t5])
            return v_ts(dst, src_, stat[:, co + 2:co + 3], stat[:, co + 6:co + 7],
                        OP.subtract, OP.mult, waits=[w9])

        LN_HYBRID = {"on": True, "l2": True, "o1": True, "o2": True}

        def ln_pair(srcs, dsts, w, wg=None, site="on"):
            if not LN_HYBRID[site]:
                a = layer_norm(srcs[0], dsts[0], w)
                bq = layer_norm(srcs[1], dsts[1],
                                [a] + (wg if wg is not None else []))
                return [a, bq]
            a2, a3 = ln_passes(srcs[0], w, 0)
            b2, b3 = ln_passes(srcs[1], w if wg is None else wg, 16)
            a = ln_finish(srcs[0], dsts[0], 0, a2, a3)
            bq = ln_finish(srcs[1], dsts[1], 16, b2, b3)
            return [a, bq]

        def front_gen(b, m, prev, pub):
            gi = b * nmt_per_seq + m
            par = gi % 2
            BxP = (Bx, Bx2)[par]
            pub["xpar"] = par
            lxx, lcc = [], []
            for blk in range(NBLK):
                lxx.append(DMA("sync", BxP[:, blk * D:(blk + 1) * D],
                               dr["x"][b, m * MT + blk * 128:m * MT + (blk + 1) * 128, :],
                               [xread_tok[par]], sem="X%d" % blk))
                lcc.append(DMA("sync", Bcos[:, blk * D:(blk + 1) * D],
                               sc["cos"][m * MT + blk * 128:m * MT + (blk + 1) * 128, :],
                               [cosread_tok[0], prev["mix"]], sem="O%d" % blk))
            pub["lcos"] = lcc
            yield
            if m == 0:
                z0 = v_ms(carry, 0.0, [prev["mix"]])
                z0 = v_ms(ktail, 0.0, [z0])
                hcp = v_ms(hv[:, :, 0:3], 0.0, [z0, prev["XC"]])
            else:
                hcp = v_cp(hv[:, :, 0:3], hv[:, :, MT:MT + 3], [prev["XC"]])
            pub["hz"] = hcp
            yield
            ln1 = []
            for blk in range(NBLK):
                t = layer_norm(BxP[:, blk * D:(blk + 1) * D],
                               Bv[:, blk * D:(blk + 1) * D],
                               [lxx[blk], prev["MIXFM"]] + ([ln1[-1]] if ln1 else []),
                               co=8, sq=Bg[:, 0:D])
                ln1.append(t)
                yield
            ev = hcp
            for blk in range(NBLK):
                w = v_cp(Bxc[:, 0:1024], Bv[:, blk * D:(blk + 1) * D],
                         [ln1[blk], prev["mix"]])
                w = v_cp(Bg[:, D:2 * D], Bxc[:, 0:1024], [w])
                w = v_tt(Bg[:, 0:D], Bv[:, blk * D:(blk + 1) * D], Bg[:, D:2 * D],
                         OP.subtract, waits=[w])
                w = v_cp(Bxc[:, 1024:2048], Bg[:, 0:D], [w])
                yield
                for c in range(NCH):
                    t1 = t_tr(psT[:, 0:128], Bxc[:, c * 128:(c + 1) * 128],
                              eyeb[:, :], waits=[w, psT_evac[0]])
                    t2 = t_tr(psT[:, 1024:1152], Bxc[:, 1024 + c * 128:1024 + (c + 1) * 128],
                              eyeb[:, :], waits=[w, psT_evac[1]])
                    hs = Bhfm[:, c * HMT + 3 + blk * 128:c * HMT + 3 + (blk + 1) * 128]
                    e1 = s_cp(hs, psT[:, 0:128], [t1, ev])
                    ev = v_tt(hs, hs, psT[:, 1024:1152], OP.add, waits=[t2, e1])
                    psT_evac[0] = e1
                    psT_evac[1] = ev
                    if c % 2 == 1:
                        yield
            cv = ev
            for c in range(NCH):
                hs = Bhfm[:, c * HMT + 3:c * HMT + 3 + MT]
                cv = v_ts(hs, hs, cols[:, c:c + 1], cols[:, 8 + c:9 + c],
                          OP.mult, OP.add, waits=[cv])
                if c % 4 == 3:
                    yield
            for c in range(NCH):
                cv = v_ts(Bg[:, c * MT:(c + 1) * MT], Bhfm[:, c * HMT:c * HMT + MT],
                          cwcol[:, c * 4:c * 4 + 1], cols[:, 16 + c:17 + c],
                          OP.mult, OP.add, waits=[cv])
                if c % 4 == 3:
                    yield
            for k in range(1, 4):
                for c in range(NCH):
                    dst = (Bxc[:, c * MT:(c + 1) * MT] if k == 3
                           else Bg[:, c * MT:(c + 1) * MT])
                    cv = v_stt(dst, Bhfm[:, c * HMT + k:c * HMT + k + MT],
                               cwcol[:, c * 4 + k:c * 4 + k + 1],
                               Bg[:, c * MT:(c + 1) * MT], OP.mult, OP.add,
                               waits=[cv])
                    if c % 4 == 3:
                        yield
            pub["XC"] = cv
            yield
            st_q = {"ev": prev["mix"]}

            def qkg_proj(nm, dstb):
                for blk in range(NBLK):
                    last_mm_q = None
                    for c in range(NCH):
                        wpar = c % 2
                        lwq = DMA("gpsimd", Wq[:, wpar * D:(wpar + 1) * D],
                                  sc[nm][c, :, :], slot_hist["q"][wpar],
                                  sem="Q%d" % wpar)
                        w_ = ([lwq, cv, st_q["ev"], prev["H2FM"]] if c == 0 else [lwq])
                        t_mm(ps5[:, 0:512],
                             Bxc[:, c * MT + blk * 128:c * MT + (blk + 1) * 128],
                             Wq[:, wpar * D:wpar * D + 512], c == 0,
                             (c == 7) and nm != "g", waits=w_)
                        last_mm_q = t_mm(ps6[:, 0:512],
                                         Bxc[:, c * MT + blk * 128:c * MT + (blk + 1) * 128],
                                         Wq[:, wpar * D + 512:(wpar + 1) * D], c == 0,
                                         (c == 7) and nm != "g")
                        slot_hist["q"][wpar] = [last_mm_q]
                        if c % 2 == 1:
                            yield
                    if nm == "g":
                        t_mm(ps5[:, 0:512], ones2b[:, :], gbhl[:, 0:512], False, True)
                        last_mm_q = t_mm(ps6[:, 0:512], ones2b[:, :],
                                         gbhl[:, 512:1024], False, True)
                        e1 = s_act(dstb[:, blk * D:blk * D + 512], ps5[:, 0:512],
                                   AF.Sigmoid, waits=[last_mm_q])
                        st_q["ev"] = s_act(dstb[:, blk * D + 512:(blk + 1) * D],
                                           ps6[:, 0:512], AF.Sigmoid, waits=[e1])
                    else:
                        e1 = s_cp(dstb[:, blk * D:blk * D + 512], ps5[:, 0:512],
                                  [last_mm_q, st_q["ev"]])
                        st_q["ev"] = s_cp(dstb[:, blk * D + 512:(blk + 1) * D],
                                          ps6[:, 0:512], [e1])
                    yield

            yield from qkg_proj("qk", Bk)
            yield from qkg_proj("v", Bv)
            yield from qkg_proj("g", Bg)
            pub["QKVG"] = st_q["ev"]
            kk = st_q["ev"]
            for blk in range(NBLK):
                kk = g_tt(Bsq2[:, 0:D], Bk[:, blk * D:(blk + 1) * D],
                          Bk[:, blk * D:(blk + 1) * D], OP.mult, waits=[kk])
                kk = v_red(kn[:, 32:48],
                           Bsq2[:, 0:D].rearrange("p (h w) -> p h w", w=HD),
                           waits=[kk])
                yield
                kk = v_ts(kn[:, 32:48], kn[:, 32:48], -1e-24, None, OP.add,
                          waits=[kk])
                kk = s_act(kn[:, 32:48], kn[:, 32:48], AF.Relu, waits=[kk])
                kk = v_ts(kn[:, 32:48], kn[:, 32:48], 1e-24, None, OP.add,
                          waits=[kk])
                kk = rsqrt_pos(kn[:, 0:16], kn[:, 32:48], kn[:, 16:32], [kk])
                yield
                bkv = Bk[:, blk * D:(blk + 1) * D].rearrange("p (h w) -> p h w",
                                                                 w=HD)
                sq2v = Bsq2[:, 0:D].rearrange("p (h w) -> p h w", w=HD)
                kk = v_tt(sq2v[:, :, :], bkv[:, :, :],
                          kn[:, 0:16].unsqueeze(2).broadcast_to([128, H, HD]),
                          OP.mult, waits=[kk])
                kk = s_cp(Bk[:, blk * D:(blk + 1) * D], Bsq2[:, 0:D], [kk])
                yield
            pub["KN"] = kk
            pub["done"] = True

        def drain(fg, n=1):
            if fg is None:
                return
            for _ in range(n):
                try:
                    next(fg)
                except StopIteration:
                    return

        def fm_transpose(w0):
            w0 = w0 if isinstance(w0, list) else [w0]
            ev_ = w0[0]
            for blk in range(NBLK):
                for c in range(NCH):
                    bk = (blk * NCH + c) % 2
                    po = bk * 1024
                    tt = t_tr(psT[:, po:po + 128],
                              Bmb[:, blk * D + c * 128:blk * D + (c + 1) * 128],
                              eyeb[:, :], waits=w0 + [psT_evac[bk]])
                    ev_ = s_cp(Bfm[:, c * MT + blk * 128:c * MT + (blk + 1) * 128],
                               psT[:, po:po + 128], [tt])
                    psT_evac[bk] = ev_
            return ev_

        mts = [(b, m) for b in range(nseq) for m in range(nmt_per_seq)]
        ft = {}
        fg = front_gen(0, 0, {"mix": PRO_DONE[0], "XC": PRO_DONE[0],
                              "MIXFM": PRO_DONE[0], "H2FM": PRO_DONE[0]}, ft)
        drain(fg, 1000)

        for mi_, (b, m) in enumerate(mts):
            t0 = m * MT
            gb0 = t0 // 128
            par = ft["xpar"]
            BxP = (Bx, Bx2)[par]
            XC = ft["XC"]
            lcos = ft["lcos"]

            KN = ft["KN"]
            mix = KN
            mixg = KN
            cu_prev = KN

            for blk in range(NBLK):
                gbi = gb0 + blk
                m1 = t_mm(ps5[:, 0:512], shf_s[:, :], Bk[:, blk * D:blk * D + 512],
                          True, False, waits=[mix] + ([mixg] if blk > 0 else []))
                m2 = t_mm(ps6[:, 0:512], shf_s[:, :],
                          Bk[:, blk * D + 512:(blk + 1) * D], True, False)
                if blk == 0:
                    m1 = t_mm(ps5[:, 0:512], e0r, ktail[0:1, 0:512], False, True)
                    m2 = t_mm(ps6[:, 0:512], e0r, ktail[0:1, 512:1024], False, True)
                else:
                    m1 = t_mm(ps5[:, 0:512], sel_last_s[:, :], Bk[:, 0:512],
                              False, True)
                    m2 = t_mm(ps6[:, 0:512], sel_last_s[:, :], Bk[:, 512:1024],
                              False, True)
                s1 = v_tt(Bt1[:, blk * D:blk * D + 512],
                          Bcos[:, blk * D:blk * D + 512], ps5[:, 0:512], OP.add,
                          waits=[m1, lcos[blk]])
                s1 = v_tt(Bt1[:, blk * D + 512:(blk + 1) * D],
                          Bcos[:, blk * D + 512:(blk + 1) * D], ps6[:, 0:512],
                          OP.add, waits=[m2, s1])
                s2 = g_tt(Bg[:, blk * D:(blk + 1) * D], Bv[:, blk * D:(blk + 1) * D],
                          Bg[:, blk * D:(blk + 1) * D], OP.mult, waits=[s1])
                t1v = Bt1[:, blk * D:(blk + 1) * D].rearrange("p (h w) -> p h w",
                                                               w=HD)
                t2v = Bt2[:, blk * D:(blk + 1) * D].rearrange("p (h w) -> p h w",
                                                               w=HD)
                s3 = v_tt(Bt2[:, blk * D:(blk + 1) * D], Bt1[:, blk * D:(blk + 1) * D],
                          Bg[:, blk * D:(blk + 1) * D], OP.mult,
                          waits=[s2, fl_tok[0]])
                s3 = v_tt(t1v[:, :, :], t2v[:, :, :],
                          dec[:, gbi * H:(gbi + 1) * H].unsqueeze(2)
                          .broadcast_to([128, H, HD]), OP.mult, waits=[s3])
                c1 = t_mm(ps1[:, 0:512], tri_s[:, :], Bt1[:, blk * D:blk * D + 512],
                          True, False, waits=[s3])
                c1 = t_mm(ps1[:, 0:512], ones1, carry[0:1, 0:512], False, True)
                c2 = t_mm(ps2[:, 0:512], tri_s[:, :],
                          Bt1[:, blk * D + 512:(blk + 1) * D], True, False)
                c2 = t_mm(ps2[:, 0:512], ones1, carry[0:1, 512:1024], False, True)
                b1 = t_mm(ps3[0:1, 0:512], ones1c, Bt1[:, blk * D:blk * D + 512],
                          True, True, waits=[s3, cu_prev])
                b2 = t_mm(ps4[0:1, 0:512], ones1c,
                          Bt1[:, blk * D + 512:(blk + 1) * D], True, True)
                cu = v_tt(carry[0:1, 0:512], carry[0:1, 0:512], ps3[0:1, 0:512],
                          OP.add, waits=[c1, b1])
                cu = v_tt(carry[0:1, 512:1024], carry[0:1, 512:1024],
                          ps4[0:1, 0:512], OP.add, waits=[c2, b2, cu])
                cu_prev = cu
                bvv = Bv[:, blk * D:(blk + 1) * D].rearrange("p (h w) -> p h w",
                                                             w=HD)
                mv = v_tt(bvv[:, 0:8, :],
                          ps1[:, 0:512].rearrange("p (h w) -> p h w", w=HD),
                          rdec[:, gbi * H:gbi * H + 8].unsqueeze(2)
                          .broadcast_to([128, 8, HD]), OP.mult, waits=[c1, cu])
                mv = v_tt(bvv[:, 8:16, :],
                          ps2[:, 0:512].rearrange("p (h w) -> p h w", w=HD),
                          rdec[:, gbi * H + 8:(gbi + 1) * H].unsqueeze(2)
                          .broadcast_to([128, 8, HD]), OP.mult, waits=[c2, mv])
                mix = mv
            kt1 = t_mm(ps3[0:1, 0:512], selc_s[:, 0:1], Bk[:, D:D + 512],
                       True, True, waits=[mix, mixg])
            kt2 = t_mm(ps4[0:1, 0:512], selc_s[:, 0:1], Bk[:, D + 512:2 * D],
                       True, True)
            ktv = s_cp(ktail[0:1, 0:512], ps3[0:1, 0:512], [kt1])
            ktv = s_cp(ktail[0:1, 512:1024], ps4[0:1, 0:512], [ktv, kt2])
            mx0 = v_tt(Bmb[:, 0:D], Bv[:, 0:D], Bk[:, 0:D], OP.mult,
                       waits=[mix, mixg, ktv])
            mx1 = g_tt(Bmb[:, D:2 * D], Bv[:, D:2 * D], Bk[:, D:2 * D], OP.mult,
                       waits=[mix, mixg, ktv])
            mix = [mx0, mx1]
            MIXFM = fm_transpose(mix)

            mend = None
            for c in range(NCH):
                wpar = c % 2
                lw = DMA("gpsimd", Wq[:, wpar * D:(wpar + 1) * D], sc["o"][c, :, :],
                         slot_hist["q"][wpar], sem="Q%d" % wpar)
                for blk in range(NBLK):
                    pa, pb = (psA, psB)[blk]
                    w_ = [lw, MIXFM] if c == 0 else [lw]
                    t_mm(pa[:, 0:512],
                         Bfm[:, c * MT + blk * 128:c * MT + (blk + 1) * 128],
                         Wq[:, wpar * D:wpar * D + 512], c == 0, c == 7, waits=w_)
                    mend = t_mm(pb[:, 0:512],
                                Bfm[:, c * MT + blk * 128:c * MT + (blk + 1) * 128],
                                Wq[:, wpar * D + 512:(wpar + 1) * D], c == 0, c == 7)
                slot_hist["q"][wpar] = [mend]
            ev = mend
            evb = []
            for blk in range(NBLK):
                pa, pb = (psA, psB)[blk]
                ev = s_cp(Bt2[:, blk * D:blk * D + 512], pa[:, 0:512],
                          [ev, fl_tok[0]])
                ev = s_cp(Bt2[:, blk * D + 512:(blk + 1) * D], pb[:, 0:512], [ev])
                evb.append(ev)
            on = ln_pair([Bt2[:, 0:D], Bt2[:, D:2 * D]],
                         [Bt2[:, 0:D], Bt2[:, D:2 * D]], [evb[0]], wg=[evb[1]])
            xa = v_tt(Bt3[:, 0:D], Bt2[:, 0:D], sob[:, :], OP.mult,
                      waits=[on[0], store_tok[0]])
            xa = v_tt(Bt3[:, 0:D], Bt3[:, 0:D], bob[:, :], OP.add, waits=[xa])
            xa = v_tt(Bk[:, 0:D], Bt3[:, 0:D], BxP[:, 0:D], OP.add, waits=[xa])
            xb = g_tt(Bt3[:, D:2 * D], Bt2[:, D:2 * D], sob[:, :], OP.mult,
                      waits=[on[1], store_tok[0]])
            xb = g_tt(Bt3[:, D:2 * D], Bt3[:, D:2 * D], bob[:, :], OP.add,
                      waits=[xb])
            xb = g_tt(Bk[:, D:2 * D], Bt3[:, D:2 * D], BxP[:, D:2 * D], OP.add,
                      waits=[xb])

            l2t = ln_pair([Bk[:, 0:D], Bk[:, D:2 * D]],
                          [Bt2[:, 0:D], Bt2[:, D:2 * D]], [xa], wg=[xb],
                          site="l2")
            l2c0 = v_cp(Bmb[:, 0:D], Bt2[:, 0:D], [l2t[0]])
            l2c1 = g_cp(Bmb[:, D:2 * D], Bt2[:, D:2 * D], [l2t[1]])
            H2FM = fm_transpose([l2c0, l2c1])

            nxt = mts[mi_ + 1] if mi_ + 1 < len(mts) else None
            ft_next = {}
            fg = None
            if nxt is not None:
                fg = front_gen(nxt[0], nxt[1],
                               {"mix": mix, "XC": XC, "MIXFM": MIXFM,
                                "H2FM": H2FM}, ft_next)

            gevs = []
            for mi in range(32):
                wpar = mi % 4
                lg = DMA("gpsimd", Wgu[:, wpar * 2048:wpar * 2048 + 1024],
                         sc["gate"][mi, :, :], slot_hist["gu"][wpar], sem="G%d" % wpar)
                lu = DMA("gpsimd", Wgu[:, wpar * 2048 + 1024:(wpar + 1) * 2048],
                         sc["up"][mi, :, :], slot_hist["gu"][wpar], sem="G%d" % wpar)
                pg = (ps1, ps3)[mi % 2]
                pu = (ps2, ps4)[mi % 2]
                mg = mu = None
                for c in range(NCH):
                    w_ = [lg, lu, H2FM]
                    if c == 0 and mi >= 2:
                        w_.append(gevs[mi - 2])
                    mg = t_mm(pg[:, 0:MT],
                              Wgu[:, wpar * 2048 + c * 128:wpar * 2048 + (c + 1) * 128],
                              Bfm[:, c * MT:(c + 1) * MT], c == 0, c == 7,
                              waits=w_ if c == 0 else [lg])
                    mu = t_mm(pu[:, 0:MT],
                              Wgu[:, wpar * 2048 + 1024 + c * 128:wpar * 2048 + 1024 + (c + 1) * 128],
                              Bfm[:, c * MT:(c + 1) * MT], c == 0, c == 7)
                slot_hist["gu"][wpar] = [mu]
                pp2 = mi % 2
                sl = s_act(Bsl[:, pp2 * MT:(pp2 + 1) * MT], pg[:, 0:MT], AF.Silu,
                           bias=bgcol[:, mi:mi + 1], waits=[mu])
                gv = v_stt(Bgu[:, mi * MT:(mi + 1) * MT], pu[:, 0:MT],
                           bucol[:, mi:mi + 1], Bsl[:, pp2 * MT:(pp2 + 1) * MT],
                           OP.add, OP.mult, waits=[sl])
                gevs.append(gv)
                drain(fg)
            GU = gevs[-1]

            dm = None
            for f in range(32):
                wpar = f % 4
                lw = DMA("gpsimd", Wdn[:, wpar * 1024:(wpar + 1) * 1024],
                         sc["down"][f, :, :], slot_hist["dn"][wpar], sem="D%d" % wpar)
                for blk in range(NBLK):
                    pa, pb = (psA, psB)[blk]
                    w_ = [lw, GU, gevs[30]] if f == 0 else [lw]
                    t_mm(pa[:, 0:512],
                         Bgu[:, f * MT + blk * 128:f * MT + (blk + 1) * 128],
                         Wdn[:, wpar * 1024:wpar * 1024 + 512], f == 0, f == 31,
                         waits=w_)
                    dm = t_mm(pb[:, 0:512],
                              Bgu[:, f * MT + blk * 128:f * MT + (blk + 1) * 128],
                              Wdn[:, wpar * 1024 + 512:(wpar + 1) * 1024],
                              f == 0, f == 31)
                slot_hist["dn"][wpar] = [dm]
                drain(fg)
            yv = dm
            for blk in range(NBLK):
                pa, pb = (psA, psB)[blk]
                yv = v_tt(Bt3[:, blk * D:blk * D + 512], pa[:, 0:512],
                          Bt3[:, blk * D:blk * D + 512], OP.add,
                          waits=[yv] + ([xb] if blk == 1 else []))
                yv = v_tt(Bt3[:, blk * D + 512:(blk + 1) * D], pb[:, 0:512],
                          Bt3[:, blk * D + 512:(blk + 1) * D], OP.add, waits=[yv])
                yv = v_stt(Bt3[:, blk * D:(blk + 1) * D],
                           BxP[:, blk * D:(blk + 1) * D], 2.0,
                           Bt3[:, blk * D:(blk + 1) * D], OP.mult, OP.add,
                           waits=[yv])
            xread_tok[par] = yv

            drain(fg, 1000)

            ol = ln_pair([Bt3[:, 0:D], Bt3[:, D:2 * D]],
                         [Bt1[:, 0:D], Bt1[:, D:2 * D]], [yv], site="o1")
            ol20 = v_cp(Bmb[:, 0:D], Bt1[:, 0:D], [ol[0]])
            ol21 = g_cp(Bmb[:, D:2 * D], Bt1[:, D:2 * D], [ol[1]])
            XFFM = fm_transpose([ol20, ol21])

            cp_last = XFFM
            zq = ft_next.get("QKVG", XFFM) if fg is not None else XFFM
            zms = [[XFFM, zq], [XFFM, zq]]
            ges = {}
            lcts = {}

            def emit_z(f):
                if f % 4 == 0:
                    gs = (f // 4) % 2
                    lcts[gs] = DMA("gpsimd",
                                   Wct[:, gs * 512:(gs + 1) * 512].rearrange(
                                       "p (t w) -> p t w", w=128),
                                   sc["ct"][f:f + 4, :, :].rearrange("t p w -> p t w"),
                                   zms[gs] if isinstance(zms[gs], list)
                                   else [zms[gs]], sem="T%d" % gs)
                gs = (f // 4) % 2
                pz = psZ[f % 2]
                cq = (f // 4) % 8
                gprev = ges.get(f - 2, zq if f < 2 else None)
                zm = t_mm(pz[:, 0:MT],
                          Wct[:, gs * 512 + (f % 4) * 128:gs * 512 + (f % 4 + 1) * 128],
                          Bfm[:, cq * MT:(cq + 1) * MT], True, True,
                          waits=[XFFM, lcts[gs]] + ([gprev] if gprev else []))
                if f % 4 == 3:
                    zms[gs] = [zm]
                ges[f] = s_act(Bha[:, (f % 2) * MT:(f % 2 + 1) * MT], pz[:, 0:MT],
                               AF.Gelu, waits=[zm])

            emit_z(0)
            for f in range(128):
                wpar = f % 4
                lw = DMA("sync", Wcp[:, wpar * 1024:(wpar + 1) * 1024],
                         sc["wc"][f, :, :], slot_hist["cp"][wpar], sem="C%d" % wpar)
                if f + 1 < 128:
                    emit_z(f + 1)
                pp2 = f % 2
                for blk in range(NBLK):
                    pa, pb = (psA, psB)[blk]
                    t_mm(pa[:, 0:512],
                         Bha[:, pp2 * MT + blk * 128:pp2 * MT + (blk + 1) * 128],
                         Wcp[:, wpar * 1024:wpar * 1024 + 512], f == 0, f == 127,
                         waits=[ges[f], lw])
                    cp_last = t_mm(pb[:, 0:512],
                                   Bha[:, pp2 * MT + blk * 128:pp2 * MT + (blk + 1) * 128],
                                   Wcp[:, wpar * 1024 + 512:(wpar + 1) * 1024],
                                   f == 0, f == 127)
                slot_hist["cp"][wpar] = [cp_last]
                if f % 2 == 0:
                    drain(fg)
            drain(fg, 1000)

            fin = cp_last
            for blk in range(NBLK):
                pa, pb = (psA, psB)[blk]
                fin = v_tt(Bt2[:, blk * D:blk * D + 512], pa[:, 0:512],
                           Bt1[:, blk * D:blk * D + 512], OP.add, waits=[fin])
                fin = v_tt(Bt2[:, blk * D + 512:(blk + 1) * D], pb[:, 0:512],
                           Bt1[:, blk * D + 512:(blk + 1) * D], OP.add, waits=[fin])
            fl = ln_pair([Bt2[:, 0:D], Bt2[:, D:2 * D]],
                         [Bo[:, 0:D], Bo[:, D:2 * D]], [fin], site="o2")
            DMA("sync", out_d[b, t0:t0 + 128, :], Bo[:, 0:D], [fl[0]], sem="ST")
            st = DMA("sync", out_d[b, t0 + 128:t0 + 256, :], Bo[:, D:2 * D],
                     [fl[1]], sem="ST")
            store_tok[0] = st
            fl_tok[0] = fl
            fin_tok[0] = fin
            cosread_tok[0] = mix
            ft = ft_next

        E("sync", lambda e: e.nop(), [P.tok("ST")])

        @block.sync
        def _(e):
            P.replay("sync", e, sems)

        @block.gpsimd
        def _(e):
            P.replay("gpsimd", e, sems)

        @block.vector
        def _(e):
            P.replay("vector", e, sems)

        @block.scalar
        def _(e):
            P.replay("scalar", e, sems)

        @block.tensor
        def _(e):
            P.replay("tensor", e, sems)

    return nc


def _replay(self, eng, e, sems):
    for fn, ws, inc, inc_n in self.stream[eng]:
        for (s, v) in ws:
            e.wait_ge(sems[s], v)
        ins = fn(e)
        if inc is not None:
            if ins is not None:
                ins.then_inc(sems[inc], inc_n)
            else:
                e.sem_inc(sems[inc], inc_n)

Prog.replay = _replay


def kernel(**inputs):
    x = np.asarray(inputs["L0_attn_x"], dtype=np.float32)
    prep = host_prep(inputs)
    nc = build_nc()
    in_maps = []
    for c in range(NCORES):
        mm = dict(prep)
        mm["x"] = np.ascontiguousarray(x[c * BPC:(c + 1) * BPC])
        in_maps.append(mm)
    res = run_bass_kernel_spmd(nc, in_maps, core_ids=list(range(NCORES)))
    out = np.concatenate([np.asarray(r["out"]) for r in res.results], axis=0)
    return out.astype(np.float32)
